# version 12
# baseline (speedup 1.0000x reference)
import math
from contextlib import ExitStack

import numpy as np
import ml_dtypes

import concourse.bass as bass
import concourse.mybir as mybir
from concourse.bass_utils import run_bass_kernel_spmd

F32 = mybir.dt.float32
BF16 = mybir.dt.bfloat16
AF = mybir.ActivationFunctionType
ALU = mybir.AluOpType
AX = mybir.AxisListType

B, S, H, NH, HD, INTER = 4, 4096, 2048, 16, 128, 8192
NF = H // 128
NI = INTER // 128
TOK = 2048
NEGBIG = -1.0e30
MCONST = 32.0
SQ2INV64 = 45.254834
SLOPES = [2.0 ** (-(i + 1) / 2.0) for i in range(NH)]
EPS = 1e-5
KQOFF = 31 * 128


def rect_kts(l):
    return list(range(8 * l)) + [28, 29, 30, 31]


RECT_BASE = [0, 4, 16, 36]
NSLOT_RECT = 64


def _hadamard(n):
    h = np.array([[1.0]])
    while h.shape[0] < n:
        h = np.block([[h, h], [h, -h]])
    return h


class Prog:
    def __init__(self, nc, es):
        self.nc = nc
        self.sems = {}
        self.n = {}
        for name in ("c_pe", "c_dve", "c_act", "c_pool",
                     "d_in", "d_w", "d_scr", "d_out"):
            self.sems[name] = es.enter_context(nc.semaphore(name))
            self.n[name] = 0
        self.new_walk()

    def new_walk(self):
        self.ops = {e: [] for e in ("pe", "dve", "act", "pool", "syn")}

    def emit(self, eng, fn, inc=None, inck=1, wait=None):
        if wait:
            for (sname, val) in ([wait] if isinstance(wait, tuple) else wait):
                if val > 0:
                    sem = self.sems[sname]
                    self.ops[eng].append(
                        lambda e, s=sem, v=val: e.wait_ge(s, v))
        if inc is not None:
            sem = self.sems[inc]
            self.n[inc] += inck

            def run(e, f=fn, s=sem, k=inck):
                f(e).then_inc(s, k)
            self.ops[eng].append(run)
        else:
            self.ops[eng].append(fn)

    def dma(self, eng, out_ap, in_ap, sem="d_in", wait=None):
        self.emit(eng, lambda e, o=out_ap, i=in_ap: e.dma_start(o, i),
                  inc=sem, inck=16, wait=wait)

    def run_block(self):
        for sname in ("d_in", "d_w", "d_scr", "d_out"):
            if self.n[sname] > 0:
                sem, v = self.sems[sname], self.n[sname]
                self.ops["syn"].append(lambda e, s=sem, vv=v: e.wait_ge(s, vv))
        ops = self.ops
        with self.nc.Block() as blk:
            if ops["syn"]:
                @blk.sync
                def _(e, lst=list(ops["syn"])):
                    for f in lst:
                        f(e)
            if ops["pe"]:
                @blk.tensor
                def _(e, lst=list(ops["pe"])):
                    for f in lst:
                        f(e)
            if ops["dve"]:
                @blk.vector
                def _(e, lst=list(ops["dve"])):
                    for f in lst:
                        f(e)
            if ops["act"]:
                @blk.scalar
                def _(e, lst=list(ops["act"])):
                    for f in lst:
                        f(e)
            if ops["pool"]:
                @blk.gpsimd
                def _(e, lst=list(ops["pool"])):
                    for f in lst:
                        f(e)
        self.new_walk()


def build(tap=None):
    nc = bass.Bass("TRN2", target_bir_lowering=False)

    def din(name, shape, dt=F32):
        return nc.dram_tensor(name, shape, dt, kind="ExternalInput").ap()

    x_own = din("x_own", [TOK, H])
    x_kv = din("x_kv", [S, H])
    w_qkv = din("w_qkv", [H, 3 * H])
    w_proj = din("w_proj", [H, H])
    wup = din("wup", [INTER, H])
    wgate = din("wgate", [INTER, H])
    wdown = din("wdown", [H, INTER])
    bqkv_cols = din("bqkv_cols", [128, 48])
    bvrep_in = din("bvrep", [128, H])
    bproj_cols = din("bproj_cols", [128, 16])
    su1_cols = din("su1_cols", [128, NF])
    su2_cols = din("su2_cols", [128, NI])
    ws_cols = din("ws_cols", [128, 144])
    h128_in = din("h128", [128, 128])
    ident_b = din("ident_b", [128, 128], BF16)
    ident_f = din("ident_f", [128, 128])
    expb_in = din("expb", [128, NH * NSLOT_RECT])

    out_own = nc.dram_tensor("out_own", [TOK, H], F32,
                             kind="ExternalOutput").ap()

    def mkscr(name, shape, dt):
        return nc.dram_tensor(name, shape, dt, kind="Internal").ap()

    sK = mkscr("sK", [NH, 128, S], BF16)
    sV = mkscr("sV", [NH, S, HD], BF16)
    sKo = mkscr("sKo", [NH, 128, TOK], BF16)
    sVo = mkscr("sVo", [NH, TOK, HD], BF16)
    sQ = mkscr("sQ", [NH, 128, TOK], BF16)
    sU1 = mkscr("sU1", [NF, 128, TOK], BF16)
    sSp = mkscr("sSp", [NH, 128, TOK], BF16)
    sS32 = mkscr("sS32", [TOK, H], F32)
    sX2b = mkscr("sX2b", [INTER, TOK], BF16)
    sWdT = mkscr("sWdT", [128, 16, NI, 128], BF16)
    sG = mkscr("sG", [TOK, H], F32)
    sXN = mkscr("sXN", [NF, 128, 2048], BF16)
    scr_map = {"sXN": sXN,"sK": sK, "sV": sV, "sKo": sKo, "sVo": sVo, "sQ": sQ,
               "sU1": sU1, "sSp": sSp, "sS32": sS32,
               "sX2b": sX2b, "sG": sG}

    with ExitStack() as es:
        P = Prog(nc, es)

        def sb(stk, name, shape, dt):
            return stk.enter_context(nc.sbuf_tensor(name, shape, dt))

        c_bqkv = sb(es, "c_bqkv", [128, 48], F32)
        c_bproj = sb(es, "c_bproj", [128, 16], F32)
        c_su2 = sb(es, "c_su2", [128, NI], F32)
        c_vuvg = sb(es, "c_vuvg", [128, 144], F32)
        c_idb = sb(es, "c_idb", [128, 128], BF16)
        c_idf = sb(es, "c_idf", [128, 128], F32)
        c_h128 = sb(es, "c_h128", [128, 128], F32)
        ones_b = sb(es, "ones_b", [128, 1], BF16)
        ones_f = sb(es, "ones_f", [1, 128], F32)
        def mkpsum(stk, pref, n, shape=None, dt=F32):
            return [stk.enter_context(nc.psum_tensor(
                f"{pref}{i}", shape or [128, 512], dt)) for i in range(n)]

        with ExitStack() as ph:
            csu1 = sb(ph, "csu1", [128, NF], F32)
            P.dma("syn", c_bqkv[:, :], bqkv_cols[:, :])
            P.dma("syn", c_bproj[:, :], bproj_cols[:, :])
            P.dma("syn", csu1[:, :], su1_cols[:, :])
            P.dma("syn", c_su2[:, :], su2_cols[:, :])
            P.dma("syn", c_vuvg[:, :], ws_cols[:, :])
            P.dma("syn", c_idb[:, :], ident_b[:, :])
            P.dma("syn", c_idf[:, :], ident_f[:, :])
            P.dma("syn", c_h128[:, :], h128_in[:, :])
            P.emit("pool", lambda e: e.memset(ones_b[:, :], 1.0))
            P.emit("pool", lambda e: e.memset(ones_f[:, :], 1.0))
            d0 = P.n["d_in"]
            P.emit("dve", lambda e: e.tensor_scalar(
                c_vuvg[:, :], c_vuvg[:, :], SQ2INV64, None, op0=ALU.mult),
                wait=("d_in", d0), inc="c_dve")
            hsu1_holder = {}
            P.run_block()

            with ExitStack() as p1:
                pacc = mkpsum(p1, "pacc", 4)
                ptr_b = mkpsum(p1, "ptrb", 2, [128, 1024], BF16)
                TW = 2048
                xn_buf = sb(p1, "xn_buf", [128, NF * TW], BF16)
                xnt = sb(p1, "xnt", [128, H], BF16)
                ln_sm = sb(p1, "ln_sm", [128, 228], F32)
                xst = [sb(p1, f"xst{i}", [128, H], F32) for i in range(2)]
                c_bvrep = sb(p1, "c_bvrep", [128, H], F32)
                hsu1 = sb(p1, "hsu1", [128, NF * 128], BF16)
                wf_a = [sb(p1, f"wf_a{i}", [128, NF * 128], F32)
                        for i in range(2)]
                scrap = wf_a[0]
                wb_a = [sb(p1, f"wba{i}", [128, NF * 128], BF16)
                        for i in range(2)]
                wf_v = sb(p1, "wf_v", [128, NF * 256], F32)
                wb_v = [sb(p1, f"wbv{i}", [128, NF * 512], BF16)
                        for i in range(2)]
                kst = [sb(p1, f"kst{i}", [128, 512], BF16) for i in range(2)]
                vst = [sb(p1, f"vst{i}", [128, 512], BF16) for i in range(2)]
                u1t = [sb(p1, f"u1t{i}", [128, NF * 128], F32)
                       for i in range(2)]
                u1s = [sb(p1, f"u1s{i}", [128, 128], BF16) for i in range(2)]

                P.dma("syn", c_bvrep[:, :], bvrep_in[:, :])
                for c in range(NF):
                    P.emit("dve", lambda e, c=c: e.tensor_scalar(
                        hsu1[:, c * 128:(c + 1) * 128], c_h128[:, :],
                        csu1[:, c:c + 1], 2.0 ** -5,
                        op0=ALU.mult, op1=ALU.mult), inc="c_dve")

                P.emit("pool", lambda e: e.memset(ln_sm[:, 224:225], EPS),
                       inc="c_pool")
                xn3 = xn_buf[:, :].rearrange("p (c t) -> p c t", c=NF)
                st = {"norm": [0, 0], "ev": [0, 0], "tr": 0, "sums": [0, 0]}

                def ln_pass(src_ap, n_tiles):
                    for i in range(n_tiles):
                        xb = xst[i % 2]
                        P.dma("syn", xb[:, :],
                              src_ap[i * 128:(i + 1) * 128, :], "d_in",
                              wait=[("c_dve", st["sums"][i % 2])])
                        dn = P.n["d_in"]
                        P.emit("dve", lambda e, xb=xb, i=i: e.reduce_sum(
                            ln_sm[:, i:i + 1], xb[:, :], axis=AX.X,
                            negate=True),
                            wait=("d_in", dn), inc="c_dve")
                        P.emit("dve", lambda e, xb=xb, i=i:
                               e.scalar_tensor_tensor(
                                   scrap[:, :], xb[:, :], 1.0, xb[:, :],
                                   op0=ALU.mult, op1=ALU.mult,
                                   accum_out=ln_sm[:, 16 + i:17 + i]),
                               inc="c_dve")
                        st["sums"][i % 2] = P.n["c_dve"]
                    NM, S1, M2, A_, LN_, LNC, R0, T3, T4, T5, RS, NMA = (
                        32, 48, 64, 80, 96, 112, 128, 144, 160, 176, 192, 208)
                    nt = n_tiles
                    vd = P.n["c_dve"]
                    P.emit("act", lambda e: e.activation(
                        ln_sm[:, NM:NM + nt], ln_sm[:, 0:nt], AF.Identity,
                        bias=0.0, scale=1.0 / H),
                        wait=("c_dve", vd), inc="c_act")
                    P.emit("act", lambda e: e.activation(
                        ln_sm[:, S1:S1 + nt], ln_sm[:, 16:16 + nt],
                        AF.Identity, bias=ln_sm[:, 224:225],
                        scale=1.0 / H), inc="c_act")
                    va = P.n["c_act"]
                    P.emit("dve", lambda e: e.tensor_tensor(
                        ln_sm[:, M2:M2 + nt], ln_sm[:, NM:NM + nt],
                        ln_sm[:, NM:NM + nt], op=ALU.mult),
                        wait=("c_act", va), inc="c_dve")
                    vd = P.n["c_dve"]
                    P.emit("pool", lambda e: e.tensor_tensor(
                        ln_sm[:, A_:A_ + nt], ln_sm[:, S1:S1 + nt],
                        ln_sm[:, M2:M2 + nt], op=ALU.subtract),
                        wait=[("c_act", va), ("c_dve", vd)], inc="c_pool")
                    vp = P.n["c_pool"]
                    P.emit("act", lambda e: e.activation(
                        ln_sm[:, LN_:LN_ + nt], ln_sm[:, A_:A_ + nt], AF.Ln),
                        wait=("c_pool", vp), inc="c_act")
                    va = P.n["c_act"]
                    P.emit("pool", lambda e: e.tensor_copy(
                        ln_sm[:, LNC:LNC + nt], ln_sm[:, LN_:LN_ + nt]),
                        wait=("c_act", va), inc="c_pool")
                    vp = P.n["c_pool"]
                    P.emit("act", lambda e: e.activation(
                        ln_sm[:, R0:R0 + nt], ln_sm[:, LNC:LNC + nt],
                        AF.Exp, scale=-0.5),
                        wait=("c_pool", vp), inc="c_act")
                    va = P.n["c_act"]
                    P.emit("dve", lambda e: e.tensor_tensor(
                        ln_sm[:, T3:T3 + nt], ln_sm[:, R0:R0 + nt],
                        ln_sm[:, R0:R0 + nt], op=ALU.mult),
                        wait=("c_act", va), inc="c_dve")
                    vd = P.n["c_dve"]
                    P.emit("pool", lambda e: e.tensor_tensor(
                        ln_sm[:, T4:T4 + nt], ln_sm[:, T3:T3 + nt],
                        ln_sm[:, A_:A_ + nt], op=ALU.mult),
                        wait=("c_dve", vd), inc="c_pool")
                    vp = P.n["c_pool"]
                    P.emit("dve", lambda e: e.tensor_scalar(
                        ln_sm[:, T5:T5 + nt], ln_sm[:, T4:T4 + nt], -0.5,
                        1.5, op0=ALU.mult, op1=ALU.add),
                        wait=("c_pool", vp), inc="c_dve")
                    vd = P.n["c_dve"]
                    P.emit("pool", lambda e: e.tensor_tensor(
                        ln_sm[:, RS:RS + nt], ln_sm[:, T5:T5 + nt],
                        ln_sm[:, R0:R0 + nt], op=ALU.mult),
                        wait=("c_dve", vd), inc="c_pool")
                    vp = P.n["c_pool"]
                    P.emit("dve", lambda e: e.tensor_tensor(
                        ln_sm[:, NMA:NMA + nt], ln_sm[:, NM:NM + nt],
                        ln_sm[:, RS:RS + nt], op=ALU.mult),
                        wait=("c_pool", vp), inc="c_dve")
                    chain_d = P.n["c_dve"]
                    chain_p = P.n["c_pool"]
                    sums_done = chain_d
                    for i in range(n_tiles):
                        xb = xst[i % 2]
                        P.dma("syn", xb[:, :],
                              src_ap[i * 128:(i + 1) * 128, :], "d_in",
                              wait=[("c_act", st["norm"][i % 2]),
                                    ("c_dve", sums_done)])
                        dn = P.n["d_in"]
                        P.emit("act", lambda e, xb=xb, i=i: e.activation(
                            xnt[:, :], xb[:, :], AF.Identity,
                            bias=ln_sm[:, NMA + i:NMA + i + 1],
                            scale=ln_sm[:, RS + i:RS + i + 1]),
                            wait=[("d_in", dn), ("c_dve", chain_d),
                                  ("c_pool", chain_p), ("c_pe", st["tr"])],
                            inc="c_act")
                        st["norm"][i % 2] = P.n["c_act"]
                        a3 = P.n["c_act"]
                        for hf in range(2):
                            pb = ptr_b[hf]

                            def trs(e, hf=hf, pb=pb):
                                for cc in range(8):
                                    c = hf * 8 + cc
                                    last = e.transpose(
                                        pb[:, cc * 128:(cc + 1) * 128],
                                        xnt[:, c * 128:(c + 1) * 128],
                                        c_idb[:, :])
                                return last
                            P.emit("pe", trs,
                                   wait=[("c_act", a3),
                                         ("c_dve", st["ev"][hf])],
                                   inc="c_pe")
                            pn = P.n["c_pe"]
                            P.emit("dve",
                                   lambda e, hf=hf, pb=pb, i=i:
                                   e.tensor_copy(
                                       xn3[:, hf * 8:(hf + 1) * 8,
                                           i * 128:(i + 1) * 128],
                                       pb[:, :].rearrange(
                                           "p (c t) -> p c t", c=8)),
                                   wait=("c_pe", pn), inc="c_dve")
                            st["ev"][hf] = P.n["c_dve"]
                        st["tr"] = P.n["c_pe"]

                w_qkv3 = w_qkv.rearrange("(c p) o -> p c o", p=128)
                pool_n = {"a": 0, "v0": 0, "v1": 0}
                wb_pe = {}
                scr_n = {}
                OWN = {0: 0, 1024: 1, 2048: 2, 3072: 3}

                def kq_pass(col_off, bias_off, qscale, dst, own_dst, half):
                    for oc in range(NH):
                        b = oc % 2
                        P.dma("syn",
                              wf_a[b][:, :].rearrange("p (c o) -> p c o",
                                                      c=NF),
                              w_qkv3[:, :, col_off + oc * 128:
                                     col_off + (oc + 1) * 128],
                              "d_w", wait=[("c_pool", pool_n.get(
                                  ("a", b), 0))])
                        dw = P.n["d_w"]
                        P.emit("pool", lambda e, b=b: e.tensor_copy(
                            wb_a[b][:, :], wf_a[b][:, :]),
                            wait=[("d_w", dw), ("c_pe", wb_pe.get(b, 0))],
                            inc="c_pool")
                        pool_n[("a", b)] = P.n["c_pool"]
                        cn = P.n["c_pool"]
                        for ts in range(4):
                            pb = pacc[ts]
                            waits = [("c_pool", cn),
                                     ("c_dve", P.n["c_dve"] - 3)]

                            def mms(e, b=b, ts=ts, pb=pb):
                                for fc in range(NF):
                                    last = e.matmul(
                                        pb[:, :],
                                        wb_a[b][:, fc * 128:(fc + 1) * 128],
                                        xn_buf[:, fc * TW + ts * 512:
                                               fc * TW + ts * 512 + 512],
                                        start=(fc == 0), stop=(fc == NF - 1))
                                return last
                            P.emit("pe", mms, wait=waits, inc="c_pe")
                            pn = P.n["c_pe"]
                            wb_pe[b] = P.n["c_pe"]
                            kb = kst[ts % 2]
                            wv = [("c_pe", pn)]
                            if ("k", ts % 2) in scr_n:
                                wv.append(("d_scr", scr_n[("k", ts % 2)]))
                            if qscale is None:
                                P.emit("dve", lambda e, pb=pb, kb=kb, oc=oc:
                                       e.tensor_scalar(
                                           kb[:, :], pb[:, :],
                                           c_bqkv[:, bias_off + oc:
                                                  bias_off + oc + 1],
                                           None, op0=ALU.add),
                                       wait=wv, inc="c_dve")
                            else:
                                P.emit("dve", lambda e, pb=pb, kb=kb, oc=oc:
                                       e.tensor_scalar(
                                           kb[:, :], pb[:, :],
                                           c_bqkv[:, bias_off + oc:
                                                  bias_off + oc + 1],
                                           qscale, op0=ALU.add, op1=ALU.mult),
                                       wait=wv, inc="c_dve")
                            vn = P.n["c_dve"]
                            t0 = half * 2048 + ts * 512
                            P.dma("syn", dst[oc, :, t0:t0 + 512], kb[:, :],
                                  "d_scr", wait=[("c_dve", vn)])
                            if own_dst is not None and t0 in OWN:
                                l = OWN[t0]
                                P.dma("syn",
                                      own_dst[oc, :, l * 512:(l + 1) * 512],
                                      kb[:, :], "d_scr")
                            scr_n[("k", ts % 2)] = P.n["d_scr"]

                def v_pass(half):
                    for os_ in range(4):
                        b = os_ % 2
                        wbv3 = wb_v[b][:, :].rearrange(
                            "p (c o) -> p c o", c=NF)
                        for hv in range(2):
                            P.dma("syn",
                                  wf_v[:, :].rearrange(
                                      "p (c o) -> p c o", c=NF),
                                  w_qkv3[:, :, 2 * H + os_ * 512 + hv * 256:
                                         2 * H + os_ * 512 + hv * 256 + 256],
                                  "d_w",
                                  wait=[("c_pool", pool_n.get("vf", 0))])
                            dw = P.n["d_w"]
                            P.emit("pool", lambda e, b=b, hv=hv, wbv3=wbv3:
                                   e.tensor_copy(
                                       wbv3[:, :, hv * 256:(hv + 1) * 256],
                                       wf_v[:, :].rearrange(
                                           "p (c o) -> p c o", c=NF)),
                                   wait=[("d_w", dw),
                                         ("c_pe", wb_pe.get(("v", b), 0))],
                                   inc="c_pool")
                            pool_n["vf"] = P.n["c_pool"]
                        pool_n[f"v{b}"] = P.n["c_pool"]
                        cn = P.n["c_pool"]
                        for tsub in range(16):
                            pb = pacc[tsub % 4]
                            waits = [("c_pool", cn),
                                     ("c_dve", P.n["c_dve"] - 3)]

                            def mmv(e, b=b, tsub=tsub, pb=pb):
                                for fc in range(NF):
                                    last = e.matmul(
                                        pb[:, :],
                                        xn_buf[:, fc * TW + tsub * 128:
                                               fc * TW + tsub * 128 + 128],
                                        wb_v[b][:, fc * 512:(fc + 1) * 512],
                                        start=(fc == 0), stop=(fc == NF - 1))
                                return last
                            P.emit("pe", mmv, wait=waits, inc="c_pe")
                            pn = P.n["c_pe"]
                            wb_pe[("v", b)] = P.n["c_pe"]
                            vb = vst[tsub % 2]
                            wv = [("c_pe", pn)]
                            if ("v", tsub % 2) in scr_n:
                                wv.append(("d_scr", scr_n[("v", tsub % 2)]))
                            P.emit("dve", lambda e, pb=pb, vb=vb, os_=os_:
                                   e.tensor_tensor(
                                       vb[:, :], pb[:, :],
                                       c_bvrep[:, os_ * 512:(os_ + 1) * 512],
                                       op=ALU.add),
                                   wait=wv, inc="c_dve")
                            vn = P.n["c_dve"]
                            t0 = half * 2048 + tsub * 128
                            vr = vb[:, :].rearrange("p (h d) -> p h d", h=4)
                            for hh in range(4):
                                hg = os_ * 4 + hh
                                P.dma("syn", sV[hg, t0:t0 + 128, :],
                                      vr[:, hh], "d_scr",
                                      wait=[("c_dve", vn)] if hh == 0
                                      else None)
                            g0 = t0 - (t0 % 512)
                            if g0 in OWN:
                                tl = OWN[g0] * 512 + (t0 % 512)
                                for hh in range(4):
                                    hg = os_ * 4 + hh
                                    P.dma("syn", sVo[hg, tl:tl + 128, :],
                                          vr[:, hh], "d_scr")
                            scr_n[("v", tsub % 2)] = P.n["d_scr"]

                def u1_pass():
                    for ts in range(16):
                        ping, pong = u1t[0], u1t[1]
                        cp0 = P.n["c_pool"]
                        for c in range(NF):
                            pb = pacc[c % 4]
                            P.emit("pe", lambda e, c=c, ts=ts, pb=pb:
                                   e.matmul(
                                       pb[:, 0:128],
                                       hsu1[:, c * 128:(c + 1) * 128],
                                       xn_buf[:, c * TW + ts * 128:
                                              c * TW + ts * 128 + 128],
                                       start=True, stop=True),
                                   wait=[("c_dve", P.n["c_dve"] - 3)],
                                   inc="c_pe")
                            pn = P.n["c_pe"]
                            P.emit("dve", lambda e, c=c, pb=pb, ping=ping:
                                   e.tensor_copy(
                                       ping[:, c * 128:(c + 1) * 128],
                                       pb[:, 0:128]),
                                   wait=[("c_pe", pn), ("c_pool", cp0)],
                                   inc="c_dve")
                        for stg in range(4):
                            bit = 1 << stg

                            def bf(e, ping=ping, pong=pong, bit=bit):
                                for c in range(NF):
                                    lo, hi = c & ~bit, c | bit
                                    op = (ALU.add if (c & bit) == 0
                                          else ALU.subtract)
                                    last = e.tensor_tensor(
                                        pong[:, c * 128:(c + 1) * 128],
                                        ping[:, lo * 128:(lo + 1) * 128],
                                        ping[:, hi * 128:(hi + 1) * 128],
                                        op=op)
                                return last
                            eng = "dve" if stg % 2 == 0 else "pool"
                            cw = "c_pool" if stg % 2 == 0 else "c_dve"
                            P.emit(eng, bf, wait=(cw, P.n[cw]),
                                   inc="c_dve" if eng == "dve" else "c_pool")
                            ping, pong = pong, ping
                        bf_done_p = P.n["c_pool"]
                        for c in range(NF):
                            ub = u1s[c % 2]
                            wv = [("c_pool", bf_done_p)]
                            if ("u", c % 2) in scr_n:
                                wv.append(("d_scr", scr_n[("u", c % 2)]))
                            P.emit("dve", lambda e, c=c, ping=ping, ub=ub:
                                   e.tensor_copy(ub[:, :],
                                                 ping[:, c * 128:
                                                      (c + 1) * 128]),
                                   wait=wv, inc="c_dve")
                            vn = P.n["c_dve"]
                            P.dma("syn",
                                  sU1[c, :, ts * 128:(ts + 1) * 128],
                                  ub[:, :], "d_scr", wait=[("c_dve", vn)])
                            scr_n[("u", c % 2)] = P.n["d_scr"]

                for half in range(2):
                    ln_pass(x_kv[half * 2048:(half + 1) * 2048, :], 16)
                    P.run_block()
                    kq_pass(H, 16, None, sK, sKo, half)
                    v_pass(half)
                    P.run_block()
                ln_pass(x_own, 16)
                if tap == "sXN":
                    P.dma("syn", sXN.rearrange("c p t -> p c t"),
                          xn_buf[:, :].rearrange("p (c t) -> p c t", c=NF),
                          "d_scr",
                          wait=[("c_dve", P.n["c_dve"]),
                                ("c_pool", P.n["c_pool"]),
                                ("c_act", P.n["c_act"]),
                                ("c_pe", P.n["c_pe"])])
                P.run_block()
                kq_pass(0, 0, 2.0 ** -7, sQ, None, 0)
                u1_pass()
                P.run_block()

            with ExitStack() as p2:
                kqw = sb(p2, "kqw", [128, KQOFF + 512], F32)
                trb = sb(p2, "trb", [128, 4 * 512], F32)
                trm = sb(p2, "trm", [128, 4 * 512], F32)
                trh = sb(p2, "trh", [128, 4 * 512], F32)
                c_expb = sb(p2, "c_expb", [128, NH * NSLOT_RECT], F32)
                kbuf = [sb(p2, f"kbuf{i}", [128, S], BF16) for i in range(2)]
                kobuf = [sb(p2, f"kobuf{i}", [128, TOK], BF16)
                         for i in range(2)]
                vbuf = [sb(p2, f"vbuf{i}", [128, 32 * HD], BF16)
                        for i in range(2)]
                vobuf = [sb(p2, f"vobuf{i}", [128, 16 * HD], BF16)
                         for i in range(2)]
                qbuf = [sb(p2, f"qbuf{i}", [128, TOK], BF16)
                        for i in range(2)]
                s2b = [sb(p2, f"s2b{i}", [128, 512], F32) for i in range(2)]
                ptb = [sb(p2, f"ptb{i}", [128, 512], BF16) for i in range(2)]
                rrep = sb(p2, "rrep", [128, 512], F32)
                spst = [sb(p2, f"spst{i}", [128, 512], BF16)
                        for i in range(2)]

                P.dma("syn", c_expb[:, :], expb_in[:, :])
                P.emit("pool", lambda e: e.iota(
                    kqw[:, :], [[-1, KQOFF + 512]], base=KQOFF,
                    channel_multiplier=1,
                    allow_small_or_imprecise_dtypes=True), inc="c_pool")
                for t in range(4):
                    P.emit("pool", lambda e, t=t: e.iota(
                        trb[:, t * 512:(t + 1) * 512], [[-1, 512]],
                        base=128 * t, channel_multiplier=1,
                        allow_small_or_imprecise_dtypes=True), inc="c_pool")
                pn0 = P.n["c_pool"]
                P.emit("dve", lambda e: e.tensor_scalar(
                    trm[:, :], trb[:, :], 0.0, NEGBIG,
                    op0=ALU.is_gt, op1=ALU.mult),
                    wait=("c_pool", pn0), inc="c_dve")
                P.emit("dve", lambda e: e.tensor_scalar(
                    trm[:, :], trm[:, :], -MCONST, None, op0=ALU.add),
                    inc="c_dve")
                P.run_block()

                def head_dmas(h, b, guard_pe):
                    P.dma("syn", kbuf[b][:, :], sK[h, :, :], "d_scr",
                          wait=[("c_pe", guard_pe)])
                    P.dma("syn", kobuf[b][:, :], sKo[h, :, :], "d_scr")
                    P.dma("syn",
                          vbuf[b][:, :].rearrange("p (k d) -> p k d", k=32),
                          sV.rearrange("h (k p) d -> h p k d", p=128)[h],
                          "d_scr")
                    P.dma("syn",
                          vobuf[b][:, :].rearrange("p (k d) -> p k d", k=16),
                          sVo.rearrange("h (k p) d -> h p k d", p=128)[h],
                          "d_scr")
                    P.dma("syn", qbuf[b][:, :], sQ[h, :, :], "d_scr")

                SBK = mkpsum(p2, "p2s", 3)
                PVB = mkpsum(p2, "p2pv", 2)
                LB = mkpsum(p2, "p2l", 1)[0]
                rrep2 = [sb(p2, f"rrep2{i}", [128, 512], F32)
                         for i in range(2)]
                rrow = sb(p2, "rrow", [1, 512], F32)
                s2b3 = [sb(p2, f"s2b3{i}", [128, 512], F32) for i in range(3)]
                ptb3 = [sb(p2, f"ptb3{i}", [128, 512], BF16)
                        for i in range(3)]

                flat = []
                for h in range(NH):
                    for l in range(4):
                        kts = rect_kts(l)
                        slots = ([("rect", kt) for kt in kts]
                                 + [("tri", t) for t in range(4)])
                        nsl = len(slots)
                        for si, (kind, kt) in enumerate(slots):
                            flat.append(dict(
                                h=h, l=l, si=si, kind=kind, kt=kt,
                                first=(si == 0), last=(si == nsl - 1),
                                g=h * 4 + l, nsl=nsl,
                                col=(h * NSLOT_RECT + RECT_BASE[l] + si
                                     if kind == "rect" else -1)))
                NSL = len(flat)
                hist_qk = [0] * NSL
                hist_s2 = [0] * NSL
                hist_exp = [0] * NSL
                hist_pv = [0] * NSL
                hist_recip = {}
                hist_actcopy = {}
                hist_tt = {}
                head_ready = [0, 0]
                spst_dma = [0, 0]
                trh_hist = [0] * NH

                head_dmas(0, 0, 0)
                head_ready[0] = P.n["d_scr"]

                def emit_qk(i):
                    s = flat[i]
                    hb = s["h"] % 2
                    sbank = SBK[i % 3]
                    if s["kind"] == "rect":
                        koff, kb = s["kt"] * 128, kbuf[hb]
                    else:
                        koff, kb = s["l"] * 512 + s["kt"] * 128, kobuf[hb]
                    w = []
                    if s["first"] and s["l"] == 0:
                        w.append(("d_scr", head_ready[hb]))
                    if i >= 3:
                        w.append(("c_dve", hist_s2[i - 3]))
                    P.emit("pe", lambda e, kb=kb, koff=koff, sbank=sbank,
                           hb=hb, l=s["l"]: e.matmul(
                               sbank[:, :], kb[:, koff:koff + 128],
                               qbuf[hb][:, l * 512:(l + 1) * 512],
                               start=True, stop=True),
                           wait=w or None, inc="c_pe")
                    hist_qk[i] = P.n["c_pe"]

                def emit_s2(i):
                    s = flat[i]
                    sbank = SBK[i % 3]
                    s2 = s2b3[i % 3]
                    w = [("c_pe", hist_qk[i])]
                    if i >= 3:
                        w.append(("c_act", hist_exp[i - 3]))
                    if s["kind"] == "rect":
                        sl = KQOFF - s["kt"] * 128
                        P.emit("dve", lambda e, s2=s2, sl=sl, sbank=sbank,
                               h=s["h"]: e.scalar_tensor_tensor(
                                   s2[:, :], kqw[:, sl:sl + 512],
                                   SLOPES[h], sbank[:, :],
                                   op0=ALU.mult, op1=ALU.add),
                               wait=w, inc="c_dve")
                    else:
                        P.emit("dve", lambda e, s2=s2, kt=s["kt"],
                               sbank=sbank: e.tensor_tensor(
                                   s2[:, :], sbank[:, :],
                                   trh[:, kt * 512:(kt + 1) * 512],
                                   op=ALU.add),
                               wait=w, inc="c_dve")
                    hist_s2[i] = P.n["c_dve"]

                def emit_exp(i):
                    s = flat[i]
                    s2 = s2b3[i % 3]
                    pt = ptb3[i % 3]
                    w = [("c_dve", hist_s2[i])]
                    if i >= 3:
                        w.append(("c_pe", hist_pv[i - 3]))
                    if s["kind"] == "rect":
                        P.emit("act", lambda e, pt=pt, s2=s2, col=s["col"]:
                               e.activation(pt[:, :], s2[:, :], AF.Exp,
                                            bias=c_expb[:, col:col + 1],
                                            scale=1.0),
                               wait=w, inc="c_act")
                    else:
                        P.emit("act", lambda e, pt=pt, s2=s2: e.activation(
                            pt[:, :], s2[:, :], AF.Exp, bias=0.0, scale=1.0),
                            wait=w, inc="c_act")
                    hist_exp[i] = P.n["c_act"]

                def emit_pv(i):
                    s = flat[i]
                    hb = s["h"] % 2
                    g = s["g"]
                    pt = ptb3[i % 3]
                    vb = vbuf[hb] if s["kind"] == "rect" else vobuf[hb]
                    voff = (s["kt"] * 128 if s["kind"] == "rect"
                            else (s["l"] * 4 + s["kt"]) * 128)
                    pvb = PVB[g % 2]
                    w = [("c_act", hist_exp[i])]
                    if s["first"]:
                        if g - 2 in hist_tt:
                            w.append(("c_dve", hist_tt[g - 2]))
                        if g - 1 in hist_recip:
                            w.append(("c_dve", hist_recip[g - 1]))
                        if g - 1 in hist_actcopy:
                            w.append(("c_act", hist_actcopy[g - 1]))

                    def pv(e, vb=vb, voff=voff, pt=pt, si=s["si"],
                           nsl=s["nsl"], pvb=pvb):
                        e.matmul(pvb[:, :], vb[:, voff:voff + 128],
                                 pt[:, :], start=(si == 0),
                                 stop=(si == nsl - 1))
                        last = e.matmul(LB[0:1, :], ones_b[:, :], pt[:, :],
                                        start=(si == 0),
                                        stop=(si == nsl - 1))
                        return last
                    P.emit("pe", pv, wait=w, inc="c_pe")
                    hist_pv[i] = P.n["c_pe"]
                    if s["last"]:
                        hist_pv_last[s["g"]] = P.n["c_pe"]
                        emit_gend(s)

                def emit_gend(s):
                    g = s["g"]
                    h, l = s["h"], s["l"]
                    P.emit("dve", lambda e: e.reciprocal(
                        rrow[:, :], LB[0:1, :]),
                        wait=[("c_pe", hist_pv_last[g])], inc="c_dve")
                    hist_recip[g] = P.n["c_dve"]
                    P.emit("pe", lambda e: e.matmul(
                        LB[:, :], ones_f[:, :], rrow[:, :],
                        start=True, stop=True),
                        wait=[("c_dve", hist_recip[g])], inc="c_pe")
                    rr_n = P.n["c_pe"]
                    rr = rrep2[g % 2]
                    w = [("c_pe", rr_n)]
                    if g - 2 in hist_tt:
                        w.append(("c_dve", hist_tt[g - 2]))
                    P.emit("act", lambda e, rr=rr: e.activation(
                        rr[:, :], LB[:, :], AF.Identity, bias=0.0,
                        scale=1.0), wait=w, inc="c_act")
                    hist_actcopy[g] = P.n["c_act"]
                    sp = spst[g % 2]
                    P.emit("dve", lambda e, sp=sp, rr=rr, pvb=PVB[g % 2]:
                           e.tensor_tensor(sp[:, :], pvb[:, :], rr[:, :],
                                           op=ALU.mult),
                           wait=[("c_act", hist_actcopy[g]),
                                 ("d_scr", spst_dma[g % 2])],
                           inc="c_dve")
                    hist_tt[g] = P.n["c_dve"]
                    P.dma("syn", sSp[h, :, l * 512:(l + 1) * 512],
                          sp[:, :], "d_scr",
                          wait=[("c_dve", hist_tt[g])])
                    spst_dma[g % 2] = P.n["d_scr"]

                hist_pv_last = {}
                DEPTH = 3
                for i in range(NSL):
                    s = flat[i]
                    if s["first"] and s["l"] == 0:
                        h = s["h"]
                        if h + 1 < NH:
                            head_dmas(h + 1, (h + 1) % 2, P.n["c_pe"])
                            head_ready[(h + 1) % 2] = P.n["d_scr"]
                        P.emit("dve", lambda e, h=h: e.scalar_tensor_tensor(
                            trh[:, :], trb[:, :], SLOPES[h], trm[:, :],
                            op0=ALU.mult, op1=ALU.add), inc="c_dve")
                        trh_hist[h] = P.n["c_dve"]
                    if i >= DEPTH:
                        emit_pv(i - DEPTH)
                    emit_qk(i)
                    emit_s2(i)
                    emit_exp(i)
                for j in range(NSL - DEPTH, NSL):
                    emit_pv(j)
                P.run_block()

            with ExitStack() as p3:
                pacc = mkpsum(p3, "pacc3", 4)
                spre = sb(p3, "spre", [128, NH * TOK], BF16)
                wfp = sb(p3, "wfp", [128, NF * 128], F32)
                wbp = [sb(p3, f"wbp{i}", [128, NF * 128], BF16)
                       for i in range(2)]
                s32s = sb(p3, "s32s", [128, 512], F32)
                s32t = [sb(p3, f"s32t{i}", [128, 512], F32)
                        for i in range(2)]
                P.dma("syn", spre[:, :].rearrange("p (h q) -> p h q", h=NH),
                      sSp.rearrange("h p q -> p h q"), "d_scr")
                sp_rdy = P.n["d_scr"]
                w_proj3 = w_proj.rearrange("(c p) o -> p c o", p=128)
                pool_p = 0
                wbp_pe = {}
                s32_dma = [0, 0]
                for oc in range(16):
                    b = oc % 2
                    P.dma("syn",
                          wfp[:, :].rearrange("p (c o) -> p c o", c=NF),
                          w_proj3[:, :, oc * 128:(oc + 1) * 128], "d_w",
                          wait=[("c_pool", pool_p)])
                    dw = P.n["d_w"]
                    P.emit("pool", lambda e, b=b: e.tensor_copy(
                        wbp[b][:, :], wfp[:, :]),
                        wait=[("d_w", dw), ("c_pe", wbp_pe.get(b, 0))],
                        inc="c_pool")
                    pool_p = P.n["c_pool"]
                    cn = P.n["c_pool"]
                    for qs in range(4):
                        pb = pacc[qs % 3]
                        waits = [("c_pool", cn), ("d_scr", sp_rdy),
                                 ("c_dve", P.n["c_dve"] - 5)]

                        def mmp(e, b=b, qs=qs, pb=pb):
                            for fc in range(NF):
                                last = e.matmul(pb[:, :],
                                         wbp[b][:, fc * 128:(fc + 1) * 128],
                                         spre[:, fc * TOK + qs * 512:
                                              fc * TOK + qs * 512 + 512],
                                         start=(fc == 0),
                                         stop=(fc == NF - 1))
                            return last
                        P.emit("pe", mmp, wait=waits, inc="c_pe")
                        wbp_pe[b] = P.n["c_pe"]
                        pn = P.n["c_pe"]
                        P.emit("dve", lambda e, pb=pb, oc=oc: e.tensor_scalar(
                            s32s[:, :], pb[:, :], c_bproj[:, oc:oc + 1],
                            None, op0=ALU.add), wait=("c_pe", pn),
                            inc="c_dve")
                        ev_n = P.n["c_dve"]

                        def trp(e):
                            for i4 in range(4):
                                last = e.transpose(
                                    pacc[3][:, i4 * 128:(i4 + 1) * 128],
                                    s32s[:, i4 * 128:(i4 + 1) * 128],
                                    c_idf[:, :])
                            return last
                        P.emit("pe", trp, wait=[("c_dve", ev_n)], inc="c_pe")
                        tn = P.n["c_pe"]
                        stt = s32t[qs % 2]
                        P.emit("dve", lambda e, stt=stt: e.tensor_copy(
                            stt[:, :], pacc[3][:, :]),
                            wait=[("c_pe", tn), ("d_scr", s32_dma[qs % 2])],
                            inc="c_dve")
                        vn = P.n["c_dve"]
                        P.dma("syn",
                              sS32.rearrange("(qs i p) o -> p qs i o",
                                             i=4, p=128)[
                                  :, qs, :, oc * 128:(oc + 1) * 128],
                              stt[:, :].rearrange("p (i o) -> p i o", i=4),
                              "d_scr", wait=[("c_dve", vn)])
                        s32_dma[qs % 2] = P.n["d_scr"]
                P.run_block()

            with ExitStack() as p4p:
                ptr_b = mkpsum(p4p, "ptrb4", 2, [128, 1024], BF16)
                wf3 = [sb(p4p, f"wf3_{i}", [128, INTER], F32)
                       for i in range(3)]
                wpb = sb(p4p, "wpb", [128, INTER], BF16)
                wts = [sb(p4p, f"wts{i}", [128, 8 * 128], BF16)
                       for i in range(2)]
                wt_dma = [0, 0]
                ev_hist = [0, 0]
                trw_hist = 0
                cast_hist = 0
                s5_hist = [(0, 0)] * 18
                for r in range(16):
                    L = wf3[r % 3]
                    Sp = wf3[(r + 2) % 3]
                    P.dma("syn", L[:, 0:4096],
                          wdown[r * 128:(r + 1) * 128, 0:4096], "d_w",
                          wait=[("c_dve", s5_hist[r][0]),
                                ("c_pool", s5_hist[r][1])])
                    P.dma("syn", L[:, 4096:8192],
                          wdown[r * 128:(r + 1) * 128, 4096:8192], "d_w")
                    dw = P.n["d_w"]
                    for stg in range(6):
                        bit = 1 << stg
                        src, dst = (L, Sp) if stg % 2 == 0 else (Sp, L)
                        sv = src[:, :].rearrange("p (a s r) -> p a s r",
                                                 s=2, r=bit * 128)
                        dv = dst[:, :].rearrange("p (a s r) -> p a s r",
                                                 s=2, r=bit * 128)
                        pd, pp = P.n["c_dve"], P.n["c_pool"]
                        wd = [("c_pool", pp)]
                        wp = [("c_dve", pd)]
                        if stg == 0:
                            wd += [("d_w", dw), ("c_act", cast_hist)]
                            wp += [("d_w", dw), ("c_act", cast_hist)]
                        P.emit("dve", lambda e, sv=sv, dv=dv:
                               e.tensor_tensor(dv[:, :, 0, :], sv[:, :, 0, :],
                                               sv[:, :, 1, :], op=ALU.add),
                               wait=wd, inc="c_dve")
                        P.emit("pool", lambda e, sv=sv, dv=dv:
                               e.tensor_tensor(dv[:, :, 1, :], sv[:, :, 0, :],
                                               sv[:, :, 1, :],
                                               op=ALU.subtract),
                               wait=wp, inc="c_pool")
                    s5_dve, s5_pool = P.n["c_dve"], P.n["c_pool"]
                    s5_hist[r + 2] = (s5_dve, s5_pool)
                    for hv in range(2):
                        P.emit("act", lambda e, hv=hv, L=L: e.activation(
                            wpb[:, hv * 4096:(hv + 1) * 4096],
                            L[:, hv * 4096:(hv + 1) * 4096], AF.Identity,
                            bias=0.0, scale=2.0 ** -6),
                            wait=[("c_pool", s5_pool), ("c_dve", s5_dve),
                                  ("c_pe", trw_hist)] if hv == 0 else None,
                            inc="c_act")
                    cast_hist = P.n["c_act"]
                    cn = P.n["c_act"]
                    for g8 in range(8):
                        pb = ptr_b[g8 % 2]
                        w2 = [("c_act", cn), ("c_dve", ev_hist[g8 % 2])]

                        def trw(e, g8=g8, pb=pb):
                            for cc in range(8):
                                fc = g8 * 8 + cc
                                last = e.transpose(
                                    pb[:, cc * 128:(cc + 1) * 128],
                                    wpb[:, fc * 128:(fc + 1) * 128],
                                    c_idb[:, :])
                            return last
                        P.emit("pe", trw, wait=w2, inc="c_pe")
                        pn = P.n["c_pe"]
                        stg_b = wts[g8 % 2]
                        P.emit("dve", lambda e, stg_b=stg_b, pb=pb:
                               e.tensor_copy(stg_b[:, :], pb[:, :]),
                               wait=[("c_pe", pn),
                                     ("d_scr", wt_dma[g8 % 2])],
                               inc="c_dve")
                        vn = P.n["c_dve"]
                        ev_hist[g8 % 2] = P.n["c_dve"]
                        P.dma("syn",
                              sWdT[:, r, g8 * 8:(g8 + 1) * 8, :],
                              wts[g8 % 2][:, :].rearrange(
                                  "p (fc o) -> p fc o", fc=8),
                              "d_scr", wait=[("c_dve", vn)])
                        wt_dma[g8 % 2] = P.n["d_scr"]
                    trw_hist = P.n["c_pe"]
                P.run_block()

            with ExitStack() as p4a:
                pacc = mkpsum(p4a, "pacc4a", 6)
                ptw = mkpsum(p4a, "ptw", 2, [128, 1024], BF16)
                u1b = sb(p4a, "u1b", [128, NF * TOK], BF16)
                hsu2 = sb(p4a, "hsu2", [128, NI * 128], F32)
                wuf = [sb(p4a, f"wuf{i}", [128, H], F32) for i in range(2)]
                wgf = [sb(p4a, f"wgf{i}", [128, H], F32) for i in range(2)]
                wub = [sb(p4a, f"wub{i}", [128, H], BF16) for i in range(2)]
                wgb = [sb(p4a, f"wgb{i}", [128, H], BF16) for i in range(2)]
                wuT = [sb(p4a, f"wuT{i}", [128, NF * 128], BF16)
                       for i in range(2)]
                wgT = [sb(p4a, f"wgT{i}", [128, NF * 128], BF16)
                       for i in range(2)]
                sgb = [sb(p4a, f"sgb{i}", [128, 512], F32) for i in range(2)]
                x2s = [sb(p4a, f"x2s{i}", [128, 512], F32) for i in range(2)]
                x2a_s = [sb(p4a, f"x2as{i}", [128, 512], BF16)
                         for i in range(2)]
                P.dma("syn", u1b[:, :].rearrange("p (c t) -> p c t", c=NF),
                      sU1.rearrange("c p t -> p c t"), "d_scr")
                u1_rdy = P.n["d_scr"]
                for c in range(NI):
                    P.emit("dve", lambda e, c=c: e.tensor_scalar(
                        hsu2[:, c * 128:(c + 1) * 128], c_h128[:, :],
                        c_su2[:, c:c + 1], 2.0 ** -6,
                        op0=ALU.mult, op1=ALU.mult), inc="c_dve")
                hsu_done = P.n["c_dve"]
                dw_n = [0, 0]
                cast_hist = [0, 0]
                trw_hist = [0, 0]
                wT_done = [0, 0]
                tw_ev = [0, 0]
                mm_hist = [0, 0]
                silu_hist = [0, 0]
                stt_hist = [0, 0]
                sa_hist = [0, 0]
                ev_hist = [0, 0]
                x2a_dma = [0, 0]
                tw_idx = 0

                tw_state = {"idx": 0}

                def load_w(oc):
                    b = oc % 2
                    P.dma("syn", wuf[b][:, :],
                          wup[oc * 128:(oc + 1) * 128, :], "d_w",
                          wait=[("c_pool", cast_hist[b])])
                    P.dma("syn", wgf[b][:, :],
                          wgate[oc * 128:(oc + 1) * 128, :], "d_w")
                    dw_n[b] = P.n["d_w"]

                def prep_w(oc):
                    b = oc % 2
                    P.emit("pool", lambda e, b=b: e.tensor_scalar(
                        wub[b][:, :], wuf[b][:, :], 2.0 ** -6, None,
                        op0=ALU.mult),
                        wait=[("d_w", dw_n[b]), ("c_pe", trw_hist[b])],
                        inc="c_pool")
                    P.emit("pool", lambda e, b=b: e.tensor_scalar(
                        wgb[b][:, :], wgf[b][:, :], 2.0 ** -6, None,
                        op0=ALU.mult), inc="c_pool")
                    cast_hist[b] = P.n["c_pool"]
                    cn = P.n["c_pool"]
                    for (srcb, dstT) in ((wub, wuT), (wgb, wgT)):
                        for g in range(2):
                            ti = tw_state["idx"]
                            bank = ptw[ti % 2]
                            w2 = [("c_pool", cn),
                                  ("c_dve", tw_ev[ti % 2])]

                            def trg(e, srcb=srcb, b=b, g=g, bank=bank):
                                for cc in range(8):
                                    fc = g * 8 + cc
                                    last = e.transpose(
                                        bank[:, cc * 128:(cc + 1) * 128],
                                        srcb[b][:, fc * 128:(fc + 1) * 128],
                                        c_idb[:, :])
                                return last
                            P.emit("pe", trg, wait=w2, inc="c_pe")
                            pn = P.n["c_pe"]
                            P.emit("dve",
                                   lambda e, dstT=dstT, b=b, g=g, bank=bank:
                                   e.tensor_copy(
                                       dstT[b][:, g * 1024:(g + 1) * 1024],
                                       bank[:, :]),
                                   wait=[("c_pe", max(pn, mm_hist[b]))],
                                   inc="c_dve")
                            tw_ev[ti % 2] = P.n["c_dve"]
                            tw_state["idx"] = ti + 1
                    trw_hist[b] = P.n["c_pe"]
                    wT_done[b] = P.n["c_dve"]

                def emit_tail(k):
                    oc, ts = divmod(k, 4)
                    tp_ = ts % 2
                    pa_, pb_, pc_ = pacc[tp_], pacc[2 + tp_], pacc[4 + tp_]
                    mm_n = mm_of[k]
                    sg = sgb[tp_]
                    P.emit("act", lambda e, oc=oc, sg=sg, pb_=pb_:
                           e.activation(
                               sg[:, :], pb_[:, :], AF.Silu,
                               scale=c_vuvg[:, 64 + oc:65 + oc]),
                           wait=[("c_pe", mm_n),
                                 ("c_dve", stt_hist[tp_])],
                           inc="c_act")
                    silu_hist[tp_] = P.n["c_act"]
                    xs = x2s[tp_]
                    P.emit("dve", lambda e, oc=oc, xs=xs, sg=sg,
                           pa_=pa_: e.scalar_tensor_tensor(
                               xs[:, :], pa_[:, :],
                               c_vuvg[:, oc:oc + 1], sg[:, :],
                               op0=ALU.mult, op1=ALU.mult),
                           wait=[("c_act", silu_hist[tp_]),
                                 ("c_pe", sa_hist[tp_])],
                           inc="c_dve")
                    stt_hist[tp_] = P.n["c_dve"]
                    P.emit("pe", lambda e, oc=oc, xs=xs, pc_=pc_:
                           e.matmul(
                               pc_[:, :],
                               hsu2[:, oc * 128:(oc + 1) * 128],
                               xs[:, :], start=True, stop=True),
                           wait=[("c_dve", max(stt_hist[tp_],
                                               ev_hist[tp_]))],
                           inc="c_pe")
                    sa_hist[tp_] = P.n["c_pe"]
                    sa_n = P.n["c_pe"]
                    xa = x2a_s[tp_]
                    P.emit("dve", lambda e, xa=xa, pc_=pc_:
                           e.tensor_copy(xa[:, :], pc_[:, :]),
                           wait=[("c_pe", sa_n),
                                 ("d_scr", x2a_dma[tp_])],
                           inc="c_dve")
                    ev_hist[tp_] = P.n["c_dve"]
                    vn = P.n["c_dve"]
                    P.dma("syn",
                          sX2b[oc * 128:(oc + 1) * 128,
                               ts * 512:(ts + 1) * 512],
                          xa[:, :], "d_scr", wait=[("c_dve", vn)])
                    x2a_dma[tp_] = P.n["d_scr"]

                mm_of = {}
                load_w(0)
                load_w(1)
                prep_w(0)
                for k in range(NI * 4):
                    oc, ts = divmod(k, 4)
                    b = oc % 2
                    tp_ = ts % 2
                    pa_, pb_ = pacc[tp_], pacc[2 + tp_]
                    w = [("c_dve", max(wT_done[b], stt_hist[tp_],
                                       hsu_done if oc == 0 else 0)),
                         ("c_act", silu_hist[tp_])]
                    if k == 0:
                        w.append(("d_scr", u1_rdy))

                    def mmu(e, b=b, ts=ts, pa_=pa_, pb_=pb_):
                        for fc in range(NF):
                            e.matmul(pa_[:, :],
                                     wuT[b][:, fc * 128:(fc + 1) * 128],
                                     u1b[:, fc * TOK + ts * 512:
                                         fc * TOK + ts * 512 + 512],
                                     start=(fc == 0),
                                     stop=(fc == NF - 1))
                        for fc in range(NF):
                            last = e.matmul(
                                pb_[:, :],
                                wgT[b][:, fc * 128:(fc + 1) * 128],
                                u1b[:, fc * TOK + ts * 512:
                                    fc * TOK + ts * 512 + 512],
                                start=(fc == 0), stop=(fc == NF - 1))
                        return last
                    P.emit("pe", mmu, wait=w, inc="c_pe")
                    mm_hist[b] = P.n["c_pe"]
                    mm_of[k] = P.n["c_pe"]
                    if k > 0:
                        emit_tail(k - 1)
                    if ts == 1 and oc + 1 < NI:
                        prep_w(oc + 1)
                    if ts == 2 and oc + 2 < NI:
                        load_w(oc + 2)
                emit_tail(NI * 4 - 1)
                P.run_block()

            with ExitStack() as p4c:
                pacc = mkpsum(p4c, "pacc4c", 4)
                x2bb = [sb(p4c, f"x2bb{i}", [128, NI * 512], BF16)
                        for i in range(2)]
                wdts = [sb(p4c, f"wdtt{i}", [128, NI * 128], BF16)
                        for i in range(2)]
                x3s = [sb(p4c, f"x3s{i}", [128, 512], F32) for i in range(2)]
                gst = [sb(p4c, f"gst{i}", [128, 512], F32) for i in range(2)]
                x2b_r = sX2b.rearrange("(fc p) (ts t) -> p fc ts t",
                                       p=128, t=512)
                g_dma = [0, 0]
                wdt_pe = [0, 0]
                mm_hist = [0, 0]
                ev_hist = [0, 0]
                tr_hist = [0, 0]
                silu_hist = [0, 0]
                xr = [0, 0]

                def load_x2b(ts):
                    P.dma("syn",
                          x2bb[ts % 2][:, :].rearrange(
                              "p (c t) -> p c t", c=NI),
                          x2b_r[:, :, ts, :], "d_scr",
                          wait=[("c_pe", mm_hist[ts % 2])])
                    xr[ts % 2] = P.n["d_scr"]

                mm_of = {}

                def emit_tail_c(k):
                    ts, oc = divmod(k, 16)
                    pa_ = pacc[oc % 2]
                    x3 = x3s[oc % 2]
                    P.emit("dve", lambda e, oc=oc, x3=x3, pa_=pa_:
                           e.tensor_scalar(
                               x3[:, :], pa_[:, :],
                               c_vuvg[:, 128 + oc:129 + oc], None,
                               op0=ALU.mult),
                           wait=[("c_pe", max(mm_of[k],
                                              tr_hist[oc % 2]))],
                           inc="c_dve")
                    ev_hist[oc % 2] = P.n["c_dve"]
                    ev_n = P.n["c_dve"]
                    ptr_ = pacc[2 + oc % 2]

                    def trg(e, x3=x3, ptr_=ptr_):
                        for i4 in range(4):
                            last = e.transpose(
                                ptr_[:, i4 * 128:(i4 + 1) * 128],
                                x3[:, i4 * 128:(i4 + 1) * 128],
                                c_idf[:, :])
                        return last
                    P.emit("pe", trg,
                           wait=[("c_dve", ev_n),
                                 ("c_act", silu_hist[oc % 2])],
                           inc="c_pe")
                    tr_hist[oc % 2] = P.n["c_pe"]
                    tn = P.n["c_pe"]
                    gb = gst[oc % 2]
                    P.emit("act", lambda e, gb=gb, ptr_=ptr_:
                           e.activation(gb[:, :], ptr_[:, :], AF.Silu),
                           wait=[("c_pe", tn), ("d_scr", g_dma[oc % 2])],
                           inc="c_act")
                    silu_hist[oc % 2] = P.n["c_act"]
                    an = P.n["c_act"]
                    P.dma("syn",
                          sG.rearrange("(ts i p) o -> p ts i o",
                                       i=4, p=128)[
                              :, ts, :, oc * 128:(oc + 1) * 128],
                          gb[:, :].rearrange("p (i o) -> p i o", i=4),
                          "d_scr", wait=[("c_act", an)])
                    g_dma[oc % 2] = P.n["d_scr"]

                load_x2b(0)
                for k in range(64):
                    ts, oc = divmod(k, 16)
                    tb = ts % 2
                    b = oc % 2
                    if oc == 0 and ts + 1 < 4:
                        load_x2b(ts + 1)
                    P.dma("syn",
                          wdts[b][:, :].rearrange("p (c o) -> p c o",
                                                  c=NI),
                          sWdT[:, oc, :, :], "d_scr",
                          wait=[("c_pe", wdt_pe[b])])
                    dwn = P.n["d_scr"]
                    pa_ = pacc[oc % 2]

                    def mmd(e, b=b, tb=tb, pa_=pa_):
                        for fc in range(NI):
                            last = e.matmul(
                                pa_[:, :],
                                wdts[b][:, fc * 128:(fc + 1) * 128],
                                x2bb[tb][:, fc * 512:(fc + 1) * 512],
                                start=(fc == 0), stop=(fc == NI - 1))
                        return last
                    P.emit("pe", mmd,
                           wait=[("d_scr", max(dwn, xr[tb])),
                                 ("c_dve", ev_hist[oc % 2])],
                           inc="c_pe")
                    wdt_pe[b] = P.n["c_pe"]
                    mm_hist[tb] = P.n["c_pe"]
                    mm_of[k] = P.n["c_pe"]
                    if k > 0:
                        emit_tail_c(k - 1)
                emit_tail_c(63)
                P.run_block()

            with ExitStack() as p4d:
                grow = [sb(p4d, f"grow{i}", [128, H], F32) for i in range(2)]
                srow = [sb(p4d, f"srow{i}", [128, H], F32) for i in range(2)]
                xrow = [sb(p4d, f"xrow{i}", [128, H], F32) for i in range(2)]
                orow = [sb(p4d, f"orow{i}", [128, H], F32) for i in range(2)]
                comb_n = [0, 0]
                out_n = [0, 0]
                pool_n4 = [0, 0]
                for tt in range(16):
                    b = tt % 2
                    w = ([("c_dve", comb_n[b]),
                          ("c_pool", pool_n4[b])] if tt >= 2 else None)
                    P.dma("syn", grow[b][:, :],
                          sG[tt * 128:(tt + 1) * 128, :], "d_scr", wait=w)
                    P.dma("syn", srow[b][:, :],
                          sS32[tt * 128:(tt + 1) * 128, :], "d_scr")
                    P.dma("syn", xrow[b][:, :],
                          x_own[tt * 128:(tt + 1) * 128, :], "d_in")
                    dr, di = P.n["d_scr"], P.n["d_in"]

                    P.emit("dve", lambda e, b=b: e.tensor_tensor(
                        grow[b][:, :], grow[b][:, :], srow[b][:, :],
                        op=ALU.mult),
                        wait=[("d_scr", dr), ("d_in", di),
                              ("d_out", out_n[b])], inc="c_dve")
                    mul_n = P.n["c_dve"]
                    P.emit("pool", lambda e, b=b: e.tensor_tensor(
                        orow[b][:, :], grow[b][:, :], xrow[b][:, :],
                        op=ALU.add),
                        wait=[("c_dve", mul_n)], inc="c_pool")
                    comb_n[b] = P.n["c_dve"]
                    pool_n4[b] = P.n["c_pool"]
                    vn = P.n["c_pool"]
                    P.dma("syn", out_own[tt * 128:(tt + 1) * 128, :],
                          orow[b][:, :], "d_out", wait=[("c_pool", vn)])
                    out_n[b] = P.n["d_out"]
                if tap is not None:
                    tsrc = scr_map[tap]
                    shp = list(tsrc.tensor.shape)
                    tout = nc.dram_tensor(
                        "tapout", shp, tsrc.dtype,
                        kind="ExternalOutput").ap()
                    nch = max(1, shp[0] // 2048)
                    step = shp[0] // nch
                    for ci in range(nch):
                        P.dma("syn", tout[ci * step:(ci + 1) * step],
                              tsrc[ci * step:(ci + 1) * step], "d_out")
                P.run_block()
    return nc


def make_in_maps(x, w_qkv, b_qkv, w_proj, b_proj, su_ug, su_dp, ws_ug, ws_dp,
                 w_up, w_gate, w_down):
    h128 = _hadamard(128).astype(np.float32)
    identf = np.eye(128, dtype=np.float32)

    shared = {
        "w_qkv": np.ascontiguousarray(w_qkv, np.float32),
        "w_proj": np.ascontiguousarray(w_proj, np.float32),
        "wup": np.ascontiguousarray(w_up, np.float32),
        "wgate": np.ascontiguousarray(w_gate, np.float32),
        "wdown": np.ascontiguousarray(w_down, np.float32),
        "bqkv_cols": np.ascontiguousarray(
            np.asarray(b_qkv, np.float32).reshape(48, 128).T),
        "bvrep": np.ascontiguousarray(np.broadcast_to(
            np.asarray(b_qkv, np.float32)[4096:6144], (128, 2048))),
        "bproj_cols": np.ascontiguousarray(
            np.asarray(b_proj, np.float32).reshape(16, 128).T),
        "su1_cols": np.ascontiguousarray(
            np.asarray(su_ug, np.float32).reshape(16, 128).T),
        "su2_cols": np.ascontiguousarray(
            np.asarray(su_dp, np.float32).reshape(64, 128).T),
        "ws_cols": np.ascontiguousarray(np.concatenate([
            np.asarray(ws_ug, np.float32)[:8192].reshape(64, 128).T,
            np.asarray(ws_ug, np.float32)[8192:].reshape(64, 128).T,
            np.asarray(ws_dp, np.float32).reshape(16, 128).T], axis=1)),
        "h128": h128,
        "ident_f": identf,
        "ident_b": identf.astype(ml_dtypes.bfloat16),
    }
    in_maps = []
    for c in range(8):
        bi, pa = c // 2, c % 2
        qgs = [1024 * l + 512 * pa for l in range(4)]
        x_own = np.ascontiguousarray(
            np.concatenate([x[bi, qg:qg + 512] for qg in qgs], axis=0))
        x_kv = np.ascontiguousarray(np.roll(x[bi], -512 * pa, axis=0))
        expb = np.zeros((NH, NSLOT_RECT), np.float32)
        for hh in range(NH):
            s = 0
            for l in range(4):
                for kt in rect_kts(l):
                    if kt < 28:
                        expb[hh, s] = -(SLOPES[hh] * 1024 * l + MCONST)
                    elif pa == 1:
                        expb[hh, s] = -(SLOPES[hh] * (4096 + 1024 * l)
                                        + MCONST)
                    else:
                        expb[hh, s] = NEGBIG
                    s += 1
        m = dict(shared)
        m["x_own"] = x_own
        m["x_kv"] = x_kv
        m["expb"] = np.ascontiguousarray(np.broadcast_to(
            expb.reshape(1, -1), (128, NH * NSLOT_RECT)))
        in_maps.append(m)
    return in_maps


def kernel(L0_attn_hidden_states, L0_attn_w_qkv, L0_attn_b_qkv,
           L0_attn_w_proj, L0_attn_b_proj, L0_ffn_SU_ug, L0_ffn_SU_dp,
           L0_ffn_Wscale_ug, L0_ffn_Wscale_dp, L0_ffn_W_up, L0_ffn_W_gate,
           L0_ffn_W_down):
    x = np.ascontiguousarray(L0_attn_hidden_states, dtype=np.float32)
    in_maps = make_in_maps(x, L0_attn_w_qkv, L0_attn_b_qkv, L0_attn_w_proj,
                           L0_attn_b_proj, L0_ffn_SU_ug, L0_ffn_SU_dp,
                           L0_ffn_Wscale_ug, L0_ffn_Wscale_dp, L0_ffn_W_up,
                           L0_ffn_W_gate, L0_ffn_W_down)
    nc = build()
    res = run_bass_kernel_spmd(nc, in_maps, core_ids=list(range(8)))
    out = np.empty((B, S, H), np.float32)
    for c in range(8):
        bi, pa = c // 2, c % 2
        shard = res.results[c]["out_own"]
        for l in range(4):
            qg = 1024 * l + 512 * pa
            out[bi, qg:qg + 512] = shard[l * 512:(l + 1) * 512]
    return out


# version 13
# speedup vs baseline: 1.2024x; 1.2024x over previous
import math
from contextlib import ExitStack

import numpy as np
import ml_dtypes

import concourse.bass as bass
import concourse.mybir as mybir
from concourse.bass_utils import run_bass_kernel_spmd

F32 = mybir.dt.float32
BF16 = mybir.dt.bfloat16
AF = mybir.ActivationFunctionType
ALU = mybir.AluOpType
AX = mybir.AxisListType

B, S, H, NH, HD, INTER = 4, 4096, 2048, 16, 128, 8192
NF = H // 128
NI = INTER // 128
TOK = 2048
NEGBIG = -1.0e30
MCONST = 32.0
SQ2INV64 = 45.254834
SLOPES = [2.0 ** (-(i + 1) / 2.0) for i in range(NH)]
EPS = 1e-5
KQOFF = 31 * 128


def rect_kts(l):
    return list(range(8 * l)) + [28, 29, 30, 31]


RECT_BASE = [0, 4, 16, 36]
NSLOT_RECT = 64


def _hadamard(n):
    h = np.array([[1.0]])
    while h.shape[0] < n:
        h = np.block([[h, h], [h, -h]])
    return h


class Prog:
    def __init__(self, nc, es):
        self.nc = nc
        self.sems = {}
        self.n = {}
        for name in ("c_pe", "c_dve", "c_act", "c_pool",
                     "d_in", "d_w", "d_scr", "d_out"):
            self.sems[name] = es.enter_context(nc.semaphore(name))
            self.n[name] = 0
        self.new_walk()

    def new_walk(self):
        self.ops = {e: [] for e in ("pe", "dve", "act", "pool", "syn")}

    def emit(self, eng, fn, inc=None, inck=1, wait=None):
        if wait:
            for (sname, val) in ([wait] if isinstance(wait, tuple) else wait):
                if val > 0:
                    sem = self.sems[sname]
                    self.ops[eng].append(
                        lambda e, s=sem, v=val: e.wait_ge(s, v))
        if inc is not None:
            sem = self.sems[inc]
            self.n[inc] += inck

            def run(e, f=fn, s=sem, k=inck):
                f(e).then_inc(s, k)
            self.ops[eng].append(run)
        else:
            self.ops[eng].append(fn)

    def dma(self, eng, out_ap, in_ap, sem="d_in", wait=None):
        self.emit(eng, lambda e, o=out_ap, i=in_ap: e.dma_start(o, i),
                  inc=sem, inck=16, wait=wait)

    def run_block(self):
        for sname in ("d_in", "d_w", "d_scr", "d_out"):
            if self.n[sname] > 0:
                sem, v = self.sems[sname], self.n[sname]
                self.ops["syn"].append(lambda e, s=sem, vv=v: e.wait_ge(s, vv))
        ops = self.ops
        with self.nc.Block() as blk:
            if ops["syn"]:
                @blk.sync
                def _(e, lst=list(ops["syn"])):
                    for f in lst:
                        f(e)
            if ops["pe"]:
                @blk.tensor
                def _(e, lst=list(ops["pe"])):
                    for f in lst:
                        f(e)
            if ops["dve"]:
                @blk.vector
                def _(e, lst=list(ops["dve"])):
                    for f in lst:
                        f(e)
            if ops["act"]:
                @blk.scalar
                def _(e, lst=list(ops["act"])):
                    for f in lst:
                        f(e)
            if ops["pool"]:
                @blk.gpsimd
                def _(e, lst=list(ops["pool"])):
                    for f in lst:
                        f(e)
        self.new_walk()


def build(tap=None):
    nc = bass.Bass("TRN2", target_bir_lowering=False)

    def din(name, shape, dt=F32):
        return nc.dram_tensor(name, shape, dt, kind="ExternalInput").ap()

    x_own = din("x_own", [TOK, H])
    x_kv = din("x_kv", [S, H])
    w_qkv = din("w_qkv", [H, 3 * H])
    w_proj = din("w_proj", [H, H])
    wup = din("wup", [INTER, H])
    wgate = din("wgate", [INTER, H])
    wdown = din("wdown", [H, INTER])
    bqkv_cols = din("bqkv_cols", [128, 48])
    bvrep_in = din("bvrep", [128, H])
    bproj_cols = din("bproj_cols", [128, 16])
    su1_cols = din("su1_cols", [128, NF])
    su2_cols = din("su2_cols", [128, NI])
    ws_cols = din("ws_cols", [128, 144])
    h128_in = din("h128", [128, 128])
    ident_b = din("ident_b", [128, 128], BF16)
    ident_f = din("ident_f", [128, 128])
    expb_in = din("expb", [128, NH * NSLOT_RECT])

    out_own = nc.dram_tensor("out_own", [TOK, H], F32,
                             kind="ExternalOutput").ap()

    def mkscr(name, shape, dt):
        return nc.dram_tensor(name, shape, dt, kind="Internal").ap()

    sK = mkscr("sK", [NH, 128, S], BF16)
    sV = mkscr("sV", [NH, S, HD], BF16)
    sKo = mkscr("sKo", [NH, 128, TOK], BF16)
    sVo = mkscr("sVo", [NH, TOK, HD], BF16)
    sQ = mkscr("sQ", [NH, 128, TOK], BF16)
    sU1 = mkscr("sU1", [NF, 128, TOK], BF16)
    sSp = mkscr("sSp", [NH, 128, TOK], BF16)
    sS32 = mkscr("sS32", [TOK, H], F32)
    sX2b = mkscr("sX2b", [INTER, TOK], BF16)
    sWdT = mkscr("sWdT", [128, 16, NI, 128], BF16)
    sG = mkscr("sG", [TOK, H], F32)
    sXN = mkscr("sXN", [NF, 128, 2048], BF16)
    scr_map = {"sXN": sXN,"sK": sK, "sV": sV, "sKo": sKo, "sVo": sVo, "sQ": sQ,
               "sU1": sU1, "sSp": sSp, "sS32": sS32,
               "sX2b": sX2b, "sG": sG}

    with ExitStack() as es:
        P = Prog(nc, es)

        def sb(stk, name, shape, dt):
            return stk.enter_context(nc.sbuf_tensor(name, shape, dt))

        c_bqkv = sb(es, "c_bqkv", [128, 48], F32)
        c_bproj = sb(es, "c_bproj", [128, 16], F32)
        c_su2 = sb(es, "c_su2", [128, NI], F32)
        c_vuvg = sb(es, "c_vuvg", [128, 144], F32)
        c_idb = sb(es, "c_idb", [128, 128], BF16)
        c_idf = sb(es, "c_idf", [128, 128], F32)
        c_h128 = sb(es, "c_h128", [128, 128], F32)
        ones_b = sb(es, "ones_b", [128, 1], BF16)
        ones_f = sb(es, "ones_f", [1, 128], F32)
        def mkpsum(stk, pref, n, shape=None, dt=F32):
            return [stk.enter_context(nc.psum_tensor(
                f"{pref}{i}", shape or [128, 512], dt)) for i in range(n)]

        with ExitStack() as ph:
            csu1 = sb(ph, "csu1", [128, NF], F32)
            P.dma("syn", c_bqkv[:, :], bqkv_cols[:, :])
            P.dma("syn", c_bproj[:, :], bproj_cols[:, :])
            P.dma("syn", csu1[:, :], su1_cols[:, :])
            P.dma("syn", c_su2[:, :], su2_cols[:, :])
            P.dma("syn", c_vuvg[:, :], ws_cols[:, :])
            P.dma("syn", c_idb[:, :], ident_b[:, :])
            P.dma("syn", c_idf[:, :], ident_f[:, :])
            P.dma("syn", c_h128[:, :], h128_in[:, :])
            P.emit("pool", lambda e: e.memset(ones_b[:, :], 1.0))
            P.emit("pool", lambda e: e.memset(ones_f[:, :], 1.0))
            d0 = P.n["d_in"]
            P.emit("dve", lambda e: e.tensor_scalar(
                c_vuvg[:, :], c_vuvg[:, :], SQ2INV64, None, op0=ALU.mult),
                wait=("d_in", d0), inc="c_dve")
            hsu1_holder = {}
            P.run_block()

            with ExitStack() as p1:
                pacc = mkpsum(p1, "pacc", 4)
                ptr_b = mkpsum(p1, "ptrb", 2, [128, 1024], BF16)
                TW = 2048
                xn_buf = sb(p1, "xn_buf", [128, NF * TW], BF16)
                xnt = sb(p1, "xnt", [128, H], BF16)
                ln_sm = sb(p1, "ln_sm", [128, 228], F32)
                xst = [sb(p1, f"xst{i}", [128, H], F32) for i in range(2)]
                c_bvrep = sb(p1, "c_bvrep", [128, H], F32)
                hsu1 = sb(p1, "hsu1", [128, NF * 128], BF16)
                wf_a = [sb(p1, f"wf_a{i}", [128, NF * 128], F32)
                        for i in range(2)]
                scrap = wf_a[0]
                wb_a = [sb(p1, f"wba{i}", [128, NF * 128], BF16)
                        for i in range(2)]
                wf_v = sb(p1, "wf_v", [128, NF * 256], F32)
                wb_v = [sb(p1, f"wbv{i}", [128, NF * 512], BF16)
                        for i in range(2)]
                kst = [sb(p1, f"kst{i}", [128, 512], BF16) for i in range(2)]
                vst = [sb(p1, f"vst{i}", [128, 512], BF16) for i in range(2)]
                u1t = [sb(p1, f"u1t{i}", [128, NF * 128], F32)
                       for i in range(2)]
                u1s = [sb(p1, f"u1s{i}", [128, 128], BF16) for i in range(2)]

                P.dma("syn", c_bvrep[:, :], bvrep_in[:, :])
                for c in range(NF):
                    P.emit("dve", lambda e, c=c: e.tensor_scalar(
                        hsu1[:, c * 128:(c + 1) * 128], c_h128[:, :],
                        csu1[:, c:c + 1], 2.0 ** -5,
                        op0=ALU.mult, op1=ALU.mult), inc="c_dve")

                P.emit("pool", lambda e: e.memset(ln_sm[:, 224:225], EPS),
                       inc="c_pool")
                xn3 = xn_buf[:, :].rearrange("p (c t) -> p c t", c=NF)
                st = {"norm": [0, 0], "ev": [0, 0], "tr": 0, "sums": [0, 0]}

                def ln_pass(src_ap, n_tiles):
                    for i in range(n_tiles):
                        xb = xst[i % 2]
                        P.dma("syn", xb[:, :],
                              src_ap[i * 128:(i + 1) * 128, :], "d_in",
                              wait=[("c_dve", st["sums"][i % 2])])
                        dn = P.n["d_in"]
                        P.emit("dve", lambda e, xb=xb, i=i: e.reduce_sum(
                            ln_sm[:, i:i + 1], xb[:, :], axis=AX.X,
                            negate=True),
                            wait=("d_in", dn), inc="c_dve")
                        P.emit("dve", lambda e, xb=xb, i=i:
                               e.scalar_tensor_tensor(
                                   scrap[:, :], xb[:, :], 1.0, xb[:, :],
                                   op0=ALU.mult, op1=ALU.mult,
                                   accum_out=ln_sm[:, 16 + i:17 + i]),
                               inc="c_dve")
                        st["sums"][i % 2] = P.n["c_dve"]
                    NM, S1, M2, A_, LN_, LNC, R0, T3, T4, T5, RS, NMA = (
                        32, 48, 64, 80, 96, 112, 128, 144, 160, 176, 192, 208)
                    nt = n_tiles
                    vd = P.n["c_dve"]
                    P.emit("act", lambda e: e.activation(
                        ln_sm[:, NM:NM + nt], ln_sm[:, 0:nt], AF.Identity,
                        bias=0.0, scale=1.0 / H),
                        wait=("c_dve", vd), inc="c_act")
                    P.emit("act", lambda e: e.activation(
                        ln_sm[:, S1:S1 + nt], ln_sm[:, 16:16 + nt],
                        AF.Identity, bias=ln_sm[:, 224:225],
                        scale=1.0 / H), inc="c_act")
                    va = P.n["c_act"]
                    P.emit("dve", lambda e: e.tensor_tensor(
                        ln_sm[:, M2:M2 + nt], ln_sm[:, NM:NM + nt],
                        ln_sm[:, NM:NM + nt], op=ALU.mult),
                        wait=("c_act", va), inc="c_dve")
                    vd = P.n["c_dve"]
                    P.emit("pool", lambda e: e.tensor_tensor(
                        ln_sm[:, A_:A_ + nt], ln_sm[:, S1:S1 + nt],
                        ln_sm[:, M2:M2 + nt], op=ALU.subtract),
                        wait=[("c_act", va), ("c_dve", vd)], inc="c_pool")
                    vp = P.n["c_pool"]
                    P.emit("act", lambda e: e.activation(
                        ln_sm[:, LN_:LN_ + nt], ln_sm[:, A_:A_ + nt], AF.Ln),
                        wait=("c_pool", vp), inc="c_act")
                    va = P.n["c_act"]
                    P.emit("pool", lambda e: e.tensor_copy(
                        ln_sm[:, LNC:LNC + nt], ln_sm[:, LN_:LN_ + nt]),
                        wait=("c_act", va), inc="c_pool")
                    vp = P.n["c_pool"]
                    P.emit("act", lambda e: e.activation(
                        ln_sm[:, R0:R0 + nt], ln_sm[:, LNC:LNC + nt],
                        AF.Exp, scale=-0.5),
                        wait=("c_pool", vp), inc="c_act")
                    va = P.n["c_act"]
                    P.emit("dve", lambda e: e.tensor_tensor(
                        ln_sm[:, T3:T3 + nt], ln_sm[:, R0:R0 + nt],
                        ln_sm[:, R0:R0 + nt], op=ALU.mult),
                        wait=("c_act", va), inc="c_dve")
                    vd = P.n["c_dve"]
                    P.emit("pool", lambda e: e.tensor_tensor(
                        ln_sm[:, T4:T4 + nt], ln_sm[:, T3:T3 + nt],
                        ln_sm[:, A_:A_ + nt], op=ALU.mult),
                        wait=("c_dve", vd), inc="c_pool")
                    vp = P.n["c_pool"]
                    P.emit("dve", lambda e: e.tensor_scalar(
                        ln_sm[:, T5:T5 + nt], ln_sm[:, T4:T4 + nt], -0.5,
                        1.5, op0=ALU.mult, op1=ALU.add),
                        wait=("c_pool", vp), inc="c_dve")
                    vd = P.n["c_dve"]
                    P.emit("pool", lambda e: e.tensor_tensor(
                        ln_sm[:, RS:RS + nt], ln_sm[:, T5:T5 + nt],
                        ln_sm[:, R0:R0 + nt], op=ALU.mult),
                        wait=("c_dve", vd), inc="c_pool")
                    vp = P.n["c_pool"]
                    P.emit("dve", lambda e: e.tensor_tensor(
                        ln_sm[:, NMA:NMA + nt], ln_sm[:, NM:NM + nt],
                        ln_sm[:, RS:RS + nt], op=ALU.mult),
                        wait=("c_pool", vp), inc="c_dve")
                    chain_d = P.n["c_dve"]
                    chain_p = P.n["c_pool"]
                    sums_done = chain_d
                    for i in range(n_tiles):
                        xb = xst[i % 2]
                        P.dma("syn", xb[:, :],
                              src_ap[i * 128:(i + 1) * 128, :], "d_in",
                              wait=[("c_act", st["norm"][i % 2]),
                                    ("c_dve", sums_done)])
                        dn = P.n["d_in"]
                        P.emit("act", lambda e, xb=xb, i=i: e.activation(
                            xnt[:, :], xb[:, :], AF.Identity,
                            bias=ln_sm[:, NMA + i:NMA + i + 1],
                            scale=ln_sm[:, RS + i:RS + i + 1]),
                            wait=[("d_in", dn), ("c_dve", chain_d),
                                  ("c_pool", chain_p), ("c_pe", st["tr"])],
                            inc="c_act")
                        st["norm"][i % 2] = P.n["c_act"]
                        a3 = P.n["c_act"]
                        for hf in range(2):
                            pb = ptr_b[hf]

                            def trs(e, hf=hf, pb=pb):
                                for cc in range(8):
                                    c = hf * 8 + cc
                                    last = e.transpose(
                                        pb[:, cc * 128:(cc + 1) * 128],
                                        xnt[:, c * 128:(c + 1) * 128],
                                        c_idb[:, :])
                                return last
                            P.emit("pe", trs,
                                   wait=[("c_act", a3),
                                         ("c_dve", st["ev"][hf])],
                                   inc="c_pe")
                            pn = P.n["c_pe"]
                            P.emit("dve",
                                   lambda e, hf=hf, pb=pb, i=i:
                                   e.tensor_copy(
                                       xn3[:, hf * 8:(hf + 1) * 8,
                                           i * 128:(i + 1) * 128],
                                       pb[:, :].rearrange(
                                           "p (c t) -> p c t", c=8)),
                                   wait=("c_pe", pn), inc="c_dve")
                            st["ev"][hf] = P.n["c_dve"]
                        st["tr"] = P.n["c_pe"]

                w_qkv3 = w_qkv.rearrange("(c p) o -> p c o", p=128)
                pool_n = {"a": 0, "v0": 0, "v1": 0}
                wb_pe = {}
                scr_n = {}
                OWN = {0: 0, 1024: 1, 2048: 2, 3072: 3}

                def kq_pass(col_off, bias_off, qscale, dst, own_dst, half):
                    for oc in range(NH):
                        b = oc % 2
                        P.dma("syn",
                              wf_a[b][:, :].rearrange("p (c o) -> p c o",
                                                      c=NF),
                              w_qkv3[:, :, col_off + oc * 128:
                                     col_off + (oc + 1) * 128],
                              "d_w", wait=[("c_pool", pool_n.get(
                                  ("a", b), 0))])
                        dw = P.n["d_w"]
                        P.emit("pool", lambda e, b=b: e.tensor_copy(
                            wb_a[b][:, :], wf_a[b][:, :]),
                            wait=[("d_w", dw), ("c_pe", wb_pe.get(b, 0))],
                            inc="c_pool")
                        pool_n[("a", b)] = P.n["c_pool"]
                        cn = P.n["c_pool"]
                        for ts in range(4):
                            pb = pacc[ts]
                            waits = [("c_pool", cn),
                                     ("c_dve", P.n["c_dve"] - 3)]

                            def mms(e, b=b, ts=ts, pb=pb):
                                for fc in range(NF):
                                    last = e.matmul(
                                        pb[:, :],
                                        wb_a[b][:, fc * 128:(fc + 1) * 128],
                                        xn_buf[:, fc * TW + ts * 512:
                                               fc * TW + ts * 512 + 512],
                                        start=(fc == 0), stop=(fc == NF - 1))
                                return last
                            P.emit("pe", mms, wait=waits, inc="c_pe")
                            pn = P.n["c_pe"]
                            wb_pe[b] = P.n["c_pe"]
                            kb = kst[ts % 2]
                            wv = [("c_pe", pn)]
                            if ("k", ts % 2) in scr_n:
                                wv.append(("d_scr", scr_n[("k", ts % 2)]))
                            if qscale is None:
                                P.emit("dve", lambda e, pb=pb, kb=kb, oc=oc:
                                       e.tensor_scalar(
                                           kb[:, :], pb[:, :],
                                           c_bqkv[:, bias_off + oc:
                                                  bias_off + oc + 1],
                                           None, op0=ALU.add),
                                       wait=wv, inc="c_dve")
                            else:
                                P.emit("dve", lambda e, pb=pb, kb=kb, oc=oc:
                                       e.tensor_scalar(
                                           kb[:, :], pb[:, :],
                                           c_bqkv[:, bias_off + oc:
                                                  bias_off + oc + 1],
                                           qscale, op0=ALU.add, op1=ALU.mult),
                                       wait=wv, inc="c_dve")
                            vn = P.n["c_dve"]
                            t0 = half * 2048 + ts * 512
                            P.dma("syn", dst[oc, :, t0:t0 + 512], kb[:, :],
                                  "d_scr", wait=[("c_dve", vn)])
                            if own_dst is not None and t0 in OWN:
                                l = OWN[t0]
                                P.dma("syn",
                                      own_dst[oc, :, l * 512:(l + 1) * 512],
                                      kb[:, :], "d_scr")
                            scr_n[("k", ts % 2)] = P.n["d_scr"]

                def v_pass(half):
                    for os_ in range(4):
                        b = os_ % 2
                        wbv3 = wb_v[b][:, :].rearrange(
                            "p (c o) -> p c o", c=NF)
                        for hv in range(2):
                            P.dma("syn",
                                  wf_v[:, :].rearrange(
                                      "p (c o) -> p c o", c=NF),
                                  w_qkv3[:, :, 2 * H + os_ * 512 + hv * 256:
                                         2 * H + os_ * 512 + hv * 256 + 256],
                                  "d_w",
                                  wait=[("c_pool", pool_n.get("vf", 0))])
                            dw = P.n["d_w"]
                            P.emit("pool", lambda e, b=b, hv=hv, wbv3=wbv3:
                                   e.tensor_copy(
                                       wbv3[:, :, hv * 256:(hv + 1) * 256],
                                       wf_v[:, :].rearrange(
                                           "p (c o) -> p c o", c=NF)),
                                   wait=[("d_w", dw),
                                         ("c_pe", wb_pe.get(("v", b), 0))],
                                   inc="c_pool")
                            pool_n["vf"] = P.n["c_pool"]
                        pool_n[f"v{b}"] = P.n["c_pool"]
                        cn = P.n["c_pool"]
                        for tsub in range(16):
                            pb = pacc[tsub % 4]
                            waits = [("c_pool", cn),
                                     ("c_dve", P.n["c_dve"] - 3)]

                            def mmv(e, b=b, tsub=tsub, pb=pb):
                                for fc in range(NF):
                                    last = e.matmul(
                                        pb[:, :],
                                        xn_buf[:, fc * TW + tsub * 128:
                                               fc * TW + tsub * 128 + 128],
                                        wb_v[b][:, fc * 512:(fc + 1) * 512],
                                        start=(fc == 0), stop=(fc == NF - 1))
                                return last
                            P.emit("pe", mmv, wait=waits, inc="c_pe")
                            pn = P.n["c_pe"]
                            wb_pe[("v", b)] = P.n["c_pe"]
                            vb = vst[tsub % 2]
                            wv = [("c_pe", pn)]
                            if ("v", tsub % 2) in scr_n:
                                wv.append(("d_scr", scr_n[("v", tsub % 2)]))
                            P.emit("dve", lambda e, pb=pb, vb=vb, os_=os_:
                                   e.tensor_tensor(
                                       vb[:, :], pb[:, :],
                                       c_bvrep[:, os_ * 512:(os_ + 1) * 512],
                                       op=ALU.add),
                                   wait=wv, inc="c_dve")
                            vn = P.n["c_dve"]
                            t0 = half * 2048 + tsub * 128
                            vr = vb[:, :].rearrange("p (h d) -> p h d", h=4)
                            for hh in range(4):
                                hg = os_ * 4 + hh
                                P.dma("syn", sV[hg, t0:t0 + 128, :],
                                      vr[:, hh], "d_scr",
                                      wait=[("c_dve", vn)] if hh == 0
                                      else None)
                            g0 = t0 - (t0 % 512)
                            if g0 in OWN:
                                tl = OWN[g0] * 512 + (t0 % 512)
                                for hh in range(4):
                                    hg = os_ * 4 + hh
                                    P.dma("syn", sVo[hg, tl:tl + 128, :],
                                          vr[:, hh], "d_scr")
                            scr_n[("v", tsub % 2)] = P.n["d_scr"]

                def u1_pass():
                    for ts in range(16):
                        ping, pong = u1t[0], u1t[1]
                        cp0 = P.n["c_pool"]
                        for c in range(NF):
                            pb = pacc[c % 4]
                            P.emit("pe", lambda e, c=c, ts=ts, pb=pb:
                                   e.matmul(
                                       pb[:, 0:128],
                                       hsu1[:, c * 128:(c + 1) * 128],
                                       xn_buf[:, c * TW + ts * 128:
                                              c * TW + ts * 128 + 128],
                                       start=True, stop=True),
                                   wait=[("c_dve", P.n["c_dve"] - 3)],
                                   inc="c_pe")
                            pn = P.n["c_pe"]
                            P.emit("dve", lambda e, c=c, pb=pb, ping=ping:
                                   e.tensor_copy(
                                       ping[:, c * 128:(c + 1) * 128],
                                       pb[:, 0:128]),
                                   wait=[("c_pe", pn), ("c_pool", cp0)],
                                   inc="c_dve")
                        for stg in range(4):
                            bit = 1 << stg

                            def bf(e, ping=ping, pong=pong, bit=bit):
                                for c in range(NF):
                                    lo, hi = c & ~bit, c | bit
                                    op = (ALU.add if (c & bit) == 0
                                          else ALU.subtract)
                                    last = e.tensor_tensor(
                                        pong[:, c * 128:(c + 1) * 128],
                                        ping[:, lo * 128:(lo + 1) * 128],
                                        ping[:, hi * 128:(hi + 1) * 128],
                                        op=op)
                                return last
                            eng = "dve" if stg % 2 == 0 else "pool"
                            cw = "c_pool" if stg % 2 == 0 else "c_dve"
                            P.emit(eng, bf, wait=(cw, P.n[cw]),
                                   inc="c_dve" if eng == "dve" else "c_pool")
                            ping, pong = pong, ping
                        bf_done_p = P.n["c_pool"]
                        for c in range(NF):
                            ub = u1s[c % 2]
                            wv = [("c_pool", bf_done_p)]
                            if ("u", c % 2) in scr_n:
                                wv.append(("d_scr", scr_n[("u", c % 2)]))
                            P.emit("dve", lambda e, c=c, ping=ping, ub=ub:
                                   e.tensor_copy(ub[:, :],
                                                 ping[:, c * 128:
                                                      (c + 1) * 128]),
                                   wait=wv, inc="c_dve")
                            vn = P.n["c_dve"]
                            P.dma("syn",
                                  sU1[c, :, ts * 128:(ts + 1) * 128],
                                  ub[:, :], "d_scr", wait=[("c_dve", vn)])
                            scr_n[("u", c % 2)] = P.n["d_scr"]

                for half in range(2):
                    ln_pass(x_kv[half * 2048:(half + 1) * 2048, :], 16)
                    P.run_block()
                    kq_pass(H, 16, None, sK, sKo, half)
                    v_pass(half)
                    P.run_block()
                ln_pass(x_own, 16)
                if tap == "sXN":
                    P.dma("syn", sXN.rearrange("c p t -> p c t"),
                          xn_buf[:, :].rearrange("p (c t) -> p c t", c=NF),
                          "d_scr",
                          wait=[("c_dve", P.n["c_dve"]),
                                ("c_pool", P.n["c_pool"]),
                                ("c_act", P.n["c_act"]),
                                ("c_pe", P.n["c_pe"])])
                P.run_block()
                kq_pass(0, 0, 2.0 ** -7, sQ, None, 0)
                u1_pass()
                P.run_block()

            with ExitStack() as p2:
                kqw = sb(p2, "kqw", [128, KQOFF + 512], F32)
                trb = sb(p2, "trb", [128, 4 * 512], F32)
                trm = sb(p2, "trm", [128, 4 * 512], F32)
                trh = sb(p2, "trh", [128, 4 * 512], F32)
                c_expb = sb(p2, "c_expb", [128, NH * NSLOT_RECT], F32)
                kbuf = [sb(p2, f"kbuf{i}", [128, S], BF16) for i in range(2)]
                kobuf = [sb(p2, f"kobuf{i}", [128, TOK], BF16)
                         for i in range(2)]
                vbuf = [sb(p2, f"vbuf{i}", [128, 32 * HD], BF16)
                        for i in range(2)]
                vobuf = [sb(p2, f"vobuf{i}", [128, 16 * HD], BF16)
                         for i in range(2)]
                qbuf = [sb(p2, f"qbuf{i}", [128, TOK], BF16)
                        for i in range(2)]
                s2b = [sb(p2, f"s2b{i}", [128, 512], F32) for i in range(2)]
                ptb = [sb(p2, f"ptb{i}", [128, 512], BF16) for i in range(2)]
                rrep = sb(p2, "rrep", [128, 512], F32)
                spst = [sb(p2, f"spst{i}", [128, 512], BF16)
                        for i in range(2)]

                P.dma("syn", c_expb[:, :], expb_in[:, :])
                P.emit("pool", lambda e: e.iota(
                    kqw[:, :], [[-1, KQOFF + 512]], base=KQOFF,
                    channel_multiplier=1,
                    allow_small_or_imprecise_dtypes=True), inc="c_pool")
                for t in range(4):
                    P.emit("pool", lambda e, t=t: e.iota(
                        trb[:, t * 512:(t + 1) * 512], [[-1, 512]],
                        base=128 * t, channel_multiplier=1,
                        allow_small_or_imprecise_dtypes=True), inc="c_pool")
                pn0 = P.n["c_pool"]
                P.emit("dve", lambda e: e.tensor_scalar(
                    trm[:, :], trb[:, :], 0.0, NEGBIG,
                    op0=ALU.is_gt, op1=ALU.mult),
                    wait=("c_pool", pn0), inc="c_dve")
                P.emit("dve", lambda e: e.tensor_scalar(
                    trm[:, :], trm[:, :], -MCONST, None, op0=ALU.add),
                    inc="c_dve")
                P.run_block()

                def head_dmas(h, b, guard_pe):
                    P.dma("syn", kbuf[b][:, :], sK[h, :, :], "d_scr",
                          wait=[("c_pe", guard_pe)])
                    P.dma("syn", kobuf[b][:, :], sKo[h, :, :], "d_scr")
                    P.dma("syn",
                          vbuf[b][:, :].rearrange("p (k d) -> p k d", k=32),
                          sV.rearrange("h (k p) d -> h p k d", p=128)[h],
                          "d_scr")
                    P.dma("syn",
                          vobuf[b][:, :].rearrange("p (k d) -> p k d", k=16),
                          sVo.rearrange("h (k p) d -> h p k d", p=128)[h],
                          "d_scr")
                    P.dma("syn", qbuf[b][:, :], sQ[h, :, :], "d_scr")

                SBK = mkpsum(p2, "p2s", 3)
                PVB = mkpsum(p2, "p2pv", 2)
                LB = mkpsum(p2, "p2l", 1)[0]
                rrep2 = [sb(p2, f"rrep2{i}", [128, 512], F32)
                         for i in range(2)]
                rrow = sb(p2, "rrow", [1, 512], F32)
                s2b3 = [sb(p2, f"s2b3{i}", [128, 512], F32) for i in range(3)]
                ptb3 = [sb(p2, f"ptb3{i}", [128, 512], BF16)
                        for i in range(3)]

                flat = []
                for h in range(NH):
                    for l in range(4):
                        kts = rect_kts(l)
                        slots = ([("rect", kt) for kt in kts]
                                 + [("tri", t) for t in range(4)])
                        nsl = len(slots)
                        for si, (kind, kt) in enumerate(slots):
                            flat.append(dict(
                                h=h, l=l, si=si, kind=kind, kt=kt,
                                first=(si == 0), last=(si == nsl - 1),
                                g=h * 4 + l, nsl=nsl,
                                col=(h * NSLOT_RECT + RECT_BASE[l] + si
                                     if kind == "rect" else -1)))
                NSL = len(flat)
                hist_qk = [0] * NSL
                hist_s2 = [0] * NSL
                hist_exp = [0] * NSL
                hist_pv = [0] * NSL
                hist_recip = {}
                hist_actcopy = {}
                hist_tt = {}
                head_ready = [0, 0]
                spst_dma = [0, 0]
                trh_hist = [0] * NH

                head_dmas(0, 0, 0)
                head_ready[0] = P.n["d_scr"]

                def emit_qk(i):
                    s = flat[i]
                    hb = s["h"] % 2
                    sbank = SBK[i % 3]
                    if s["kind"] == "rect":
                        koff, kb = s["kt"] * 128, kbuf[hb]
                    else:
                        koff, kb = s["l"] * 512 + s["kt"] * 128, kobuf[hb]
                    w = []
                    if s["first"] and s["l"] == 0:
                        w.append(("d_scr", head_ready[hb]))
                    if i >= 3:
                        w.append(("c_dve", hist_s2[i - 3]))
                    P.emit("pe", lambda e, kb=kb, koff=koff, sbank=sbank,
                           hb=hb, l=s["l"]: e.matmul(
                               sbank[:, :], kb[:, koff:koff + 128],
                               qbuf[hb][:, l * 512:(l + 1) * 512],
                               start=True, stop=True),
                           wait=w or None, inc="c_pe")
                    hist_qk[i] = P.n["c_pe"]

                def emit_s2(i):
                    s = flat[i]
                    sbank = SBK[i % 3]
                    s2 = s2b3[i % 3]
                    w = [("c_pe", hist_qk[i])]
                    if i >= 3:
                        w.append(("c_act", hist_exp[i - 3]))
                    if s["kind"] == "rect":
                        sl = KQOFF - s["kt"] * 128
                        P.emit("dve", lambda e, s2=s2, sl=sl, sbank=sbank,
                               h=s["h"]: e.scalar_tensor_tensor(
                                   s2[:, :], kqw[:, sl:sl + 512],
                                   SLOPES[h], sbank[:, :],
                                   op0=ALU.mult, op1=ALU.add),
                               wait=w, inc="c_dve")
                    else:
                        P.emit("dve", lambda e, s2=s2, kt=s["kt"],
                               sbank=sbank: e.tensor_tensor(
                                   s2[:, :], sbank[:, :],
                                   trh[:, kt * 512:(kt + 1) * 512],
                                   op=ALU.add),
                               wait=w, inc="c_dve")
                    hist_s2[i] = P.n["c_dve"]

                def emit_exp(i):
                    s = flat[i]
                    s2 = s2b3[i % 3]
                    pt = ptb3[i % 3]
                    w = [("c_dve", hist_s2[i])]
                    if i >= 3:
                        w.append(("c_pe", hist_pv[i - 3]))
                    if s["kind"] == "rect":
                        P.emit("act", lambda e, pt=pt, s2=s2, col=s["col"]:
                               e.activation(pt[:, :], s2[:, :], AF.Exp,
                                            bias=c_expb[:, col:col + 1],
                                            scale=1.0),
                               wait=w, inc="c_act")
                    else:
                        P.emit("act", lambda e, pt=pt, s2=s2: e.activation(
                            pt[:, :], s2[:, :], AF.Exp, bias=0.0, scale=1.0),
                            wait=w, inc="c_act")
                    hist_exp[i] = P.n["c_act"]

                def emit_pv(i):
                    s = flat[i]
                    hb = s["h"] % 2
                    g = s["g"]
                    pt = ptb3[i % 3]
                    vb = vbuf[hb] if s["kind"] == "rect" else vobuf[hb]
                    voff = (s["kt"] * 128 if s["kind"] == "rect"
                            else (s["l"] * 4 + s["kt"]) * 128)
                    pvb = PVB[g % 2]
                    w = [("c_act", hist_exp[i])]
                    if s["first"]:
                        if g - 2 in hist_tt:
                            w.append(("c_dve", hist_tt[g - 2]))
                        if g - 1 in hist_recip:
                            w.append(("c_dve", hist_recip[g - 1]))
                        if g - 1 in hist_actcopy:
                            w.append(("c_act", hist_actcopy[g - 1]))

                    def pv(e, vb=vb, voff=voff, pt=pt, si=s["si"],
                           nsl=s["nsl"], pvb=pvb):
                        e.matmul(pvb[:, :], vb[:, voff:voff + 128],
                                 pt[:, :], start=(si == 0),
                                 stop=(si == nsl - 1))
                        last = e.matmul(LB[0:1, :], ones_b[:, :], pt[:, :],
                                        start=(si == 0),
                                        stop=(si == nsl - 1))
                        return last
                    P.emit("pe", pv, wait=w, inc="c_pe")
                    hist_pv[i] = P.n["c_pe"]
                    if s["last"]:
                        hist_pv_last[s["g"]] = P.n["c_pe"]
                        emit_gend(s)

                def emit_gend(s):
                    g = s["g"]
                    h, l = s["h"], s["l"]
                    P.emit("dve", lambda e: e.reciprocal(
                        rrow[:, :], LB[0:1, :]),
                        wait=[("c_pe", hist_pv_last[g])], inc="c_dve")
                    hist_recip[g] = P.n["c_dve"]
                    P.emit("pe", lambda e: e.matmul(
                        LB[:, :], ones_f[:, :], rrow[:, :],
                        start=True, stop=True),
                        wait=[("c_dve", hist_recip[g])], inc="c_pe")
                    rr_n = P.n["c_pe"]
                    rr = rrep2[g % 2]
                    w = [("c_pe", rr_n)]
                    if g - 2 in hist_tt:
                        w.append(("c_dve", hist_tt[g - 2]))
                    P.emit("act", lambda e, rr=rr: e.activation(
                        rr[:, :], LB[:, :], AF.Identity, bias=0.0,
                        scale=1.0), wait=w, inc="c_act")
                    hist_actcopy[g] = P.n["c_act"]
                    sp = spst[g % 2]
                    P.emit("dve", lambda e, sp=sp, rr=rr, pvb=PVB[g % 2]:
                           e.tensor_tensor(sp[:, :], pvb[:, :], rr[:, :],
                                           op=ALU.mult),
                           wait=[("c_act", hist_actcopy[g]),
                                 ("d_scr", spst_dma[g % 2])],
                           inc="c_dve")
                    hist_tt[g] = P.n["c_dve"]
                    P.dma("syn", sSp[h, :, l * 512:(l + 1) * 512],
                          sp[:, :], "d_scr",
                          wait=[("c_dve", hist_tt[g])])
                    spst_dma[g % 2] = P.n["d_scr"]

                hist_pv_last = {}
                DEPTH = 3
                for i in range(NSL):
                    s = flat[i]
                    if s["first"] and s["l"] == 0:
                        h = s["h"]
                        if h + 1 < NH:
                            head_dmas(h + 1, (h + 1) % 2, P.n["c_pe"])
                            head_ready[(h + 1) % 2] = P.n["d_scr"]
                        P.emit("dve", lambda e, h=h: e.scalar_tensor_tensor(
                            trh[:, :], trb[:, :], SLOPES[h], trm[:, :],
                            op0=ALU.mult, op1=ALU.add), inc="c_dve")
                        trh_hist[h] = P.n["c_dve"]
                    if i >= DEPTH:
                        emit_pv(i - DEPTH)
                    emit_qk(i)
                    emit_s2(i)
                    emit_exp(i)
                for j in range(NSL - DEPTH, NSL):
                    emit_pv(j)
                P.run_block()

            with ExitStack() as p3:
                pacc = mkpsum(p3, "pacc3", 4)
                spre = sb(p3, "spre", [128, NH * TOK], BF16)
                wfp = sb(p3, "wfp", [128, NF * 128], F32)
                wbp = [sb(p3, f"wbp{i}", [128, NF * 128], BF16)
                       for i in range(2)]
                s32s = sb(p3, "s32s", [128, 512], F32)
                s32t = [sb(p3, f"s32t{i}", [128, 512], F32)
                        for i in range(2)]
                P.dma("syn", spre[:, :].rearrange("p (h q) -> p h q", h=NH),
                      sSp.rearrange("h p q -> p h q"), "d_scr")
                sp_rdy = P.n["d_scr"]
                w_proj3 = w_proj.rearrange("(c p) o -> p c o", p=128)
                pool_p = 0
                wbp_pe = {}
                s32_dma = [0, 0]
                for oc in range(16):
                    b = oc % 2
                    P.dma("syn",
                          wfp[:, :].rearrange("p (c o) -> p c o", c=NF),
                          w_proj3[:, :, oc * 128:(oc + 1) * 128], "d_w",
                          wait=[("c_pool", pool_p)])
                    dw = P.n["d_w"]
                    P.emit("pool", lambda e, b=b: e.tensor_copy(
                        wbp[b][:, :], wfp[:, :]),
                        wait=[("d_w", dw), ("c_pe", wbp_pe.get(b, 0))],
                        inc="c_pool")
                    pool_p = P.n["c_pool"]
                    cn = P.n["c_pool"]
                    for qs in range(4):
                        pb = pacc[qs % 3]
                        waits = [("c_pool", cn), ("d_scr", sp_rdy),
                                 ("c_dve", P.n["c_dve"] - 5)]

                        def mmp(e, b=b, qs=qs, pb=pb):
                            for fc in range(NF):
                                last = e.matmul(pb[:, :],
                                         wbp[b][:, fc * 128:(fc + 1) * 128],
                                         spre[:, fc * TOK + qs * 512:
                                              fc * TOK + qs * 512 + 512],
                                         start=(fc == 0),
                                         stop=(fc == NF - 1))
                            return last
                        P.emit("pe", mmp, wait=waits, inc="c_pe")
                        wbp_pe[b] = P.n["c_pe"]
                        pn = P.n["c_pe"]
                        P.emit("dve", lambda e, pb=pb, oc=oc: e.tensor_scalar(
                            s32s[:, :], pb[:, :], c_bproj[:, oc:oc + 1],
                            None, op0=ALU.add), wait=("c_pe", pn),
                            inc="c_dve")
                        ev_n = P.n["c_dve"]

                        def trp(e):
                            for i4 in range(4):
                                last = e.transpose(
                                    pacc[3][:, i4 * 128:(i4 + 1) * 128],
                                    s32s[:, i4 * 128:(i4 + 1) * 128],
                                    c_idf[:, :])
                            return last
                        P.emit("pe", trp, wait=[("c_dve", ev_n)], inc="c_pe")
                        tn = P.n["c_pe"]
                        stt = s32t[qs % 2]
                        P.emit("dve", lambda e, stt=stt: e.tensor_copy(
                            stt[:, :], pacc[3][:, :]),
                            wait=[("c_pe", tn), ("d_scr", s32_dma[qs % 2])],
                            inc="c_dve")
                        vn = P.n["c_dve"]
                        P.dma("syn",
                              sS32.rearrange("(qs i p) o -> p qs i o",
                                             i=4, p=128)[
                                  :, qs, :, oc * 128:(oc + 1) * 128],
                              stt[:, :].rearrange("p (i o) -> p i o", i=4),
                              "d_scr", wait=[("c_dve", vn)])
                        s32_dma[qs % 2] = P.n["d_scr"]
                P.run_block()

            with ExitStack() as p4p:
                ptr_b = mkpsum(p4p, "ptrb4", 2, [128, 1024], BF16)
                wfL = [sb(p4p, f"wfL{i}", [128, INTER], F32)
                       for i in range(2)]
                wfS = [sb(p4p, f"wfS{i}", [128, INTER], F32)
                       for i in range(2)]
                wpb = [sb(p4p, f"wpb{i}", [128, INTER], BF16)
                       for i in range(2)]
                wts = [sb(p4p, f"wts{i}", [128, 8 * 128], BF16)
                       for i in range(2)]
                wt_dma = [0, 0]
                ev_hist = [0, 0]
                trw_p = [0, 0]
                cast_p = [0, 0]
                for r in range(16):
                    pr = r % 2
                    eng = "dve" if pr == 0 else "pool"
                    cnt = "c_dve" if pr == 0 else "c_pool"
                    L, Sp = wfL[pr], wfS[pr]
                    P.dma("syn", L[:, 0:4096],
                          wdown[r * 128:(r + 1) * 128, 0:4096], "d_w",
                          wait=[("c_act", cast_p[pr])])
                    P.dma("syn", L[:, 4096:8192],
                          wdown[r * 128:(r + 1) * 128, 4096:8192], "d_w")
                    dw = P.n["d_w"]
                    for stg in range(6):
                        bit = 1 << stg
                        src, dst = (L, Sp) if stg % 2 == 0 else (Sp, L)
                        sv = src[:, :].rearrange("p (a s r) -> p a s r",
                                                 s=2, r=bit * 128)
                        dv = dst[:, :].rearrange("p (a s r) -> p a s r",
                                                 s=2, r=bit * 128)
                        w = None
                        if stg == 0:
                            w = [("d_w", dw)]
                        elif stg == 1:
                            w = [("c_act", cast_p[pr])]
                        P.emit(eng, lambda e, sv=sv, dv=dv:
                               e.tensor_tensor(dv[:, :, 0, :], sv[:, :, 0, :],
                                               sv[:, :, 1, :], op=ALU.add),
                               wait=w, inc=cnt)
                        P.emit(eng, lambda e, sv=sv, dv=dv:
                               e.tensor_tensor(dv[:, :, 1, :], sv[:, :, 0, :],
                                               sv[:, :, 1, :],
                                               op=ALU.subtract), inc=cnt)
                    chain_n = P.n[cnt]
                    for hv in range(2):
                        P.emit("act", lambda e, hv=hv, L=L, pr=pr:
                               e.activation(
                                   wpb[pr][:, hv * 4096:(hv + 1) * 4096],
                                   L[:, hv * 4096:(hv + 1) * 4096],
                                   AF.Identity, bias=0.0, scale=2.0 ** -6),
                               wait=[(cnt, chain_n),
                                     ("c_pe", trw_p[pr])] if hv == 0
                               else None,
                               inc="c_act")
                    cast_p[pr] = P.n["c_act"]
                    cn = P.n["c_act"]
                    for g8 in range(8):
                        pb = ptr_b[g8 % 2]
                        w2 = [("c_act", max(cn, ev_hist[g8 % 2]))]

                        def trw(e, g8=g8, pb=pb, pr=pr):
                            for cc in range(8):
                                fc = g8 * 8 + cc
                                last = e.transpose(
                                    pb[:, cc * 128:(cc + 1) * 128],
                                    wpb[pr][:, fc * 128:(fc + 1) * 128],
                                    c_idb[:, :])
                            return last
                        P.emit("pe", trw, wait=w2, inc="c_pe")
                        pn = P.n["c_pe"]
                        stg_b = wts[g8 % 2]
                        P.emit("act", lambda e, stg_b=stg_b, pb=pb:
                               e.activation(stg_b[:, :], pb[:, :],
                                            AF.Identity, bias=0.0,
                                            scale=1.0),
                               wait=[("c_pe", pn),
                                     ("d_scr", wt_dma[g8 % 2])],
                               inc="c_act")
                        vn = P.n["c_act"]
                        ev_hist[g8 % 2] = P.n["c_act"]
                        P.dma("syn",
                              sWdT[:, r, g8 * 8:(g8 + 1) * 8, :],
                              wts[g8 % 2][:, :].rearrange(
                                  "p (fc o) -> p fc o", fc=8),
                              "d_scr", wait=[("c_act", vn)])
                        wt_dma[g8 % 2] = P.n["d_scr"]
                    trw_p[pr] = P.n["c_pe"]
                P.run_block()

            with ExitStack() as p4a:
                pacc = mkpsum(p4a, "pacc4a", 6)
                ptw = mkpsum(p4a, "ptw", 2, [128, 1024], BF16)
                u1b = sb(p4a, "u1b", [128, NF * TOK], BF16)
                hsu2 = sb(p4a, "hsu2", [128, NI * 128], F32)
                wuf = [sb(p4a, f"wuf{i}", [128, H], F32) for i in range(2)]
                wgf = [sb(p4a, f"wgf{i}", [128, H], F32) for i in range(2)]
                wub = [sb(p4a, f"wub{i}", [128, H], BF16) for i in range(2)]
                wgb = [sb(p4a, f"wgb{i}", [128, H], BF16) for i in range(2)]
                wuT = [sb(p4a, f"wuT{i}", [128, NF * 128], BF16)
                       for i in range(2)]
                wgT = [sb(p4a, f"wgT{i}", [128, NF * 128], BF16)
                       for i in range(2)]
                sgb = [sb(p4a, f"sgb{i}", [128, 512], F32) for i in range(2)]
                x2s = [sb(p4a, f"x2s{i}", [128, 512], F32) for i in range(2)]
                x2a_s = [sb(p4a, f"x2as{i}", [128, 512], BF16)
                         for i in range(2)]
                P.dma("syn", u1b[:, :].rearrange("p (c t) -> p c t", c=NF),
                      sU1.rearrange("c p t -> p c t"), "d_scr")
                u1_rdy = P.n["d_scr"]
                for c in range(NI):
                    P.emit("dve", lambda e, c=c: e.tensor_scalar(
                        hsu2[:, c * 128:(c + 1) * 128], c_h128[:, :],
                        c_su2[:, c:c + 1], 2.0 ** -6,
                        op0=ALU.mult, op1=ALU.mult), inc="c_dve")
                hsu_done = P.n["c_dve"]
                dw_n = [0, 0]
                cast_hist = [0, 0]
                trw_hist = [0, 0]
                wT_done = [0, 0]
                tw_ev = [0, 0]
                mm_hist = [0, 0]
                silu_hist = [0, 0]
                stt_hist = [0, 0]
                sa_hist = [0, 0]
                ev_hist = [0, 0]
                x2a_dma = [0, 0]
                tw_idx = 0

                tw_state = {"idx": 0}

                def load_w(oc):
                    b = oc % 2
                    P.dma("syn", wuf[b][:, :],
                          wup[oc * 128:(oc + 1) * 128, :], "d_w",
                          wait=[("c_dve", cast_hist[b])])
                    P.dma("syn", wgf[b][:, :],
                          wgate[oc * 128:(oc + 1) * 128, :], "d_w")
                    dw_n[b] = P.n["d_w"]

                def prep_w(oc):
                    b = oc % 2
                    P.emit("dve", lambda e, b=b: e.tensor_scalar(
                        wub[b][:, :], wuf[b][:, :], 2.0 ** -6, None,
                        op0=ALU.mult),
                        wait=[("d_w", dw_n[b]), ("c_pe", trw_hist[b])],
                        inc="c_dve")
                    P.emit("dve", lambda e, b=b: e.tensor_scalar(
                        wgb[b][:, :], wgf[b][:, :], 2.0 ** -6, None,
                        op0=ALU.mult), inc="c_dve")
                    cast_hist[b] = P.n["c_dve"]
                    cn = P.n["c_dve"]
                    for (srcb, dstT) in ((wub, wuT), (wgb, wgT)):
                        for g in range(2):
                            ti = tw_state["idx"]
                            bank = ptw[ti % 2]
                            w2 = [("c_dve", cn),
                                  ("c_dve", tw_ev[ti % 2])]

                            def trg(e, srcb=srcb, b=b, g=g, bank=bank):
                                for cc in range(8):
                                    fc = g * 8 + cc
                                    last = e.transpose(
                                        bank[:, cc * 128:(cc + 1) * 128],
                                        srcb[b][:, fc * 128:(fc + 1) * 128],
                                        c_idb[:, :])
                                return last
                            P.emit("pe", trg, wait=w2, inc="c_pe")
                            pn = P.n["c_pe"]
                            P.emit("dve",
                                   lambda e, dstT=dstT, b=b, g=g, bank=bank:
                                   e.tensor_copy(
                                       dstT[b][:, g * 1024:(g + 1) * 1024],
                                       bank[:, :]),
                                   wait=[("c_pe", max(pn, mm_hist[b]))],
                                   inc="c_dve")
                            tw_ev[ti % 2] = P.n["c_dve"]
                            tw_state["idx"] = ti + 1
                    trw_hist[b] = P.n["c_pe"]
                    wT_done[b] = P.n["c_dve"]

                def emit_tail(k):
                    oc, ts = divmod(k, 4)
                    tp_ = ts % 2
                    pa_, pb_, pc_ = pacc[tp_], pacc[2 + tp_], pacc[4 + tp_]
                    mm_n = mm_of[k]
                    sg = sgb[tp_]
                    P.emit("act", lambda e, oc=oc, sg=sg, pb_=pb_:
                           e.activation(
                               sg[:, :], pb_[:, :], AF.Silu,
                               scale=c_vuvg[:, 64 + oc:65 + oc]),
                           wait=[("c_pe", mm_n),
                                 ("c_dve", stt_hist[tp_])],
                           inc="c_act")
                    silu_hist[tp_] = P.n["c_act"]
                    xs = x2s[tp_]
                    P.emit("dve", lambda e, oc=oc, xs=xs, sg=sg,
                           pa_=pa_: e.scalar_tensor_tensor(
                               xs[:, :], pa_[:, :],
                               c_vuvg[:, oc:oc + 1], sg[:, :],
                               op0=ALU.mult, op1=ALU.mult),
                           wait=[("c_act", silu_hist[tp_]),
                                 ("c_pe", sa_hist[tp_])],
                           inc="c_dve")
                    stt_hist[tp_] = P.n["c_dve"]
                    P.emit("pe", lambda e, oc=oc, xs=xs, pc_=pc_:
                           e.matmul(
                               pc_[:, :],
                               hsu2[:, oc * 128:(oc + 1) * 128],
                               xs[:, :], start=True, stop=True),
                           wait=[("c_dve", max(stt_hist[tp_],
                                               ev_hist[tp_]))],
                           inc="c_pe")
                    sa_hist[tp_] = P.n["c_pe"]
                    sa_n = P.n["c_pe"]
                    xa = x2a_s[tp_]
                    P.emit("dve", lambda e, xa=xa, pc_=pc_:
                           e.tensor_copy(xa[:, :], pc_[:, :]),
                           wait=[("c_pe", sa_n),
                                 ("d_scr", x2a_dma[tp_])],
                           inc="c_dve")
                    ev_hist[tp_] = P.n["c_dve"]
                    vn = P.n["c_dve"]
                    P.dma("syn",
                          sX2b[oc * 128:(oc + 1) * 128,
                               ts * 512:(ts + 1) * 512],
                          xa[:, :], "d_scr", wait=[("c_dve", vn)])
                    x2a_dma[tp_] = P.n["d_scr"]

                mm_of = {}
                load_w(0)
                load_w(1)
                prep_w(0)
                for k in range(NI * 4):
                    oc, ts = divmod(k, 4)
                    b = oc % 2
                    tp_ = ts % 2
                    pa_, pb_ = pacc[tp_], pacc[2 + tp_]
                    w = [("c_dve", max(wT_done[b], stt_hist[tp_],
                                       hsu_done if oc == 0 else 0)),
                         ("c_act", silu_hist[tp_])]
                    if k == 0:
                        w.append(("d_scr", u1_rdy))

                    def mmu(e, b=b, ts=ts, pa_=pa_, pb_=pb_):
                        for fc in range(NF):
                            e.matmul(pa_[:, :],
                                     wuT[b][:, fc * 128:(fc + 1) * 128],
                                     u1b[:, fc * TOK + ts * 512:
                                         fc * TOK + ts * 512 + 512],
                                     start=(fc == 0),
                                     stop=(fc == NF - 1))
                        for fc in range(NF):
                            last = e.matmul(
                                pb_[:, :],
                                wgT[b][:, fc * 128:(fc + 1) * 128],
                                u1b[:, fc * TOK + ts * 512:
                                    fc * TOK + ts * 512 + 512],
                                start=(fc == 0), stop=(fc == NF - 1))
                        return last
                    P.emit("pe", mmu, wait=w, inc="c_pe")
                    mm_hist[b] = P.n["c_pe"]
                    mm_of[k] = P.n["c_pe"]
                    if k > 0:
                        emit_tail(k - 1)
                    if ts == 1 and oc + 1 < NI:
                        prep_w(oc + 1)
                    if ts == 2 and oc + 2 < NI:
                        load_w(oc + 2)
                emit_tail(NI * 4 - 1)
                P.run_block()

            with ExitStack() as p4c:
                pacc = mkpsum(p4c, "pacc4c", 4)
                x2bb = [sb(p4c, f"x2bb{i}", [128, NI * 512], BF16)
                        for i in range(2)]
                wdts = [sb(p4c, f"wdtt{i}", [128, NI * 128], BF16)
                        for i in range(2)]
                x3s = [sb(p4c, f"x3s{i}", [128, 512], F32) for i in range(2)]
                gst = [sb(p4c, f"gst{i}", [128, 512], F32) for i in range(2)]
                x2b_r = sX2b.rearrange("(fc p) (ts t) -> p fc ts t",
                                       p=128, t=512)
                g_dma = [0, 0]
                wdt_pe = [0, 0]
                mm_hist = [0, 0]
                ev_hist = [0, 0]
                tr_hist = [0, 0]
                silu_hist = [0, 0]
                xr = [0, 0]

                def load_x2b(ts):
                    P.dma("syn",
                          x2bb[ts % 2][:, :].rearrange(
                              "p (c t) -> p c t", c=NI),
                          x2b_r[:, :, ts, :], "d_scr",
                          wait=[("c_pe", mm_hist[ts % 2])])
                    xr[ts % 2] = P.n["d_scr"]

                mm_of = {}

                def emit_tail_c(k):
                    ts, oc = divmod(k, 16)
                    pa_ = pacc[oc % 2]
                    x3 = x3s[oc % 2]
                    P.emit("dve", lambda e, oc=oc, x3=x3, pa_=pa_:
                           e.tensor_scalar(
                               x3[:, :], pa_[:, :],
                               c_vuvg[:, 128 + oc:129 + oc], None,
                               op0=ALU.mult),
                           wait=[("c_pe", max(mm_of[k],
                                              tr_hist[oc % 2]))],
                           inc="c_dve")
                    ev_hist[oc % 2] = P.n["c_dve"]
                    ev_n = P.n["c_dve"]
                    ptr_ = pacc[2 + oc % 2]

                    def trg(e, x3=x3, ptr_=ptr_):
                        for i4 in range(4):
                            last = e.transpose(
                                ptr_[:, i4 * 128:(i4 + 1) * 128],
                                x3[:, i4 * 128:(i4 + 1) * 128],
                                c_idf[:, :])
                        return last
                    P.emit("pe", trg,
                           wait=[("c_dve", ev_n),
                                 ("c_act", silu_hist[oc % 2])],
                           inc="c_pe")
                    tr_hist[oc % 2] = P.n["c_pe"]
                    tn = P.n["c_pe"]
                    gb = gst[oc % 2]
                    P.emit("act", lambda e, gb=gb, ptr_=ptr_:
                           e.activation(gb[:, :], ptr_[:, :], AF.Silu),
                           wait=[("c_pe", tn), ("d_scr", g_dma[oc % 2])],
                           inc="c_act")
                    silu_hist[oc % 2] = P.n["c_act"]
                    an = P.n["c_act"]
                    P.dma("syn",
                          sG.rearrange("(ts i p) o -> p ts i o",
                                       i=4, p=128)[
                              :, ts, :, oc * 128:(oc + 1) * 128],
                          gb[:, :].rearrange("p (i o) -> p i o", i=4),
                          "d_scr", wait=[("c_act", an)])
                    g_dma[oc % 2] = P.n["d_scr"]

                load_x2b(0)
                for k in range(64):
                    ts, oc = divmod(k, 16)
                    tb = ts % 2
                    b = oc % 2
                    if oc == 0 and ts + 1 < 4:
                        load_x2b(ts + 1)
                    P.dma("syn",
                          wdts[b][:, :].rearrange("p (c o) -> p c o",
                                                  c=NI),
                          sWdT[:, oc, :, :], "d_scr",
                          wait=[("c_pe", wdt_pe[b])])
                    dwn = P.n["d_scr"]
                    pa_ = pacc[oc % 2]

                    def mmd(e, b=b, tb=tb, pa_=pa_):
                        for fc in range(NI):
                            last = e.matmul(
                                pa_[:, :],
                                wdts[b][:, fc * 128:(fc + 1) * 128],
                                x2bb[tb][:, fc * 512:(fc + 1) * 512],
                                start=(fc == 0), stop=(fc == NI - 1))
                        return last
                    P.emit("pe", mmd,
                           wait=[("d_scr", max(dwn, xr[tb])),
                                 ("c_dve", ev_hist[oc % 2])],
                           inc="c_pe")
                    wdt_pe[b] = P.n["c_pe"]
                    mm_hist[tb] = P.n["c_pe"]
                    mm_of[k] = P.n["c_pe"]
                    if k > 0:
                        emit_tail_c(k - 1)
                emit_tail_c(63)
                P.run_block()

            with ExitStack() as p4d:
                grow = [sb(p4d, f"grow{i}", [128, H], F32) for i in range(2)]
                srow = [sb(p4d, f"srow{i}", [128, H], F32) for i in range(2)]
                xrow = [sb(p4d, f"xrow{i}", [128, H], F32) for i in range(2)]
                orow = [sb(p4d, f"orow{i}", [128, H], F32) for i in range(2)]
                comb_n = [0, 0]
                out_n = [0, 0]
                pool_n4 = [0, 0]
                for tt in range(16):
                    b = tt % 2
                    w = ([("c_dve", comb_n[b]),
                          ("c_pool", pool_n4[b])] if tt >= 2 else None)
                    P.dma("syn", grow[b][:, :],
                          sG[tt * 128:(tt + 1) * 128, :], "d_scr", wait=w)
                    P.dma("syn", srow[b][:, :],
                          sS32[tt * 128:(tt + 1) * 128, :], "d_scr")
                    P.dma("syn", xrow[b][:, :],
                          x_own[tt * 128:(tt + 1) * 128, :], "d_in")
                    dr, di = P.n["d_scr"], P.n["d_in"]

                    P.emit("dve", lambda e, b=b: e.tensor_tensor(
                        grow[b][:, :], grow[b][:, :], srow[b][:, :],
                        op=ALU.mult),
                        wait=[("d_scr", dr), ("d_in", di),
                              ("d_out", out_n[b])], inc="c_dve")
                    mul_n = P.n["c_dve"]
                    P.emit("pool", lambda e, b=b: e.tensor_tensor(
                        orow[b][:, :], grow[b][:, :], xrow[b][:, :],
                        op=ALU.add),
                        wait=[("c_dve", mul_n)], inc="c_pool")
                    comb_n[b] = P.n["c_dve"]
                    pool_n4[b] = P.n["c_pool"]
                    vn = P.n["c_pool"]
                    P.dma("syn", out_own[tt * 128:(tt + 1) * 128, :],
                          orow[b][:, :], "d_out", wait=[("c_pool", vn)])
                    out_n[b] = P.n["d_out"]
                if tap is not None:
                    tsrc = scr_map[tap]
                    shp = list(tsrc.tensor.shape)
                    tout = nc.dram_tensor(
                        "tapout", shp, tsrc.dtype,
                        kind="ExternalOutput").ap()
                    nch = max(1, shp[0] // 2048)
                    step = shp[0] // nch
                    for ci in range(nch):
                        P.dma("syn", tout[ci * step:(ci + 1) * step],
                              tsrc[ci * step:(ci + 1) * step], "d_out")
                P.run_block()
    return nc


def make_in_maps(x, w_qkv, b_qkv, w_proj, b_proj, su_ug, su_dp, ws_ug, ws_dp,
                 w_up, w_gate, w_down):
    h128 = _hadamard(128).astype(np.float32)
    identf = np.eye(128, dtype=np.float32)

    shared = {
        "w_qkv": np.ascontiguousarray(w_qkv, np.float32),
        "w_proj": np.ascontiguousarray(w_proj, np.float32),
        "wup": np.ascontiguousarray(w_up, np.float32),
        "wgate": np.ascontiguousarray(w_gate, np.float32),
        "wdown": np.ascontiguousarray(w_down, np.float32),
        "bqkv_cols": np.ascontiguousarray(
            np.asarray(b_qkv, np.float32).reshape(48, 128).T),
        "bvrep": np.ascontiguousarray(np.broadcast_to(
            np.asarray(b_qkv, np.float32)[4096:6144], (128, 2048))),
        "bproj_cols": np.ascontiguousarray(
            np.asarray(b_proj, np.float32).reshape(16, 128).T),
        "su1_cols": np.ascontiguousarray(
            np.asarray(su_ug, np.float32).reshape(16, 128).T),
        "su2_cols": np.ascontiguousarray(
            np.asarray(su_dp, np.float32).reshape(64, 128).T),
        "ws_cols": np.ascontiguousarray(np.concatenate([
            np.asarray(ws_ug, np.float32)[:8192].reshape(64, 128).T,
            np.asarray(ws_ug, np.float32)[8192:].reshape(64, 128).T,
            np.asarray(ws_dp, np.float32).reshape(16, 128).T], axis=1)),
        "h128": h128,
        "ident_f": identf,
        "ident_b": identf.astype(ml_dtypes.bfloat16),
    }
    in_maps = []
    for c in range(8):
        bi, pa = c // 2, c % 2
        qgs = [1024 * l + 512 * pa for l in range(4)]
        x_own = np.ascontiguousarray(
            np.concatenate([x[bi, qg:qg + 512] for qg in qgs], axis=0))
        x_kv = np.ascontiguousarray(np.roll(x[bi], -512 * pa, axis=0))
        expb = np.zeros((NH, NSLOT_RECT), np.float32)
        for hh in range(NH):
            s = 0
            for l in range(4):
                for kt in rect_kts(l):
                    if kt < 28:
                        expb[hh, s] = -(SLOPES[hh] * 1024 * l + MCONST)
                    elif pa == 1:
                        expb[hh, s] = -(SLOPES[hh] * (4096 + 1024 * l)
                                        + MCONST)
                    else:
                        expb[hh, s] = NEGBIG
                    s += 1
        m = dict(shared)
        m["x_own"] = x_own
        m["x_kv"] = x_kv
        m["expb"] = np.ascontiguousarray(np.broadcast_to(
            expb.reshape(1, -1), (128, NH * NSLOT_RECT)))
        in_maps.append(m)
    return in_maps


def kernel(L0_attn_hidden_states, L0_attn_w_qkv, L0_attn_b_qkv,
           L0_attn_w_proj, L0_attn_b_proj, L0_ffn_SU_ug, L0_ffn_SU_dp,
           L0_ffn_Wscale_ug, L0_ffn_Wscale_dp, L0_ffn_W_up, L0_ffn_W_gate,
           L0_ffn_W_down):
    x = np.ascontiguousarray(L0_attn_hidden_states, dtype=np.float32)
    in_maps = make_in_maps(x, L0_attn_w_qkv, L0_attn_b_qkv, L0_attn_w_proj,
                           L0_attn_b_proj, L0_ffn_SU_ug, L0_ffn_SU_dp,
                           L0_ffn_Wscale_ug, L0_ffn_Wscale_dp, L0_ffn_W_up,
                           L0_ffn_W_gate, L0_ffn_W_down)
    nc = build()
    res = run_bass_kernel_spmd(nc, in_maps, core_ids=list(range(8)))
    out = np.empty((B, S, H), np.float32)
    for c in range(8):
        bi, pa = c // 2, c % 2
        shard = res.results[c]["out_own"]
        for l in range(4):
            qg = 1024 * l + 512 * pa
            out[bi, qg:qg + 512] = shard[l * 512:(l + 1) * 512]
    return out


# version 15
# speedup vs baseline: 1.2265x; 1.0200x over previous
import math
from contextlib import ExitStack

import numpy as np
import ml_dtypes

import concourse.bass as bass
import concourse.mybir as mybir
from concourse.bass_utils import run_bass_kernel_spmd

F32 = mybir.dt.float32
BF16 = mybir.dt.bfloat16
AF = mybir.ActivationFunctionType
ALU = mybir.AluOpType
AX = mybir.AxisListType

B, S, H, NH, HD, INTER = 4, 4096, 2048, 16, 128, 8192
NF = H // 128
NI = INTER // 128
TOK = 2048
NEGBIG = -1.0e30
MCONST = 32.0
SQ2INV64 = 45.254834
SLOPES = [2.0 ** (-(i + 1) / 2.0) for i in range(NH)]
EPS = 1e-5
KQOFF = 31 * 128


def rect_kts(l):
    return list(range(8 * l)) + [28, 29, 30, 31]


RECT_BASE = [0, 4, 16, 36]
NSLOT_RECT = 64


def _hadamard(n):
    h = np.array([[1.0]])
    while h.shape[0] < n:
        h = np.block([[h, h], [h, -h]])
    return h


class Prog:
    def __init__(self, nc, es):
        self.nc = nc
        self.sems = {}
        self.n = {}
        for name in ("c_pe", "c_dve", "c_act", "c_pool",
                     "d_in", "d_w", "d_scr", "d_out"):
            self.sems[name] = es.enter_context(nc.semaphore(name))
            self.n[name] = 0
        self.new_walk()

    def new_walk(self):
        self.ops = {e: [] for e in ("pe", "dve", "act", "pool", "syn")}

    def emit(self, eng, fn, inc=None, inck=1, wait=None):
        if wait:
            for (sname, val) in ([wait] if isinstance(wait, tuple) else wait):
                if val > 0:
                    sem = self.sems[sname]
                    self.ops[eng].append(
                        lambda e, s=sem, v=val: e.wait_ge(s, v))
        if inc is not None:
            sem = self.sems[inc]
            self.n[inc] += inck

            def run(e, f=fn, s=sem, k=inck):
                f(e).then_inc(s, k)
            self.ops[eng].append(run)
        else:
            self.ops[eng].append(fn)

    def dma(self, eng, out_ap, in_ap, sem="d_in", wait=None):
        self.emit(eng, lambda e, o=out_ap, i=in_ap: e.dma_start(o, i),
                  inc=sem, inck=16, wait=wait)

    def run_block(self):
        for sname in ("d_in", "d_w", "d_scr", "d_out"):
            if self.n[sname] > 0:
                sem, v = self.sems[sname], self.n[sname]
                self.ops["syn"].append(lambda e, s=sem, vv=v: e.wait_ge(s, vv))
        ops = self.ops
        with self.nc.Block() as blk:
            if ops["syn"]:
                @blk.sync
                def _(e, lst=list(ops["syn"])):
                    for f in lst:
                        f(e)
            if ops["pe"]:
                @blk.tensor
                def _(e, lst=list(ops["pe"])):
                    for f in lst:
                        f(e)
            if ops["dve"]:
                @blk.vector
                def _(e, lst=list(ops["dve"])):
                    for f in lst:
                        f(e)
            if ops["act"]:
                @blk.scalar
                def _(e, lst=list(ops["act"])):
                    for f in lst:
                        f(e)
            if ops["pool"]:
                @blk.gpsimd
                def _(e, lst=list(ops["pool"])):
                    for f in lst:
                        f(e)
        self.new_walk()


def build(tap=None):
    nc = bass.Bass("TRN2", target_bir_lowering=False)

    def din(name, shape, dt=F32):
        return nc.dram_tensor(name, shape, dt, kind="ExternalInput").ap()

    x_own = din("x_own", [TOK, H])
    x_kv = din("x_kv", [S, H])
    w_qkv = din("w_qkv", [H, 3 * H])
    w_proj = din("w_proj", [H, H])
    wup = din("wup", [INTER, H])
    wgate = din("wgate", [INTER, H])
    wdown = din("wdown", [H, INTER])
    bqkv_cols = din("bqkv_cols", [128, 48])
    bvrep_in = din("bvrep", [128, H])
    bproj_cols = din("bproj_cols", [128, 16])
    su1_cols = din("su1_cols", [128, NF])
    su2_cols = din("su2_cols", [128, NI])
    ws_cols = din("ws_cols", [128, 144])
    h128_in = din("h128", [128, 128])
    ident_b = din("ident_b", [128, 128], BF16)
    ident_f = din("ident_f", [128, 128])
    expb_in = din("expb", [128, NH * NSLOT_RECT])

    out_own = nc.dram_tensor("out_own", [TOK, H], F32,
                             kind="ExternalOutput").ap()

    def mkscr(name, shape, dt):
        return nc.dram_tensor(name, shape, dt, kind="Internal").ap()

    sK = mkscr("sK", [NH, 128, S], BF16)
    sV = mkscr("sV", [NH, S, HD], BF16)
    sKo = mkscr("sKo", [NH, 128, TOK], BF16)
    sVo = mkscr("sVo", [NH, TOK, HD], BF16)
    sQ = mkscr("sQ", [NH, 128, TOK], BF16)
    sU1 = mkscr("sU1", [NF, 128, TOK], BF16)
    sSp = mkscr("sSp", [NH, 128, TOK], BF16)
    sS32 = mkscr("sS32", [TOK, H], F32)
    sX2b = mkscr("sX2b", [INTER, TOK], BF16)
    sWdT = mkscr("sWdT", [128, 16, NI, 128], BF16)
    sG = mkscr("sG", [TOK, H], F32)
    sXN = mkscr("sXN", [NF, 128, 2048], BF16)
    scr_map = {"sXN": sXN,"sK": sK, "sV": sV, "sKo": sKo, "sVo": sVo, "sQ": sQ,
               "sU1": sU1, "sSp": sSp, "sS32": sS32,
               "sX2b": sX2b, "sG": sG}

    with ExitStack() as es:
        P = Prog(nc, es)

        def sb(stk, name, shape, dt):
            return stk.enter_context(nc.sbuf_tensor(name, shape, dt))

        c_bqkv = sb(es, "c_bqkv", [128, 48], F32)
        c_bproj = sb(es, "c_bproj", [128, 16], F32)
        c_su2 = sb(es, "c_su2", [128, NI], F32)
        c_vuvg = sb(es, "c_vuvg", [128, 144], F32)
        c_idb = sb(es, "c_idb", [128, 128], BF16)
        c_idf = sb(es, "c_idf", [128, 128], F32)
        c_h128 = sb(es, "c_h128", [128, 128], F32)
        ones_b = sb(es, "ones_b", [128, 1], BF16)
        ones_f = sb(es, "ones_f", [1, 128], F32)
        def mkpsum(stk, pref, n, shape=None, dt=F32):
            return [stk.enter_context(nc.psum_tensor(
                f"{pref}{i}", shape or [128, 512], dt)) for i in range(n)]

        with ExitStack() as ph:
            csu1 = sb(ph, "csu1", [128, NF], F32)
            P.dma("syn", c_bqkv[:, :], bqkv_cols[:, :])
            P.dma("syn", c_bproj[:, :], bproj_cols[:, :])
            P.dma("syn", csu1[:, :], su1_cols[:, :])
            P.dma("syn", c_su2[:, :], su2_cols[:, :])
            P.dma("syn", c_vuvg[:, :], ws_cols[:, :])
            P.dma("syn", c_idb[:, :], ident_b[:, :])
            P.dma("syn", c_idf[:, :], ident_f[:, :])
            P.dma("syn", c_h128[:, :], h128_in[:, :])
            P.emit("pool", lambda e: e.memset(ones_b[:, :], 1.0))
            P.emit("pool", lambda e: e.memset(ones_f[:, :], 1.0))
            d0 = P.n["d_in"]
            P.emit("dve", lambda e: e.tensor_scalar(
                c_vuvg[:, :], c_vuvg[:, :], SQ2INV64, None, op0=ALU.mult),
                wait=("d_in", d0), inc="c_dve")
            hsu1_holder = {}
            P.run_block()

            with ExitStack() as p1:
                pacc = mkpsum(p1, "pacc", 4)
                ptr_b = mkpsum(p1, "ptrb", 2, [128, 1024], BF16)
                TW = 2048
                xn_buf = sb(p1, "xn_buf", [128, NF * TW], BF16)
                xnt = sb(p1, "xnt", [128, H], BF16)
                ln_sm = sb(p1, "ln_sm", [128, 228], F32)
                xst = [sb(p1, f"xst{i}", [128, H], F32) for i in range(2)]
                c_bvrep = sb(p1, "c_bvrep", [128, H], F32)
                hsu1 = sb(p1, "hsu1", [128, NF * 128], BF16)
                wf_a = [sb(p1, f"wf_a{i}", [128, NF * 128], F32)
                        for i in range(2)]
                scrap = wf_a[0]
                wb_a = [sb(p1, f"wba{i}", [128, NF * 128], BF16)
                        for i in range(2)]
                wf_v = sb(p1, "wf_v", [128, NF * 256], F32)
                wb_v = [sb(p1, f"wbv{i}", [128, NF * 512], BF16)
                        for i in range(2)]
                kst = [sb(p1, f"kst{i}", [128, 512], BF16) for i in range(2)]
                vst = [sb(p1, f"vst{i}", [128, 512], BF16) for i in range(2)]
                u1t = [sb(p1, f"u1t{i}", [128, NF * 128], F32)
                       for i in range(2)]
                u1s = [sb(p1, f"u1s{i}", [128, 128], BF16) for i in range(2)]

                P.dma("syn", c_bvrep[:, :], bvrep_in[:, :])
                for c in range(NF):
                    P.emit("dve", lambda e, c=c: e.tensor_scalar(
                        hsu1[:, c * 128:(c + 1) * 128], c_h128[:, :],
                        csu1[:, c:c + 1], 2.0 ** -5,
                        op0=ALU.mult, op1=ALU.mult), inc="c_dve")

                P.emit("pool", lambda e: e.memset(ln_sm[:, 224:225], EPS),
                       inc="c_pool")
                xn3 = xn_buf[:, :].rearrange("p (c t) -> p c t", c=NF)
                st = {"norm": [0, 0], "ev": [0, 0], "tr": 0, "sums": [0, 0]}

                def ln_pass(src_ap, n_tiles):
                    for i in range(n_tiles):
                        xb = xst[i % 2]
                        P.dma("syn", xb[:, :],
                              src_ap[i * 128:(i + 1) * 128, :], "d_in",
                              wait=[("c_dve", st["sums"][i % 2])])
                        dn = P.n["d_in"]
                        P.emit("dve", lambda e, xb=xb, i=i: e.reduce_sum(
                            ln_sm[:, i:i + 1], xb[:, :], axis=AX.X,
                            negate=True),
                            wait=("d_in", dn), inc="c_dve")
                        P.emit("dve", lambda e, xb=xb, i=i:
                               e.scalar_tensor_tensor(
                                   scrap[:, :], xb[:, :], 1.0, xb[:, :],
                                   op0=ALU.mult, op1=ALU.mult,
                                   accum_out=ln_sm[:, 16 + i:17 + i]),
                               inc="c_dve")
                        st["sums"][i % 2] = P.n["c_dve"]
                    NM, S1, M2, A_, LN_, LNC, R0, T3, T4, T5, RS, NMA = (
                        32, 48, 64, 80, 96, 112, 128, 144, 160, 176, 192, 208)
                    nt = n_tiles
                    vd = P.n["c_dve"]
                    P.emit("act", lambda e: e.activation(
                        ln_sm[:, NM:NM + nt], ln_sm[:, 0:nt], AF.Identity,
                        bias=0.0, scale=1.0 / H),
                        wait=("c_dve", vd), inc="c_act")
                    P.emit("act", lambda e: e.activation(
                        ln_sm[:, S1:S1 + nt], ln_sm[:, 16:16 + nt],
                        AF.Identity, bias=ln_sm[:, 224:225],
                        scale=1.0 / H), inc="c_act")
                    va = P.n["c_act"]
                    P.emit("dve", lambda e: e.tensor_tensor(
                        ln_sm[:, M2:M2 + nt], ln_sm[:, NM:NM + nt],
                        ln_sm[:, NM:NM + nt], op=ALU.mult),
                        wait=("c_act", va), inc="c_dve")
                    vd = P.n["c_dve"]
                    P.emit("pool", lambda e: e.tensor_tensor(
                        ln_sm[:, A_:A_ + nt], ln_sm[:, S1:S1 + nt],
                        ln_sm[:, M2:M2 + nt], op=ALU.subtract),
                        wait=[("c_act", va), ("c_dve", vd)], inc="c_pool")
                    vp = P.n["c_pool"]
                    P.emit("act", lambda e: e.activation(
                        ln_sm[:, LN_:LN_ + nt], ln_sm[:, A_:A_ + nt], AF.Ln),
                        wait=("c_pool", vp), inc="c_act")
                    va = P.n["c_act"]
                    P.emit("pool", lambda e: e.tensor_copy(
                        ln_sm[:, LNC:LNC + nt], ln_sm[:, LN_:LN_ + nt]),
                        wait=("c_act", va), inc="c_pool")
                    vp = P.n["c_pool"]
                    P.emit("act", lambda e: e.activation(
                        ln_sm[:, R0:R0 + nt], ln_sm[:, LNC:LNC + nt],
                        AF.Exp, scale=-0.5),
                        wait=("c_pool", vp), inc="c_act")
                    va = P.n["c_act"]
                    P.emit("dve", lambda e: e.tensor_tensor(
                        ln_sm[:, T3:T3 + nt], ln_sm[:, R0:R0 + nt],
                        ln_sm[:, R0:R0 + nt], op=ALU.mult),
                        wait=("c_act", va), inc="c_dve")
                    vd = P.n["c_dve"]
                    P.emit("pool", lambda e: e.tensor_tensor(
                        ln_sm[:, T4:T4 + nt], ln_sm[:, T3:T3 + nt],
                        ln_sm[:, A_:A_ + nt], op=ALU.mult),
                        wait=("c_dve", vd), inc="c_pool")
                    vp = P.n["c_pool"]
                    P.emit("dve", lambda e: e.tensor_scalar(
                        ln_sm[:, T5:T5 + nt], ln_sm[:, T4:T4 + nt], -0.5,
                        1.5, op0=ALU.mult, op1=ALU.add),
                        wait=("c_pool", vp), inc="c_dve")
                    vd = P.n["c_dve"]
                    P.emit("pool", lambda e: e.tensor_tensor(
                        ln_sm[:, RS:RS + nt], ln_sm[:, T5:T5 + nt],
                        ln_sm[:, R0:R0 + nt], op=ALU.mult),
                        wait=("c_dve", vd), inc="c_pool")
                    vp = P.n["c_pool"]
                    P.emit("dve", lambda e: e.tensor_tensor(
                        ln_sm[:, NMA:NMA + nt], ln_sm[:, NM:NM + nt],
                        ln_sm[:, RS:RS + nt], op=ALU.mult),
                        wait=("c_pool", vp), inc="c_dve")
                    chain_d = P.n["c_dve"]
                    chain_p = P.n["c_pool"]
                    sums_done = chain_d
                    for i in range(n_tiles):
                        xb = xst[i % 2]
                        P.dma("syn", xb[:, :],
                              src_ap[i * 128:(i + 1) * 128, :], "d_in",
                              wait=[("c_act", st["norm"][i % 2]),
                                    ("c_dve", sums_done)])
                        dn = P.n["d_in"]
                        P.emit("act", lambda e, xb=xb, i=i: e.activation(
                            xnt[:, :], xb[:, :], AF.Identity,
                            bias=ln_sm[:, NMA + i:NMA + i + 1],
                            scale=ln_sm[:, RS + i:RS + i + 1]),
                            wait=[("d_in", dn), ("c_dve", chain_d),
                                  ("c_pool", chain_p), ("c_pe", st["tr"])],
                            inc="c_act")
                        st["norm"][i % 2] = P.n["c_act"]
                        a3 = P.n["c_act"]
                        for hf in range(2):
                            pb = ptr_b[hf]

                            def trs(e, hf=hf, pb=pb):
                                for cc in range(8):
                                    c = hf * 8 + cc
                                    last = e.transpose(
                                        pb[:, cc * 128:(cc + 1) * 128],
                                        xnt[:, c * 128:(c + 1) * 128],
                                        c_idb[:, :])
                                return last
                            P.emit("pe", trs,
                                   wait=[("c_act", a3),
                                         ("c_dve", st["ev"][hf])],
                                   inc="c_pe")
                            pn = P.n["c_pe"]
                            P.emit("dve",
                                   lambda e, hf=hf, pb=pb, i=i:
                                   e.tensor_copy(
                                       xn3[:, hf * 8:(hf + 1) * 8,
                                           i * 128:(i + 1) * 128],
                                       pb[:, :].rearrange(
                                           "p (c t) -> p c t", c=8)),
                                   wait=("c_pe", pn), inc="c_dve")
                            st["ev"][hf] = P.n["c_dve"]
                        st["tr"] = P.n["c_pe"]

                w_qkv3 = w_qkv.rearrange("(c p) o -> p c o", p=128)
                pool_n = {"a": 0, "v0": 0, "v1": 0}
                wb_pe = {}
                scr_n = {}
                OWN = {0: 0, 1024: 1, 2048: 2, 3072: 3}

                def kq_pass(col_off, bias_off, qscale, dst, own_dst, half):
                    for oc in range(NH):
                        b = oc % 2
                        P.dma("syn",
                              wf_a[b][:, :].rearrange("p (c o) -> p c o",
                                                      c=NF),
                              w_qkv3[:, :, col_off + oc * 128:
                                     col_off + (oc + 1) * 128],
                              "d_w", wait=[("c_pool", pool_n.get(
                                  ("a", b), 0))])
                        dw = P.n["d_w"]
                        P.emit("pool", lambda e, b=b: e.tensor_copy(
                            wb_a[b][:, :], wf_a[b][:, :]),
                            wait=[("d_w", dw), ("c_pe", wb_pe.get(b, 0))],
                            inc="c_pool")
                        pool_n[("a", b)] = P.n["c_pool"]
                        cn = P.n["c_pool"]
                        for ts in range(4):
                            pb = pacc[ts]
                            waits = [("c_pool", cn),
                                     ("c_dve", P.n["c_dve"] - 3)]

                            def mms(e, b=b, ts=ts, pb=pb):
                                for fc in range(NF):
                                    last = e.matmul(
                                        pb[:, :],
                                        wb_a[b][:, fc * 128:(fc + 1) * 128],
                                        xn_buf[:, fc * TW + ts * 512:
                                               fc * TW + ts * 512 + 512],
                                        start=(fc == 0), stop=(fc == NF - 1))
                                return last
                            P.emit("pe", mms, wait=waits, inc="c_pe")
                            pn = P.n["c_pe"]
                            wb_pe[b] = P.n["c_pe"]
                            kb = kst[ts % 2]
                            wv = [("c_pe", pn)]
                            if ("k", ts % 2) in scr_n:
                                wv.append(("d_scr", scr_n[("k", ts % 2)]))
                            if qscale is None:
                                P.emit("dve", lambda e, pb=pb, kb=kb, oc=oc:
                                       e.tensor_scalar(
                                           kb[:, :], pb[:, :],
                                           c_bqkv[:, bias_off + oc:
                                                  bias_off + oc + 1],
                                           None, op0=ALU.add),
                                       wait=wv, inc="c_dve")
                            else:
                                P.emit("dve", lambda e, pb=pb, kb=kb, oc=oc:
                                       e.tensor_scalar(
                                           kb[:, :], pb[:, :],
                                           c_bqkv[:, bias_off + oc:
                                                  bias_off + oc + 1],
                                           qscale, op0=ALU.add, op1=ALU.mult),
                                       wait=wv, inc="c_dve")
                            vn = P.n["c_dve"]
                            t0 = half * 2048 + ts * 512
                            P.dma("syn", dst[oc, :, t0:t0 + 512], kb[:, :],
                                  "d_scr", wait=[("c_dve", vn)])
                            if own_dst is not None and t0 in OWN:
                                l = OWN[t0]
                                P.dma("syn",
                                      own_dst[oc, :, l * 512:(l + 1) * 512],
                                      kb[:, :], "d_scr")
                            scr_n[("k", ts % 2)] = P.n["d_scr"]

                def v_pass(half):
                    for os_ in range(4):
                        b = os_ % 2
                        wbv3 = wb_v[b][:, :].rearrange(
                            "p (c o) -> p c o", c=NF)
                        for hv in range(2):
                            P.dma("syn",
                                  wf_v[:, :].rearrange(
                                      "p (c o) -> p c o", c=NF),
                                  w_qkv3[:, :, 2 * H + os_ * 512 + hv * 256:
                                         2 * H + os_ * 512 + hv * 256 + 256],
                                  "d_w",
                                  wait=[("c_pool", pool_n.get("vf", 0))])
                            dw = P.n["d_w"]
                            P.emit("pool", lambda e, b=b, hv=hv, wbv3=wbv3:
                                   e.tensor_copy(
                                       wbv3[:, :, hv * 256:(hv + 1) * 256],
                                       wf_v[:, :].rearrange(
                                           "p (c o) -> p c o", c=NF)),
                                   wait=[("d_w", dw),
                                         ("c_pe", wb_pe.get(("v", b), 0))],
                                   inc="c_pool")
                            pool_n["vf"] = P.n["c_pool"]
                        pool_n[f"v{b}"] = P.n["c_pool"]
                        cn = P.n["c_pool"]
                        for tsub in range(16):
                            pb = pacc[tsub % 4]
                            waits = [("c_pool", cn),
                                     ("c_dve", P.n["c_dve"] - 3)]

                            def mmv(e, b=b, tsub=tsub, pb=pb):
                                for fc in range(NF):
                                    last = e.matmul(
                                        pb[:, :],
                                        xn_buf[:, fc * TW + tsub * 128:
                                               fc * TW + tsub * 128 + 128],
                                        wb_v[b][:, fc * 512:(fc + 1) * 512],
                                        start=(fc == 0), stop=(fc == NF - 1))
                                return last
                            P.emit("pe", mmv, wait=waits, inc="c_pe")
                            pn = P.n["c_pe"]
                            wb_pe[("v", b)] = P.n["c_pe"]
                            vb = vst[tsub % 2]
                            wv = [("c_pe", pn)]
                            if ("v", tsub % 2) in scr_n:
                                wv.append(("d_scr", scr_n[("v", tsub % 2)]))
                            P.emit("dve", lambda e, pb=pb, vb=vb, os_=os_:
                                   e.tensor_tensor(
                                       vb[:, :], pb[:, :],
                                       c_bvrep[:, os_ * 512:(os_ + 1) * 512],
                                       op=ALU.add),
                                   wait=wv, inc="c_dve")
                            vn = P.n["c_dve"]
                            t0 = half * 2048 + tsub * 128
                            vr = vb[:, :].rearrange("p (h d) -> p h d", h=4)
                            for hh in range(4):
                                hg = os_ * 4 + hh
                                P.dma("syn", sV[hg, t0:t0 + 128, :],
                                      vr[:, hh], "d_scr",
                                      wait=[("c_dve", vn)] if hh == 0
                                      else None)
                            g0 = t0 - (t0 % 512)
                            if g0 in OWN:
                                tl = OWN[g0] * 512 + (t0 % 512)
                                for hh in range(4):
                                    hg = os_ * 4 + hh
                                    P.dma("syn", sVo[hg, tl:tl + 128, :],
                                          vr[:, hh], "d_scr")
                            scr_n[("v", tsub % 2)] = P.n["d_scr"]

                def u1_pass():
                    for ts in range(16):
                        ping, pong = u1t[0], u1t[1]
                        cp0 = P.n["c_pool"]
                        for c in range(NF):
                            pb = pacc[c % 4]
                            P.emit("pe", lambda e, c=c, ts=ts, pb=pb:
                                   e.matmul(
                                       pb[:, 0:128],
                                       hsu1[:, c * 128:(c + 1) * 128],
                                       xn_buf[:, c * TW + ts * 128:
                                              c * TW + ts * 128 + 128],
                                       start=True, stop=True),
                                   wait=[("c_dve", P.n["c_dve"] - 3)],
                                   inc="c_pe")
                            pn = P.n["c_pe"]
                            P.emit("dve", lambda e, c=c, pb=pb, ping=ping:
                                   e.tensor_copy(
                                       ping[:, c * 128:(c + 1) * 128],
                                       pb[:, 0:128]),
                                   wait=[("c_pe", pn), ("c_pool", cp0)],
                                   inc="c_dve")
                        for stg in range(4):
                            bit = 1 << stg

                            def bf(e, ping=ping, pong=pong, bit=bit):
                                for c in range(NF):
                                    lo, hi = c & ~bit, c | bit
                                    op = (ALU.add if (c & bit) == 0
                                          else ALU.subtract)
                                    last = e.tensor_tensor(
                                        pong[:, c * 128:(c + 1) * 128],
                                        ping[:, lo * 128:(lo + 1) * 128],
                                        ping[:, hi * 128:(hi + 1) * 128],
                                        op=op)
                                return last
                            eng = "dve" if stg % 2 == 0 else "pool"
                            cw = "c_pool" if stg % 2 == 0 else "c_dve"
                            P.emit(eng, bf, wait=(cw, P.n[cw]),
                                   inc="c_dve" if eng == "dve" else "c_pool")
                            ping, pong = pong, ping
                        bf_done_p = P.n["c_pool"]
                        for c in range(NF):
                            ub = u1s[c % 2]
                            wv = [("c_pool", bf_done_p)]
                            if ("u", c % 2) in scr_n:
                                wv.append(("d_scr", scr_n[("u", c % 2)]))
                            P.emit("dve", lambda e, c=c, ping=ping, ub=ub:
                                   e.tensor_copy(ub[:, :],
                                                 ping[:, c * 128:
                                                      (c + 1) * 128]),
                                   wait=wv, inc="c_dve")
                            vn = P.n["c_dve"]
                            P.dma("syn",
                                  sU1[c, :, ts * 128:(ts + 1) * 128],
                                  ub[:, :], "d_scr", wait=[("c_dve", vn)])
                            scr_n[("u", c % 2)] = P.n["d_scr"]

                for half in range(2):
                    ln_pass(x_kv[half * 2048:(half + 1) * 2048, :], 16)
                    P.run_block()
                    kq_pass(H, 16, None, sK, sKo, half)
                    v_pass(half)
                    P.run_block()
                ln_pass(x_own, 16)
                if tap == "sXN":
                    P.dma("syn", sXN.rearrange("c p t -> p c t"),
                          xn_buf[:, :].rearrange("p (c t) -> p c t", c=NF),
                          "d_scr",
                          wait=[("c_dve", P.n["c_dve"]),
                                ("c_pool", P.n["c_pool"]),
                                ("c_act", P.n["c_act"]),
                                ("c_pe", P.n["c_pe"])])
                P.run_block()
                kq_pass(0, 0, 2.0 ** -7, sQ, None, 0)
                u1_pass()
                P.run_block()

            with ExitStack() as p2:
                kqw = sb(p2, "kqw", [128, KQOFF + 512], F32)
                trb = sb(p2, "trb", [128, 4 * 512], F32)
                trm = sb(p2, "trm", [128, 4 * 512], F32)
                trh = sb(p2, "trh", [128, 4 * 512], F32)
                c_expb = sb(p2, "c_expb", [128, NH * NSLOT_RECT], F32)
                kbuf = [sb(p2, f"kbuf{i}", [128, S], BF16) for i in range(2)]
                kobuf = [sb(p2, f"kobuf{i}", [128, TOK], BF16)
                         for i in range(2)]
                vbuf = [sb(p2, f"vbuf{i}", [128, 32 * HD], BF16)
                        for i in range(2)]
                vobuf = [sb(p2, f"vobuf{i}", [128, 16 * HD], BF16)
                         for i in range(2)]
                qbuf = [sb(p2, f"qbuf{i}", [128, TOK], BF16)
                        for i in range(2)]
                s2b = [sb(p2, f"s2b{i}", [128, 512], F32) for i in range(2)]
                ptb = [sb(p2, f"ptb{i}", [128, 512], BF16) for i in range(2)]
                rrep = sb(p2, "rrep", [128, 512], F32)
                spst = [sb(p2, f"spst{i}", [128, 512], BF16)
                        for i in range(2)]

                P.dma("syn", c_expb[:, :], expb_in[:, :])
                P.emit("pool", lambda e: e.iota(
                    kqw[:, :], [[-1, KQOFF + 512]], base=KQOFF,
                    channel_multiplier=1,
                    allow_small_or_imprecise_dtypes=True), inc="c_pool")
                for t in range(4):
                    P.emit("pool", lambda e, t=t: e.iota(
                        trb[:, t * 512:(t + 1) * 512], [[-1, 512]],
                        base=128 * t, channel_multiplier=1,
                        allow_small_or_imprecise_dtypes=True), inc="c_pool")
                pn0 = P.n["c_pool"]
                P.emit("dve", lambda e: e.tensor_scalar(
                    trm[:, :], trb[:, :], 0.0, NEGBIG,
                    op0=ALU.is_gt, op1=ALU.mult),
                    wait=("c_pool", pn0), inc="c_dve")
                P.emit("dve", lambda e: e.tensor_scalar(
                    trm[:, :], trm[:, :], -MCONST, None, op0=ALU.add),
                    inc="c_dve")
                P.run_block()

                def head_dmas(h, b, guard_pe):
                    P.dma("syn", kbuf[b][:, :], sK[h, :, :], "d_scr",
                          wait=[("c_pe", guard_pe)])
                    P.dma("syn", kobuf[b][:, :], sKo[h, :, :], "d_scr")
                    P.dma("syn",
                          vbuf[b][:, :].rearrange("p (k d) -> p k d", k=32),
                          sV.rearrange("h (k p) d -> h p k d", p=128)[h],
                          "d_scr")
                    P.dma("syn",
                          vobuf[b][:, :].rearrange("p (k d) -> p k d", k=16),
                          sVo.rearrange("h (k p) d -> h p k d", p=128)[h],
                          "d_scr")
                    P.dma("syn", qbuf[b][:, :], sQ[h, :, :], "d_scr")

                SBK = mkpsum(p2, "p2s", 4)
                PVB = mkpsum(p2, "p2pv", 2)
                LB = mkpsum(p2, "p2l", 1)[0]
                rrep2 = [sb(p2, f"rrep2{i}", [128, 512], F32)
                         for i in range(2)]
                rrow = sb(p2, "rrow", [1, 512], F32)
                s2b3 = [sb(p2, f"s2b3{i}", [128, 512], F32) for i in range(4)]
                ptb3 = [sb(p2, f"ptb3{i}", [128, 512], BF16)
                        for i in range(4)]

                flat = []
                for h in range(NH):
                    for l in range(4):
                        kts = rect_kts(l)
                        slots = ([("rect", kt) for kt in kts]
                                 + [("tri", t) for t in range(4)])
                        nsl = len(slots)
                        for si, (kind, kt) in enumerate(slots):
                            flat.append(dict(
                                h=h, l=l, si=si, kind=kind, kt=kt,
                                first=(si == 0), last=(si == nsl - 1),
                                g=h * 4 + l, nsl=nsl,
                                col=(h * NSLOT_RECT + RECT_BASE[l] + si
                                     if kind == "rect" else -1)))
                NSL = len(flat)
                hist_qk = [0] * NSL
                hist_s2 = [0] * NSL
                hist_exp = [0] * NSL
                hist_pv = [0] * NSL
                hist_recip = {}
                hist_actcopy = {}
                hist_tt = {}
                head_ready = [0, 0]
                spst_dma = [0, 0]
                trh_hist = [0] * NH

                head_dmas(0, 0, 0)
                head_ready[0] = P.n["d_scr"]

                def emit_qk(i):
                    s = flat[i]
                    hb = s["h"] % 2
                    sbank = SBK[i % 4]
                    if s["kind"] == "rect":
                        koff, kb = s["kt"] * 128, kbuf[hb]
                    else:
                        koff, kb = s["l"] * 512 + s["kt"] * 128, kobuf[hb]
                    w = []
                    if s["first"] and s["l"] == 0:
                        w.append(("d_scr", head_ready[hb]))
                    if i >= 4:
                        w.append(("c_dve", hist_s2[i - 4]))
                    P.emit("pe", lambda e, kb=kb, koff=koff, sbank=sbank,
                           hb=hb, l=s["l"]: e.matmul(
                               sbank[:, :], kb[:, koff:koff + 128],
                               qbuf[hb][:, l * 512:(l + 1) * 512],
                               start=True, stop=True),
                           wait=w or None, inc="c_pe")
                    hist_qk[i] = P.n["c_pe"]

                def emit_s2(i):
                    s = flat[i]
                    sbank = SBK[i % 4]
                    s2 = s2b3[i % 4]
                    w = [("c_pe", hist_qk[i])]
                    if i >= 4:
                        w.append(("c_act", hist_exp[i - 4]))
                    if s["kind"] == "rect":
                        sl = KQOFF - s["kt"] * 128
                        P.emit("dve", lambda e, s2=s2, sl=sl, sbank=sbank,
                               h=s["h"]: e.scalar_tensor_tensor(
                                   s2[:, :], kqw[:, sl:sl + 512],
                                   SLOPES[h], sbank[:, :],
                                   op0=ALU.mult, op1=ALU.add),
                               wait=w, inc="c_dve")
                    else:
                        P.emit("dve", lambda e, s2=s2, kt=s["kt"],
                               sbank=sbank: e.tensor_tensor(
                                   s2[:, :], sbank[:, :],
                                   trh[:, kt * 512:(kt + 1) * 512],
                                   op=ALU.add),
                               wait=w, inc="c_dve")
                    hist_s2[i] = P.n["c_dve"]

                def emit_exp(i):
                    s = flat[i]
                    s2 = s2b3[i % 4]
                    pt = ptb3[i % 4]
                    w = [("c_dve", hist_s2[i])]
                    if i >= 4:
                        w.append(("c_pe", hist_pv[i - 4]))
                    if s["kind"] == "rect":
                        P.emit("act", lambda e, pt=pt, s2=s2, col=s["col"]:
                               e.activation(pt[:, :], s2[:, :], AF.Exp,
                                            bias=c_expb[:, col:col + 1],
                                            scale=1.0),
                               wait=w, inc="c_act")
                    else:
                        P.emit("act", lambda e, pt=pt, s2=s2: e.activation(
                            pt[:, :], s2[:, :], AF.Exp, bias=0.0, scale=1.0),
                            wait=w, inc="c_act")
                    hist_exp[i] = P.n["c_act"]

                def emit_pv(i):
                    s = flat[i]
                    hb = s["h"] % 2
                    g = s["g"]
                    pt = ptb3[i % 4]
                    vb = vbuf[hb] if s["kind"] == "rect" else vobuf[hb]
                    voff = (s["kt"] * 128 if s["kind"] == "rect"
                            else (s["l"] * 4 + s["kt"]) * 128)
                    pvb = PVB[g % 2]
                    w = [("c_act", hist_exp[i])]
                    if s["first"]:
                        if g - 2 in hist_tt:
                            w.append(("c_dve", hist_tt[g - 2]))
                        if g - 1 in hist_recip:
                            w.append(("c_dve", hist_recip[g - 1]))
                        if g - 1 in hist_actcopy:
                            w.append(("c_act", hist_actcopy[g - 1]))

                    def pv(e, vb=vb, voff=voff, pt=pt, si=s["si"],
                           nsl=s["nsl"], pvb=pvb):
                        e.matmul(pvb[:, :], vb[:, voff:voff + 128],
                                 pt[:, :], start=(si == 0),
                                 stop=(si == nsl - 1))
                        last = e.matmul(LB[0:1, :], ones_b[:, :], pt[:, :],
                                        start=(si == 0),
                                        stop=(si == nsl - 1))
                        return last
                    P.emit("pe", pv, wait=w, inc="c_pe")
                    hist_pv[i] = P.n["c_pe"]
                    if s["last"]:
                        hist_pv_last[s["g"]] = P.n["c_pe"]
                        emit_gend(s)

                def emit_gend(s):
                    g = s["g"]
                    h, l = s["h"], s["l"]
                    P.emit("dve", lambda e: e.reciprocal(
                        rrow[:, :], LB[0:1, :]),
                        wait=[("c_pe", hist_pv_last[g])], inc="c_dve")
                    hist_recip[g] = P.n["c_dve"]
                    P.emit("pe", lambda e: e.matmul(
                        LB[:, :], ones_f[:, :], rrow[:, :],
                        start=True, stop=True),
                        wait=[("c_dve", hist_recip[g])], inc="c_pe")
                    rr_n = P.n["c_pe"]
                    rr = rrep2[g % 2]
                    w = [("c_pe", rr_n)]
                    if g - 2 in hist_tt:
                        w.append(("c_dve", hist_tt[g - 2]))
                    P.emit("act", lambda e, rr=rr: e.activation(
                        rr[:, :], LB[:, :], AF.Identity, bias=0.0,
                        scale=1.0), wait=w, inc="c_act")
                    hist_actcopy[g] = P.n["c_act"]
                    sp = spst[g % 2]
                    P.emit("dve", lambda e, sp=sp, rr=rr, pvb=PVB[g % 2]:
                           e.tensor_tensor(sp[:, :], pvb[:, :], rr[:, :],
                                           op=ALU.mult),
                           wait=[("c_act", hist_actcopy[g]),
                                 ("d_scr", spst_dma[g % 2])],
                           inc="c_dve")
                    hist_tt[g] = P.n["c_dve"]
                    P.dma("syn", sSp[h, :, l * 512:(l + 1) * 512],
                          sp[:, :], "d_scr",
                          wait=[("c_dve", hist_tt[g])])
                    spst_dma[g % 2] = P.n["d_scr"]

                hist_pv_last = {}
                DEPTH = 4
                for i in range(NSL):
                    s = flat[i]
                    if s["first"] and s["l"] == 0:
                        h = s["h"]
                        if h + 1 < NH:
                            head_dmas(h + 1, (h + 1) % 2, P.n["c_pe"])
                            head_ready[(h + 1) % 2] = P.n["d_scr"]
                        P.emit("dve", lambda e, h=h: e.scalar_tensor_tensor(
                            trh[:, :], trb[:, :], SLOPES[h], trm[:, :],
                            op0=ALU.mult, op1=ALU.add), inc="c_dve")
                        trh_hist[h] = P.n["c_dve"]
                    if i >= DEPTH:
                        emit_pv(i - DEPTH)
                    emit_qk(i)
                    emit_s2(i)
                    emit_exp(i)
                for j in range(NSL - DEPTH, NSL):
                    emit_pv(j)
                P.run_block()

            with ExitStack() as p3:
                pacc = mkpsum(p3, "pacc3", 4)
                spre = sb(p3, "spre", [128, NH * TOK], BF16)
                wfp = sb(p3, "wfp", [128, NF * 128], F32)
                wbp = [sb(p3, f"wbp{i}", [128, NF * 128], BF16)
                       for i in range(2)]
                s32s = sb(p3, "s32s", [128, 512], F32)
                s32t = [sb(p3, f"s32t{i}", [128, 512], F32)
                        for i in range(2)]
                P.dma("syn", spre[:, :].rearrange("p (h q) -> p h q", h=NH),
                      sSp.rearrange("h p q -> p h q"), "d_scr")
                sp_rdy = P.n["d_scr"]
                w_proj3 = w_proj.rearrange("(c p) o -> p c o", p=128)
                pool_p = 0
                wbp_pe = {}
                s32_dma = [0, 0]
                for oc in range(16):
                    b = oc % 2
                    P.dma("syn",
                          wfp[:, :].rearrange("p (c o) -> p c o", c=NF),
                          w_proj3[:, :, oc * 128:(oc + 1) * 128], "d_w",
                          wait=[("c_pool", pool_p)])
                    dw = P.n["d_w"]
                    P.emit("pool", lambda e, b=b: e.tensor_copy(
                        wbp[b][:, :], wfp[:, :]),
                        wait=[("d_w", dw), ("c_pe", wbp_pe.get(b, 0))],
                        inc="c_pool")
                    pool_p = P.n["c_pool"]
                    cn = P.n["c_pool"]
                    for qs in range(4):
                        pb = pacc[qs % 3]
                        waits = [("c_pool", cn), ("d_scr", sp_rdy),
                                 ("c_dve", P.n["c_dve"] - 5)]

                        def mmp(e, b=b, qs=qs, pb=pb):
                            for fc in range(NF):
                                last = e.matmul(pb[:, :],
                                         wbp[b][:, fc * 128:(fc + 1) * 128],
                                         spre[:, fc * TOK + qs * 512:
                                              fc * TOK + qs * 512 + 512],
                                         start=(fc == 0),
                                         stop=(fc == NF - 1))
                            return last
                        P.emit("pe", mmp, wait=waits, inc="c_pe")
                        wbp_pe[b] = P.n["c_pe"]
                        pn = P.n["c_pe"]
                        P.emit("dve", lambda e, pb=pb, oc=oc: e.tensor_scalar(
                            s32s[:, :], pb[:, :], c_bproj[:, oc:oc + 1],
                            None, op0=ALU.add), wait=("c_pe", pn),
                            inc="c_dve")
                        ev_n = P.n["c_dve"]

                        def trp(e):
                            for i4 in range(4):
                                last = e.transpose(
                                    pacc[3][:, i4 * 128:(i4 + 1) * 128],
                                    s32s[:, i4 * 128:(i4 + 1) * 128],
                                    c_idf[:, :])
                            return last
                        P.emit("pe", trp, wait=[("c_dve", ev_n)], inc="c_pe")
                        tn = P.n["c_pe"]
                        stt = s32t[qs % 2]
                        P.emit("dve", lambda e, stt=stt: e.tensor_copy(
                            stt[:, :], pacc[3][:, :]),
                            wait=[("c_pe", tn), ("d_scr", s32_dma[qs % 2])],
                            inc="c_dve")
                        vn = P.n["c_dve"]
                        P.dma("syn",
                              sS32.rearrange("(qs i p) o -> p qs i o",
                                             i=4, p=128)[
                                  :, qs, :, oc * 128:(oc + 1) * 128],
                              stt[:, :].rearrange("p (i o) -> p i o", i=4),
                              "d_scr", wait=[("c_dve", vn)])
                        s32_dma[qs % 2] = P.n["d_scr"]
                P.run_block()

            with ExitStack() as p4p:
                ptr_b = mkpsum(p4p, "ptrb4", 2, [128, 1024], BF16)
                wfL = [sb(p4p, f"wfL{i}", [128, INTER], F32)
                       for i in range(2)]
                wfS = [sb(p4p, f"wfS{i}", [128, INTER], F32)
                       for i in range(2)]
                wpb = [sb(p4p, f"wpb{i}", [128, INTER], BF16)
                       for i in range(2)]
                wts = [sb(p4p, f"wts{i}", [128, 8 * 128], BF16)
                       for i in range(2)]
                wt_dma = [0, 0]
                ev_hist = [0, 0]
                trw_p = [0, 0]
                cast_p = [0, 0]
                for r in range(16):
                    pr = r % 2
                    eng = "dve" if pr == 0 else "pool"
                    cnt = "c_dve" if pr == 0 else "c_pool"
                    L, Sp = wfL[pr], wfS[pr]
                    P.dma("syn", L[:, 0:4096],
                          wdown[r * 128:(r + 1) * 128, 0:4096], "d_w",
                          wait=[("c_act", cast_p[pr])])
                    P.dma("syn", L[:, 4096:8192],
                          wdown[r * 128:(r + 1) * 128, 4096:8192], "d_w")
                    dw = P.n["d_w"]
                    for stg in range(6):
                        bit = 1 << stg
                        src, dst = (L, Sp) if stg % 2 == 0 else (Sp, L)
                        sv = src[:, :].rearrange("p (a s r) -> p a s r",
                                                 s=2, r=bit * 128)
                        dv = dst[:, :].rearrange("p (a s r) -> p a s r",
                                                 s=2, r=bit * 128)
                        w = None
                        if stg == 0:
                            w = [("d_w", dw)]
                        elif stg == 1:
                            w = [("c_act", cast_p[pr])]
                        P.emit(eng, lambda e, sv=sv, dv=dv:
                               e.tensor_tensor(dv[:, :, 0, :], sv[:, :, 0, :],
                                               sv[:, :, 1, :], op=ALU.add),
                               wait=w, inc=cnt)
                        P.emit(eng, lambda e, sv=sv, dv=dv:
                               e.tensor_tensor(dv[:, :, 1, :], sv[:, :, 0, :],
                                               sv[:, :, 1, :],
                                               op=ALU.subtract), inc=cnt)
                    chain_n = P.n[cnt]
                    for hv in range(2):
                        P.emit("act", lambda e, hv=hv, L=L, pr=pr:
                               e.activation(
                                   wpb[pr][:, hv * 4096:(hv + 1) * 4096],
                                   L[:, hv * 4096:(hv + 1) * 4096],
                                   AF.Identity, bias=0.0, scale=2.0 ** -6),
                               wait=[(cnt, chain_n),
                                     ("c_pe", trw_p[pr])] if hv == 0
                               else None,
                               inc="c_act")
                    cast_p[pr] = P.n["c_act"]
                    cn = P.n["c_act"]
                    for g8 in range(8):
                        pb = ptr_b[g8 % 2]
                        w2 = [("c_act", max(cn, ev_hist[g8 % 2]))]

                        def trw(e, g8=g8, pb=pb, pr=pr):
                            for cc in range(8):
                                fc = g8 * 8 + cc
                                last = e.transpose(
                                    pb[:, cc * 128:(cc + 1) * 128],
                                    wpb[pr][:, fc * 128:(fc + 1) * 128],
                                    c_idb[:, :])
                            return last
                        P.emit("pe", trw, wait=w2, inc="c_pe")
                        pn = P.n["c_pe"]
                        stg_b = wts[g8 % 2]
                        P.emit("act", lambda e, stg_b=stg_b, pb=pb:
                               e.activation(stg_b[:, :], pb[:, :],
                                            AF.Identity, bias=0.0,
                                            scale=1.0),
                               wait=[("c_pe", pn),
                                     ("d_scr", wt_dma[g8 % 2])],
                               inc="c_act")
                        vn = P.n["c_act"]
                        ev_hist[g8 % 2] = P.n["c_act"]
                        P.dma("syn",
                              sWdT[:, r, g8 * 8:(g8 + 1) * 8, :],
                              wts[g8 % 2][:, :].rearrange(
                                  "p (fc o) -> p fc o", fc=8),
                              "d_scr", wait=[("c_act", vn)])
                        wt_dma[g8 % 2] = P.n["d_scr"]
                    trw_p[pr] = P.n["c_pe"]
                P.run_block()

            with ExitStack() as p4a:
                pacc = mkpsum(p4a, "pacc4a", 6)
                ptw = mkpsum(p4a, "ptw", 2, [128, 1024], BF16)
                u1b = sb(p4a, "u1b", [128, NF * TOK], BF16)
                hsu2 = sb(p4a, "hsu2", [128, NI * 128], F32)
                wuf = [sb(p4a, f"wuf{i}", [128, H], F32) for i in range(2)]
                wgf = [sb(p4a, f"wgf{i}", [128, H], F32) for i in range(2)]
                wub = [sb(p4a, f"wub{i}", [128, H], BF16) for i in range(2)]
                wgb = [sb(p4a, f"wgb{i}", [128, H], BF16) for i in range(2)]
                wuT = [sb(p4a, f"wuT{i}", [128, NF * 128], BF16)
                       for i in range(2)]
                wgT = [sb(p4a, f"wgT{i}", [128, NF * 128], BF16)
                       for i in range(2)]
                sgb = [sb(p4a, f"sgb{i}", [128, 512], F32) for i in range(2)]
                x2s = [sb(p4a, f"x2s{i}", [128, 512], F32) for i in range(2)]
                x2a_s = [sb(p4a, f"x2as{i}", [128, 512], BF16)
                         for i in range(2)]
                P.dma("syn", u1b[:, :].rearrange("p (c t) -> p c t", c=NF),
                      sU1.rearrange("c p t -> p c t"), "d_scr")
                u1_rdy = P.n["d_scr"]
                for c in range(NI):
                    P.emit("dve", lambda e, c=c: e.tensor_scalar(
                        hsu2[:, c * 128:(c + 1) * 128], c_h128[:, :],
                        c_su2[:, c:c + 1], 2.0 ** -6,
                        op0=ALU.mult, op1=ALU.mult), inc="c_dve")
                hsu_done = P.n["c_dve"]
                dw_n = [0, 0]
                cast_hist = [0, 0]
                trw_hist = [0, 0]
                wT_done = [0, 0]
                tw_ev = [0, 0]
                mm_hist = [0, 0]
                silu_hist = [0, 0]
                stt_hist = [0, 0]
                sa_hist = [0, 0]
                ev_hist = [0, 0]
                x2a_dma = [0, 0]
                tw_idx = 0

                tw_state = {"idx": 0}

                def load_w(oc):
                    b = oc % 2
                    P.dma("syn", wuf[b][:, :],
                          wup[oc * 128:(oc + 1) * 128, :], "d_w",
                          wait=[("c_dve", cast_hist[b])])
                    P.dma("syn", wgf[b][:, :],
                          wgate[oc * 128:(oc + 1) * 128, :], "d_w")
                    dw_n[b] = P.n["d_w"]

                def prep_w(oc):
                    b = oc % 2
                    P.emit("dve", lambda e, b=b: e.tensor_scalar(
                        wub[b][:, :], wuf[b][:, :], 2.0 ** -6, None,
                        op0=ALU.mult),
                        wait=[("d_w", dw_n[b]), ("c_pe", trw_hist[b])],
                        inc="c_dve")
                    P.emit("dve", lambda e, b=b: e.tensor_scalar(
                        wgb[b][:, :], wgf[b][:, :], 2.0 ** -6, None,
                        op0=ALU.mult), inc="c_dve")
                    cast_hist[b] = P.n["c_dve"]
                    cn = P.n["c_dve"]
                    for (srcb, dstT) in ((wub, wuT), (wgb, wgT)):
                        for g in range(2):
                            ti = tw_state["idx"]
                            bank = ptw[ti % 2]
                            w2 = [("c_dve", cn),
                                  ("c_dve", tw_ev[ti % 2])]

                            def trg(e, srcb=srcb, b=b, g=g, bank=bank):
                                for cc in range(8):
                                    fc = g * 8 + cc
                                    last = e.transpose(
                                        bank[:, cc * 128:(cc + 1) * 128],
                                        srcb[b][:, fc * 128:(fc + 1) * 128],
                                        c_idb[:, :])
                                return last
                            P.emit("pe", trg, wait=w2, inc="c_pe")
                            pn = P.n["c_pe"]
                            P.emit("dve",
                                   lambda e, dstT=dstT, b=b, g=g, bank=bank:
                                   e.tensor_copy(
                                       dstT[b][:, g * 1024:(g + 1) * 1024],
                                       bank[:, :]),
                                   wait=[("c_pe", max(pn, mm_hist[b]))],
                                   inc="c_dve")
                            tw_ev[ti % 2] = P.n["c_dve"]
                            tw_state["idx"] = ti + 1
                    trw_hist[b] = P.n["c_pe"]
                    wT_done[b] = P.n["c_dve"]

                def emit_tail(k):
                    oc, ts = divmod(k, 4)
                    tp_ = ts % 2
                    pa_, pb_, pc_ = pacc[tp_], pacc[2 + tp_], pacc[4 + tp_]
                    mm_n = mm_of[k]
                    sg = sgb[tp_]
                    P.emit("act", lambda e, oc=oc, sg=sg, pb_=pb_:
                           e.activation(
                               sg[:, :], pb_[:, :], AF.Silu,
                               scale=c_vuvg[:, 64 + oc:65 + oc]),
                           wait=[("c_pe", mm_n),
                                 ("c_dve", stt_hist[tp_])],
                           inc="c_act")
                    silu_hist[tp_] = P.n["c_act"]
                    xs = x2s[tp_]
                    P.emit("dve", lambda e, oc=oc, xs=xs, sg=sg,
                           pa_=pa_: e.scalar_tensor_tensor(
                               xs[:, :], pa_[:, :],
                               c_vuvg[:, oc:oc + 1], sg[:, :],
                               op0=ALU.mult, op1=ALU.mult),
                           wait=[("c_act", silu_hist[tp_]),
                                 ("c_pe", sa_hist[tp_])],
                           inc="c_dve")
                    stt_hist[tp_] = P.n["c_dve"]
                    P.emit("pe", lambda e, oc=oc, xs=xs, pc_=pc_:
                           e.matmul(
                               pc_[:, :],
                               hsu2[:, oc * 128:(oc + 1) * 128],
                               xs[:, :], start=True, stop=True),
                           wait=[("c_dve", max(stt_hist[tp_],
                                               ev_hist[tp_]))],
                           inc="c_pe")
                    sa_hist[tp_] = P.n["c_pe"]
                    sa_n = P.n["c_pe"]
                    xa = x2a_s[tp_]
                    P.emit("dve", lambda e, xa=xa, pc_=pc_:
                           e.tensor_copy(xa[:, :], pc_[:, :]),
                           wait=[("c_pe", sa_n),
                                 ("d_scr", x2a_dma[tp_])],
                           inc="c_dve")
                    ev_hist[tp_] = P.n["c_dve"]
                    vn = P.n["c_dve"]
                    P.dma("syn",
                          sX2b[oc * 128:(oc + 1) * 128,
                               ts * 512:(ts + 1) * 512],
                          xa[:, :], "d_scr", wait=[("c_dve", vn)])
                    x2a_dma[tp_] = P.n["d_scr"]

                mm_of = {}
                load_w(0)
                load_w(1)
                prep_w(0)
                for k in range(NI * 4):
                    oc, ts = divmod(k, 4)
                    b = oc % 2
                    tp_ = ts % 2
                    pa_, pb_ = pacc[tp_], pacc[2 + tp_]
                    w = [("c_dve", max(wT_done[b], stt_hist[tp_],
                                       hsu_done if oc == 0 else 0)),
                         ("c_act", silu_hist[tp_])]
                    if k == 0:
                        w.append(("d_scr", u1_rdy))

                    def mmu(e, b=b, ts=ts, pa_=pa_, pb_=pb_):
                        for fc in range(NF):
                            e.matmul(pa_[:, :],
                                     wuT[b][:, fc * 128:(fc + 1) * 128],
                                     u1b[:, fc * TOK + ts * 512:
                                         fc * TOK + ts * 512 + 512],
                                     start=(fc == 0),
                                     stop=(fc == NF - 1))
                        for fc in range(NF):
                            last = e.matmul(
                                pb_[:, :],
                                wgT[b][:, fc * 128:(fc + 1) * 128],
                                u1b[:, fc * TOK + ts * 512:
                                    fc * TOK + ts * 512 + 512],
                                start=(fc == 0), stop=(fc == NF - 1))
                        return last
                    P.emit("pe", mmu, wait=w, inc="c_pe")
                    mm_hist[b] = P.n["c_pe"]
                    mm_of[k] = P.n["c_pe"]
                    if k > 0:
                        emit_tail(k - 1)
                    if ts == 1 and oc + 1 < NI:
                        prep_w(oc + 1)
                    if ts == 2 and oc + 2 < NI:
                        load_w(oc + 2)
                emit_tail(NI * 4 - 1)
                P.run_block()

            with ExitStack() as p4c:
                pacc = mkpsum(p4c, "pacc4c", 4)
                x2bb = [sb(p4c, f"x2bb{i}", [128, NI * 512], BF16)
                        for i in range(2)]
                wdts = [sb(p4c, f"wdtt{i}", [128, NI * 128], BF16)
                        for i in range(2)]
                x3s = [sb(p4c, f"x3s{i}", [128, 512], F32) for i in range(2)]
                gst = [sb(p4c, f"gst{i}", [128, 512], F32) for i in range(2)]
                x2b_r = sX2b.rearrange("(fc p) (ts t) -> p fc ts t",
                                       p=128, t=512)
                g_dma = [0, 0]
                wdt_pe = [0, 0]
                mm_hist = [0, 0]
                ev_hist = [0, 0]
                tr_hist = [0, 0]
                silu_hist = [0, 0]
                xr = [0, 0]

                def load_x2b(ts):
                    P.dma("syn",
                          x2bb[ts % 2][:, :].rearrange(
                              "p (c t) -> p c t", c=NI),
                          x2b_r[:, :, ts, :], "d_scr",
                          wait=[("c_pe", mm_hist[ts % 2])])
                    xr[ts % 2] = P.n["d_scr"]

                mm_of = {}

                def emit_tail_c(k):
                    ts, oc = divmod(k, 16)
                    pa_ = pacc[oc % 2]
                    x3 = x3s[oc % 2]
                    P.emit("dve", lambda e, oc=oc, x3=x3, pa_=pa_:
                           e.tensor_scalar(
                               x3[:, :], pa_[:, :],
                               c_vuvg[:, 128 + oc:129 + oc], None,
                               op0=ALU.mult),
                           wait=[("c_pe", max(mm_of[k],
                                              tr_hist[oc % 2]))],
                           inc="c_dve")
                    ev_hist[oc % 2] = P.n["c_dve"]
                    ev_n = P.n["c_dve"]
                    ptr_ = pacc[2 + oc % 2]

                    def trg(e, x3=x3, ptr_=ptr_):
                        for i4 in range(4):
                            last = e.transpose(
                                ptr_[:, i4 * 128:(i4 + 1) * 128],
                                x3[:, i4 * 128:(i4 + 1) * 128],
                                c_idf[:, :])
                        return last
                    P.emit("pe", trg,
                           wait=[("c_dve", ev_n),
                                 ("c_act", silu_hist[oc % 2])],
                           inc="c_pe")
                    tr_hist[oc % 2] = P.n["c_pe"]
                    tn = P.n["c_pe"]
                    gb = gst[oc % 2]
                    P.emit("act", lambda e, gb=gb, ptr_=ptr_:
                           e.activation(gb[:, :], ptr_[:, :], AF.Silu),
                           wait=[("c_pe", tn), ("d_scr", g_dma[oc % 2])],
                           inc="c_act")
                    silu_hist[oc % 2] = P.n["c_act"]
                    an = P.n["c_act"]
                    P.dma("syn",
                          sG.rearrange("(ts i p) o -> p ts i o",
                                       i=4, p=128)[
                              :, ts, :, oc * 128:(oc + 1) * 128],
                          gb[:, :].rearrange("p (i o) -> p i o", i=4),
                          "d_scr", wait=[("c_act", an)])
                    g_dma[oc % 2] = P.n["d_scr"]

                load_x2b(0)
                for k in range(64):
                    ts, oc = divmod(k, 16)
                    tb = ts % 2
                    b = oc % 2
                    if oc == 0 and ts + 1 < 4:
                        load_x2b(ts + 1)
                    P.dma("syn",
                          wdts[b][:, :].rearrange("p (c o) -> p c o",
                                                  c=NI),
                          sWdT[:, oc, :, :], "d_scr",
                          wait=[("c_pe", wdt_pe[b])])
                    dwn = P.n["d_scr"]
                    pa_ = pacc[oc % 2]

                    def mmd(e, b=b, tb=tb, pa_=pa_):
                        for fc in range(NI):
                            last = e.matmul(
                                pa_[:, :],
                                wdts[b][:, fc * 128:(fc + 1) * 128],
                                x2bb[tb][:, fc * 512:(fc + 1) * 512],
                                start=(fc == 0), stop=(fc == NI - 1))
                        return last
                    P.emit("pe", mmd,
                           wait=[("d_scr", max(dwn, xr[tb])),
                                 ("c_dve", ev_hist[oc % 2])],
                           inc="c_pe")
                    wdt_pe[b] = P.n["c_pe"]
                    mm_hist[tb] = P.n["c_pe"]
                    mm_of[k] = P.n["c_pe"]
                    if k > 0:
                        emit_tail_c(k - 1)
                emit_tail_c(63)
                P.run_block()

            with ExitStack() as p4d:
                grow = [sb(p4d, f"grow{i}", [128, H], F32) for i in range(2)]
                srow = [sb(p4d, f"srow{i}", [128, H], F32) for i in range(2)]
                xrow = [sb(p4d, f"xrow{i}", [128, H], F32) for i in range(2)]
                orow = [sb(p4d, f"orow{i}", [128, H], F32) for i in range(2)]
                comb_n = [0, 0]
                out_n = [0, 0]
                pool_n4 = [0, 0]
                for tt in range(16):
                    b = tt % 2
                    w = ([("c_dve", comb_n[b]),
                          ("c_pool", pool_n4[b])] if tt >= 2 else None)
                    P.dma("syn", grow[b][:, :],
                          sG[tt * 128:(tt + 1) * 128, :], "d_scr", wait=w)
                    P.dma("syn", srow[b][:, :],
                          sS32[tt * 128:(tt + 1) * 128, :], "d_scr")
                    P.dma("syn", xrow[b][:, :],
                          x_own[tt * 128:(tt + 1) * 128, :], "d_in")
                    dr, di = P.n["d_scr"], P.n["d_in"]

                    P.emit("dve", lambda e, b=b: e.tensor_tensor(
                        grow[b][:, :], grow[b][:, :], srow[b][:, :],
                        op=ALU.mult),
                        wait=[("d_scr", dr), ("d_in", di),
                              ("d_out", out_n[b])], inc="c_dve")
                    mul_n = P.n["c_dve"]
                    P.emit("pool", lambda e, b=b: e.tensor_tensor(
                        orow[b][:, :], grow[b][:, :], xrow[b][:, :],
                        op=ALU.add),
                        wait=[("c_dve", mul_n)], inc="c_pool")
                    comb_n[b] = P.n["c_dve"]
                    pool_n4[b] = P.n["c_pool"]
                    vn = P.n["c_pool"]
                    P.dma("syn", out_own[tt * 128:(tt + 1) * 128, :],
                          orow[b][:, :], "d_out", wait=[("c_pool", vn)])
                    out_n[b] = P.n["d_out"]
                if tap is not None:
                    tsrc = scr_map[tap]
                    shp = list(tsrc.tensor.shape)
                    tout = nc.dram_tensor(
                        "tapout", shp, tsrc.dtype,
                        kind="ExternalOutput").ap()
                    nch = max(1, shp[0] // 2048)
                    step = shp[0] // nch
                    for ci in range(nch):
                        P.dma("syn", tout[ci * step:(ci + 1) * step],
                              tsrc[ci * step:(ci + 1) * step], "d_out")
                P.run_block()
    return nc


def make_in_maps(x, w_qkv, b_qkv, w_proj, b_proj, su_ug, su_dp, ws_ug, ws_dp,
                 w_up, w_gate, w_down):
    h128 = _hadamard(128).astype(np.float32)
    identf = np.eye(128, dtype=np.float32)

    shared = {
        "w_qkv": np.ascontiguousarray(w_qkv, np.float32),
        "w_proj": np.ascontiguousarray(w_proj, np.float32),
        "wup": np.ascontiguousarray(w_up, np.float32),
        "wgate": np.ascontiguousarray(w_gate, np.float32),
        "wdown": np.ascontiguousarray(w_down, np.float32),
        "bqkv_cols": np.ascontiguousarray(
            np.asarray(b_qkv, np.float32).reshape(48, 128).T),
        "bvrep": np.ascontiguousarray(np.broadcast_to(
            np.asarray(b_qkv, np.float32)[4096:6144], (128, 2048))),
        "bproj_cols": np.ascontiguousarray(
            np.asarray(b_proj, np.float32).reshape(16, 128).T),
        "su1_cols": np.ascontiguousarray(
            np.asarray(su_ug, np.float32).reshape(16, 128).T),
        "su2_cols": np.ascontiguousarray(
            np.asarray(su_dp, np.float32).reshape(64, 128).T),
        "ws_cols": np.ascontiguousarray(np.concatenate([
            np.asarray(ws_ug, np.float32)[:8192].reshape(64, 128).T,
            np.asarray(ws_ug, np.float32)[8192:].reshape(64, 128).T,
            np.asarray(ws_dp, np.float32).reshape(16, 128).T], axis=1)),
        "h128": h128,
        "ident_f": identf,
        "ident_b": identf.astype(ml_dtypes.bfloat16),
    }
    in_maps = []
    for c in range(8):
        bi, pa = c // 2, c % 2
        qgs = [1024 * l + 512 * pa for l in range(4)]
        x_own = np.ascontiguousarray(
            np.concatenate([x[bi, qg:qg + 512] for qg in qgs], axis=0))
        x_kv = np.ascontiguousarray(np.roll(x[bi], -512 * pa, axis=0))
        expb = np.zeros((NH, NSLOT_RECT), np.float32)
        for hh in range(NH):
            s = 0
            for l in range(4):
                for kt in rect_kts(l):
                    if kt < 28:
                        expb[hh, s] = -(SLOPES[hh] * 1024 * l + MCONST)
                    elif pa == 1:
                        expb[hh, s] = -(SLOPES[hh] * (4096 + 1024 * l)
                                        + MCONST)
                    else:
                        expb[hh, s] = NEGBIG
                    s += 1
        m = dict(shared)
        m["x_own"] = x_own
        m["x_kv"] = x_kv
        m["expb"] = np.ascontiguousarray(np.broadcast_to(
            expb.reshape(1, -1), (128, NH * NSLOT_RECT)))
        in_maps.append(m)
    return in_maps


def kernel(L0_attn_hidden_states, L0_attn_w_qkv, L0_attn_b_qkv,
           L0_attn_w_proj, L0_attn_b_proj, L0_ffn_SU_ug, L0_ffn_SU_dp,
           L0_ffn_Wscale_ug, L0_ffn_Wscale_dp, L0_ffn_W_up, L0_ffn_W_gate,
           L0_ffn_W_down):
    x = np.ascontiguousarray(L0_attn_hidden_states, dtype=np.float32)
    in_maps = make_in_maps(x, L0_attn_w_qkv, L0_attn_b_qkv, L0_attn_w_proj,
                           L0_attn_b_proj, L0_ffn_SU_ug, L0_ffn_SU_dp,
                           L0_ffn_Wscale_ug, L0_ffn_Wscale_dp, L0_ffn_W_up,
                           L0_ffn_W_gate, L0_ffn_W_down)
    nc = build()
    res = run_bass_kernel_spmd(nc, in_maps, core_ids=list(range(8)))
    out = np.empty((B, S, H), np.float32)
    for c in range(8):
        bi, pa = c // 2, c % 2
        shard = res.results[c]["out_own"]
        for l in range(4):
            qg = 1024 * l + 512 * pa
            out[bi, qg:qg + 512] = shard[l * 512:(l + 1) * 512]
    return out


# version 16 (final)
# speedup vs baseline: 1.2303x; 1.0031x over previous
import math
from contextlib import ExitStack

import numpy as np
import ml_dtypes

import concourse.bass as bass
import concourse.mybir as mybir
from concourse.bass_utils import run_bass_kernel_spmd

F32 = mybir.dt.float32
BF16 = mybir.dt.bfloat16
AF = mybir.ActivationFunctionType
ALU = mybir.AluOpType
AX = mybir.AxisListType

B, S, H, NH, HD, INTER = 4, 4096, 2048, 16, 128, 8192
NF = H // 128
NI = INTER // 128
TOK = 2048
NEGBIG = -1.0e30
MCONST = 32.0
SQ2INV64 = 45.254834
SLOPES = [2.0 ** (-(i + 1) / 2.0) for i in range(NH)]
EPS = 1e-5
KQOFF = 31 * 128


def rect_kts(l):
    return list(range(8 * l)) + [28, 29, 30, 31]


RECT_BASE = [0, 4, 16, 36]
NSLOT_RECT = 64


def _hadamard(n):
    h = np.array([[1.0]])
    while h.shape[0] < n:
        h = np.block([[h, h], [h, -h]])
    return h


class Prog:
    def __init__(self, nc, es):
        self.nc = nc
        self.sems = {}
        self.n = {}
        for name in ("c_pe", "c_dve", "c_act", "c_pool",
                     "d_in", "d_w", "d_scr", "d_out"):
            self.sems[name] = es.enter_context(nc.semaphore(name))
            self.n[name] = 0
        self.new_walk()

    def new_walk(self):
        self.ops = {e: [] for e in ("pe", "dve", "act", "pool", "syn")}

    def emit(self, eng, fn, inc=None, inck=1, wait=None):
        if wait:
            for (sname, val) in ([wait] if isinstance(wait, tuple) else wait):
                if val > 0:
                    sem = self.sems[sname]
                    self.ops[eng].append(
                        lambda e, s=sem, v=val: e.wait_ge(s, v))
        if inc is not None:
            sem = self.sems[inc]
            self.n[inc] += inck

            def run(e, f=fn, s=sem, k=inck):
                f(e).then_inc(s, k)
            self.ops[eng].append(run)
        else:
            self.ops[eng].append(fn)

    def dma(self, eng, out_ap, in_ap, sem="d_in", wait=None):
        self.emit(eng, lambda e, o=out_ap, i=in_ap: e.dma_start(o, i),
                  inc=sem, inck=16, wait=wait)

    def run_block(self):
        for sname in ("d_in", "d_w", "d_scr", "d_out"):
            if self.n[sname] > 0:
                sem, v = self.sems[sname], self.n[sname]
                self.ops["syn"].append(lambda e, s=sem, vv=v: e.wait_ge(s, vv))
        ops = self.ops
        with self.nc.Block() as blk:
            if ops["syn"]:
                @blk.sync
                def _(e, lst=list(ops["syn"])):
                    for f in lst:
                        f(e)
            if ops["pe"]:
                @blk.tensor
                def _(e, lst=list(ops["pe"])):
                    for f in lst:
                        f(e)
            if ops["dve"]:
                @blk.vector
                def _(e, lst=list(ops["dve"])):
                    for f in lst:
                        f(e)
            if ops["act"]:
                @blk.scalar
                def _(e, lst=list(ops["act"])):
                    for f in lst:
                        f(e)
            if ops["pool"]:
                @blk.gpsimd
                def _(e, lst=list(ops["pool"])):
                    for f in lst:
                        f(e)
        self.new_walk()


def build(tap=None):
    nc = bass.Bass("TRN2", target_bir_lowering=False)

    def din(name, shape, dt=F32):
        return nc.dram_tensor(name, shape, dt, kind="ExternalInput").ap()

    x_own = din("x_own", [TOK, H])
    x_kv = din("x_kv", [S, H])
    w_qkv = din("w_qkv", [H, 3 * H])
    w_proj = din("w_proj", [H, H])
    wup = din("wup", [INTER, H])
    wgate = din("wgate", [INTER, H])
    wdown = din("wdown", [H, INTER])
    bqkv_cols = din("bqkv_cols", [128, 48])
    bvrep_in = din("bvrep", [128, H])
    bproj_cols = din("bproj_cols", [128, 16])
    su1_cols = din("su1_cols", [128, NF])
    su2_cols = din("su2_cols", [128, NI])
    ws_cols = din("ws_cols", [128, 144])
    h128_in = din("h128", [128, 128])
    ident_b = din("ident_b", [128, 128], BF16)
    ident_f = din("ident_f", [128, 128])
    expb_in = din("expb", [128, NH * NSLOT_RECT])

    out_own = nc.dram_tensor("out_own", [TOK, H], F32,
                             kind="ExternalOutput").ap()

    def mkscr(name, shape, dt):
        return nc.dram_tensor(name, shape, dt, kind="Internal").ap()

    sK = mkscr("sK", [NH, 128, S], BF16)
    sV = mkscr("sV", [NH, S, HD], BF16)
    sKo = mkscr("sKo", [NH, 128, TOK], BF16)
    sVo = mkscr("sVo", [NH, TOK, HD], BF16)
    sQ = mkscr("sQ", [NH, 128, TOK], BF16)
    sU1 = mkscr("sU1", [NF, 128, TOK], BF16)
    sSp = mkscr("sSp", [NH, 128, TOK], BF16)
    sS32 = mkscr("sS32", [TOK, H], F32)
    sX2b = mkscr("sX2b", [INTER, TOK], BF16)
    sWdT = mkscr("sWdT", [128, 16, NI, 128], BF16)
    sG = mkscr("sG", [TOK, H], F32)
    sXN = mkscr("sXN", [NF, 128, 2048], BF16)
    scr_map = {"sXN": sXN,"sK": sK, "sV": sV, "sKo": sKo, "sVo": sVo, "sQ": sQ,
               "sU1": sU1, "sSp": sSp, "sS32": sS32,
               "sX2b": sX2b, "sG": sG}

    with ExitStack() as es:
        P = Prog(nc, es)

        def sb(stk, name, shape, dt):
            return stk.enter_context(nc.sbuf_tensor(name, shape, dt))

        c_bqkv = sb(es, "c_bqkv", [128, 48], F32)
        c_bproj = sb(es, "c_bproj", [128, 16], F32)
        c_su2 = sb(es, "c_su2", [128, NI], F32)
        c_vuvg = sb(es, "c_vuvg", [128, 144], F32)
        c_idb = sb(es, "c_idb", [128, 128], BF16)
        c_idf = sb(es, "c_idf", [128, 128], F32)
        c_h128 = sb(es, "c_h128", [128, 128], F32)
        ones_b = sb(es, "ones_b", [128, 1], BF16)
        ones_f = sb(es, "ones_f", [1, 128], F32)
        def mkpsum(stk, pref, n, shape=None, dt=F32):
            return [stk.enter_context(nc.psum_tensor(
                f"{pref}{i}", shape or [128, 512], dt)) for i in range(n)]

        with ExitStack() as ph:
            csu1 = sb(ph, "csu1", [128, NF], F32)
            P.dma("syn", c_bqkv[:, :], bqkv_cols[:, :])
            P.dma("syn", c_bproj[:, :], bproj_cols[:, :])
            P.dma("syn", csu1[:, :], su1_cols[:, :])
            P.dma("syn", c_su2[:, :], su2_cols[:, :])
            P.dma("syn", c_vuvg[:, :], ws_cols[:, :])
            P.dma("syn", c_idb[:, :], ident_b[:, :])
            P.dma("syn", c_idf[:, :], ident_f[:, :])
            P.dma("syn", c_h128[:, :], h128_in[:, :])
            P.emit("pool", lambda e: e.memset(ones_b[:, :], 1.0))
            P.emit("pool", lambda e: e.memset(ones_f[:, :], 1.0))
            d0 = P.n["d_in"]
            P.emit("dve", lambda e: e.tensor_scalar(
                c_vuvg[:, :], c_vuvg[:, :], SQ2INV64, None, op0=ALU.mult),
                wait=("d_in", d0), inc="c_dve")
            hsu1_holder = {}
            P.run_block()

            with ExitStack() as p1:
                pacc = mkpsum(p1, "pacc", 4)
                ptr_b = mkpsum(p1, "ptrb", 2, [128, 1024], BF16)
                TW = 2048
                xn_buf = sb(p1, "xn_buf", [128, NF * TW], BF16)
                xnt = sb(p1, "xnt", [128, H], BF16)
                ln_sm = sb(p1, "ln_sm", [128, 228], F32)
                xst = [sb(p1, f"xst{i}", [128, H], F32) for i in range(2)]
                c_bvrep = sb(p1, "c_bvrep", [128, H], F32)
                hsu1 = sb(p1, "hsu1", [128, NF * 128], BF16)
                wf_a = [sb(p1, f"wf_a{i}", [128, NF * 128], F32)
                        for i in range(2)]
                scrap = wf_a[0]
                wb_a = [sb(p1, f"wba{i}", [128, NF * 128], BF16)
                        for i in range(2)]
                wf_v = sb(p1, "wf_v", [128, NF * 256], F32)
                wb_v = [sb(p1, f"wbv{i}", [128, NF * 512], BF16)
                        for i in range(2)]
                kst = [sb(p1, f"kst{i}", [128, 512], BF16) for i in range(2)]
                vst = [sb(p1, f"vst{i}", [128, 512], BF16) for i in range(2)]
                u1t = [sb(p1, f"u1t{i}", [128, NF * 128], F32)
                       for i in range(2)]
                u1s = [sb(p1, f"u1s{i}", [128, 128], BF16) for i in range(2)]

                P.dma("syn", c_bvrep[:, :], bvrep_in[:, :])
                for c in range(NF):
                    P.emit("dve", lambda e, c=c: e.tensor_scalar(
                        hsu1[:, c * 128:(c + 1) * 128], c_h128[:, :],
                        csu1[:, c:c + 1], 2.0 ** -5,
                        op0=ALU.mult, op1=ALU.mult), inc="c_dve")

                P.emit("pool", lambda e: e.memset(ln_sm[:, 224:225], EPS),
                       inc="c_pool")
                xn3 = xn_buf[:, :].rearrange("p (c t) -> p c t", c=NF)
                st = {"norm": [0, 0], "ev": [0, 0], "tr": 0, "sums": [0, 0]}

                def ln_pass(src_ap, n_tiles):
                    for i in range(n_tiles):
                        xb = xst[i % 2]
                        P.dma("syn", xb[:, :],
                              src_ap[i * 128:(i + 1) * 128, :], "d_in",
                              wait=[("c_dve", st["sums"][i % 2])])
                        dn = P.n["d_in"]
                        P.emit("dve", lambda e, xb=xb, i=i: e.reduce_sum(
                            ln_sm[:, i:i + 1], xb[:, :], axis=AX.X,
                            negate=True),
                            wait=("d_in", dn), inc="c_dve")
                        P.emit("dve", lambda e, xb=xb, i=i:
                               e.scalar_tensor_tensor(
                                   scrap[:, :], xb[:, :], 1.0, xb[:, :],
                                   op0=ALU.mult, op1=ALU.mult,
                                   accum_out=ln_sm[:, 16 + i:17 + i]),
                               inc="c_dve")
                        st["sums"][i % 2] = P.n["c_dve"]
                    NM, S1, M2, A_, LN_, LNC, R0, T3, T4, T5, RS, NMA = (
                        32, 48, 64, 80, 96, 112, 128, 144, 160, 176, 192, 208)
                    nt = n_tiles
                    vd = P.n["c_dve"]
                    P.emit("act", lambda e: e.activation(
                        ln_sm[:, NM:NM + nt], ln_sm[:, 0:nt], AF.Identity,
                        bias=0.0, scale=1.0 / H),
                        wait=("c_dve", vd), inc="c_act")
                    P.emit("act", lambda e: e.activation(
                        ln_sm[:, S1:S1 + nt], ln_sm[:, 16:16 + nt],
                        AF.Identity, bias=ln_sm[:, 224:225],
                        scale=1.0 / H), inc="c_act")
                    va = P.n["c_act"]
                    P.emit("dve", lambda e: e.tensor_tensor(
                        ln_sm[:, M2:M2 + nt], ln_sm[:, NM:NM + nt],
                        ln_sm[:, NM:NM + nt], op=ALU.mult),
                        wait=("c_act", va), inc="c_dve")
                    vd = P.n["c_dve"]
                    P.emit("pool", lambda e: e.tensor_tensor(
                        ln_sm[:, A_:A_ + nt], ln_sm[:, S1:S1 + nt],
                        ln_sm[:, M2:M2 + nt], op=ALU.subtract),
                        wait=[("c_act", va), ("c_dve", vd)], inc="c_pool")
                    vp = P.n["c_pool"]
                    P.emit("act", lambda e: e.activation(
                        ln_sm[:, LN_:LN_ + nt], ln_sm[:, A_:A_ + nt], AF.Ln),
                        wait=("c_pool", vp), inc="c_act")
                    va = P.n["c_act"]
                    P.emit("pool", lambda e: e.tensor_copy(
                        ln_sm[:, LNC:LNC + nt], ln_sm[:, LN_:LN_ + nt]),
                        wait=("c_act", va), inc="c_pool")
                    vp = P.n["c_pool"]
                    P.emit("act", lambda e: e.activation(
                        ln_sm[:, R0:R0 + nt], ln_sm[:, LNC:LNC + nt],
                        AF.Exp, scale=-0.5),
                        wait=("c_pool", vp), inc="c_act")
                    va = P.n["c_act"]
                    P.emit("dve", lambda e: e.tensor_tensor(
                        ln_sm[:, T3:T3 + nt], ln_sm[:, R0:R0 + nt],
                        ln_sm[:, R0:R0 + nt], op=ALU.mult),
                        wait=("c_act", va), inc="c_dve")
                    vd = P.n["c_dve"]
                    P.emit("pool", lambda e: e.tensor_tensor(
                        ln_sm[:, T4:T4 + nt], ln_sm[:, T3:T3 + nt],
                        ln_sm[:, A_:A_ + nt], op=ALU.mult),
                        wait=("c_dve", vd), inc="c_pool")
                    vp = P.n["c_pool"]
                    P.emit("dve", lambda e: e.tensor_scalar(
                        ln_sm[:, T5:T5 + nt], ln_sm[:, T4:T4 + nt], -0.5,
                        1.5, op0=ALU.mult, op1=ALU.add),
                        wait=("c_pool", vp), inc="c_dve")
                    vd = P.n["c_dve"]
                    P.emit("pool", lambda e: e.tensor_tensor(
                        ln_sm[:, RS:RS + nt], ln_sm[:, T5:T5 + nt],
                        ln_sm[:, R0:R0 + nt], op=ALU.mult),
                        wait=("c_dve", vd), inc="c_pool")
                    vp = P.n["c_pool"]
                    P.emit("dve", lambda e: e.tensor_tensor(
                        ln_sm[:, NMA:NMA + nt], ln_sm[:, NM:NM + nt],
                        ln_sm[:, RS:RS + nt], op=ALU.mult),
                        wait=("c_pool", vp), inc="c_dve")
                    chain_d = P.n["c_dve"]
                    chain_p = P.n["c_pool"]
                    sums_done = chain_d
                    for i in range(n_tiles):
                        xb = xst[i % 2]
                        P.dma("syn", xb[:, :],
                              src_ap[i * 128:(i + 1) * 128, :], "d_in",
                              wait=[("c_act", st["norm"][i % 2]),
                                    ("c_dve", sums_done)])
                        dn = P.n["d_in"]
                        P.emit("act", lambda e, xb=xb, i=i: e.activation(
                            xnt[:, :], xb[:, :], AF.Identity,
                            bias=ln_sm[:, NMA + i:NMA + i + 1],
                            scale=ln_sm[:, RS + i:RS + i + 1]),
                            wait=[("d_in", dn), ("c_dve", chain_d),
                                  ("c_pool", chain_p), ("c_pe", st["tr"])],
                            inc="c_act")
                        st["norm"][i % 2] = P.n["c_act"]
                        a3 = P.n["c_act"]
                        for hf in range(2):
                            pb = ptr_b[hf]

                            def trs(e, hf=hf, pb=pb):
                                for cc in range(8):
                                    c = hf * 8 + cc
                                    last = e.transpose(
                                        pb[:, cc * 128:(cc + 1) * 128],
                                        xnt[:, c * 128:(c + 1) * 128],
                                        c_idb[:, :])
                                return last
                            P.emit("pe", trs,
                                   wait=[("c_act", a3),
                                         ("c_dve", st["ev"][hf])],
                                   inc="c_pe")
                            pn = P.n["c_pe"]
                            P.emit("dve",
                                   lambda e, hf=hf, pb=pb, i=i:
                                   e.tensor_copy(
                                       xn3[:, hf * 8:(hf + 1) * 8,
                                           i * 128:(i + 1) * 128],
                                       pb[:, :].rearrange(
                                           "p (c t) -> p c t", c=8)),
                                   wait=("c_pe", pn), inc="c_dve")
                            st["ev"][hf] = P.n["c_dve"]
                        st["tr"] = P.n["c_pe"]

                w_qkv3 = w_qkv.rearrange("(c p) o -> p c o", p=128)
                pool_n = {"a": 0, "v0": 0, "v1": 0}
                wb_pe = {}
                scr_n = {}
                OWN = {0: 0, 1024: 1, 2048: 2, 3072: 3}

                def kq_pass(col_off, bias_off, qscale, dst, own_dst, half):
                    for oc in range(NH):
                        b = oc % 2
                        P.dma("syn",
                              wf_a[b][:, :].rearrange("p (c o) -> p c o",
                                                      c=NF),
                              w_qkv3[:, :, col_off + oc * 128:
                                     col_off + (oc + 1) * 128],
                              "d_w", wait=[("c_pool", pool_n.get(
                                  ("a", b), 0))])
                        dw = P.n["d_w"]
                        P.emit("pool", lambda e, b=b: e.tensor_copy(
                            wb_a[b][:, :], wf_a[b][:, :]),
                            wait=[("d_w", dw), ("c_pe", wb_pe.get(b, 0))],
                            inc="c_pool")
                        pool_n[("a", b)] = P.n["c_pool"]
                        cn = P.n["c_pool"]
                        for ts in range(4):
                            pb = pacc[ts]
                            waits = [("c_pool", cn),
                                     ("c_dve", P.n["c_dve"] - 3)]

                            def mms(e, b=b, ts=ts, pb=pb):
                                for fc in range(NF):
                                    last = e.matmul(
                                        pb[:, :],
                                        wb_a[b][:, fc * 128:(fc + 1) * 128],
                                        xn_buf[:, fc * TW + ts * 512:
                                               fc * TW + ts * 512 + 512],
                                        start=(fc == 0), stop=(fc == NF - 1))
                                return last
                            P.emit("pe", mms, wait=waits, inc="c_pe")
                            pn = P.n["c_pe"]
                            wb_pe[b] = P.n["c_pe"]
                            kb = kst[ts % 2]
                            wv = [("c_pe", pn)]
                            if ("k", ts % 2) in scr_n:
                                wv.append(("d_scr", scr_n[("k", ts % 2)]))
                            if qscale is None:
                                P.emit("dve", lambda e, pb=pb, kb=kb, oc=oc:
                                       e.tensor_scalar(
                                           kb[:, :], pb[:, :],
                                           c_bqkv[:, bias_off + oc:
                                                  bias_off + oc + 1],
                                           None, op0=ALU.add),
                                       wait=wv, inc="c_dve")
                            else:
                                P.emit("dve", lambda e, pb=pb, kb=kb, oc=oc:
                                       e.tensor_scalar(
                                           kb[:, :], pb[:, :],
                                           c_bqkv[:, bias_off + oc:
                                                  bias_off + oc + 1],
                                           qscale, op0=ALU.add, op1=ALU.mult),
                                       wait=wv, inc="c_dve")
                            vn = P.n["c_dve"]
                            t0 = half * 2048 + ts * 512
                            P.dma("syn", dst[oc, :, t0:t0 + 512], kb[:, :],
                                  "d_scr", wait=[("c_dve", vn)])
                            if own_dst is not None and t0 in OWN:
                                l = OWN[t0]
                                P.dma("syn",
                                      own_dst[oc, :, l * 512:(l + 1) * 512],
                                      kb[:, :], "d_scr")
                            scr_n[("k", ts % 2)] = P.n["d_scr"]

                def v_pass(half):
                    for os_ in range(4):
                        b = os_ % 2
                        wbv3 = wb_v[b][:, :].rearrange(
                            "p (c o) -> p c o", c=NF)
                        for hv in range(2):
                            P.dma("syn",
                                  wf_v[:, :].rearrange(
                                      "p (c o) -> p c o", c=NF),
                                  w_qkv3[:, :, 2 * H + os_ * 512 + hv * 256:
                                         2 * H + os_ * 512 + hv * 256 + 256],
                                  "d_w",
                                  wait=[("c_pool", pool_n.get("vf", 0))])
                            dw = P.n["d_w"]
                            P.emit("pool", lambda e, b=b, hv=hv, wbv3=wbv3:
                                   e.tensor_copy(
                                       wbv3[:, :, hv * 256:(hv + 1) * 256],
                                       wf_v[:, :].rearrange(
                                           "p (c o) -> p c o", c=NF)),
                                   wait=[("d_w", dw),
                                         ("c_pe", wb_pe.get(("v", b), 0))],
                                   inc="c_pool")
                            pool_n["vf"] = P.n["c_pool"]
                        pool_n[f"v{b}"] = P.n["c_pool"]
                        cn = P.n["c_pool"]
                        for tsub in range(16):
                            pb = pacc[tsub % 4]
                            waits = [("c_pool", cn),
                                     ("c_dve", P.n["c_dve"] - 3)]

                            def mmv(e, b=b, tsub=tsub, pb=pb):
                                for fc in range(NF):
                                    last = e.matmul(
                                        pb[:, :],
                                        xn_buf[:, fc * TW + tsub * 128:
                                               fc * TW + tsub * 128 + 128],
                                        wb_v[b][:, fc * 512:(fc + 1) * 512],
                                        start=(fc == 0), stop=(fc == NF - 1))
                                return last
                            P.emit("pe", mmv, wait=waits, inc="c_pe")
                            pn = P.n["c_pe"]
                            wb_pe[("v", b)] = P.n["c_pe"]
                            vb = vst[tsub % 2]
                            wv = [("c_pe", pn)]
                            if ("v", tsub % 2) in scr_n:
                                wv.append(("d_scr", scr_n[("v", tsub % 2)]))
                            P.emit("dve", lambda e, pb=pb, vb=vb, os_=os_:
                                   e.tensor_tensor(
                                       vb[:, :], pb[:, :],
                                       c_bvrep[:, os_ * 512:(os_ + 1) * 512],
                                       op=ALU.add),
                                   wait=wv, inc="c_dve")
                            vn = P.n["c_dve"]
                            t0 = half * 2048 + tsub * 128
                            vr = vb[:, :].rearrange("p (h d) -> p h d", h=4)
                            for hh in range(4):
                                hg = os_ * 4 + hh
                                P.dma("syn", sV[hg, t0:t0 + 128, :],
                                      vr[:, hh], "d_scr",
                                      wait=[("c_dve", vn)] if hh == 0
                                      else None)
                            g0 = t0 - (t0 % 512)
                            if g0 in OWN:
                                tl = OWN[g0] * 512 + (t0 % 512)
                                for hh in range(4):
                                    hg = os_ * 4 + hh
                                    P.dma("syn", sVo[hg, tl:tl + 128, :],
                                          vr[:, hh], "d_scr")
                            scr_n[("v", tsub % 2)] = P.n["d_scr"]

                def u1_pass():
                    for ts in range(16):
                        ping, pong = u1t[0], u1t[1]
                        cp0 = P.n["c_pool"]
                        for c in range(NF):
                            pb = pacc[c % 4]
                            P.emit("pe", lambda e, c=c, ts=ts, pb=pb:
                                   e.matmul(
                                       pb[:, 0:128],
                                       hsu1[:, c * 128:(c + 1) * 128],
                                       xn_buf[:, c * TW + ts * 128:
                                              c * TW + ts * 128 + 128],
                                       start=True, stop=True),
                                   wait=[("c_dve", P.n["c_dve"] - 3)],
                                   inc="c_pe")
                            pn = P.n["c_pe"]
                            P.emit("dve", lambda e, c=c, pb=pb, ping=ping:
                                   e.tensor_copy(
                                       ping[:, c * 128:(c + 1) * 128],
                                       pb[:, 0:128]),
                                   wait=[("c_pe", pn), ("c_pool", cp0)],
                                   inc="c_dve")
                        for stg in range(4):
                            bit = 1 << stg

                            def bf(e, ping=ping, pong=pong, bit=bit):
                                for c in range(NF):
                                    lo, hi = c & ~bit, c | bit
                                    op = (ALU.add if (c & bit) == 0
                                          else ALU.subtract)
                                    last = e.tensor_tensor(
                                        pong[:, c * 128:(c + 1) * 128],
                                        ping[:, lo * 128:(lo + 1) * 128],
                                        ping[:, hi * 128:(hi + 1) * 128],
                                        op=op)
                                return last
                            eng = "dve" if stg % 2 == 0 else "pool"
                            cw = "c_pool" if stg % 2 == 0 else "c_dve"
                            P.emit(eng, bf, wait=(cw, P.n[cw]),
                                   inc="c_dve" if eng == "dve" else "c_pool")
                            ping, pong = pong, ping
                        bf_done_p = P.n["c_pool"]
                        for c in range(NF):
                            ub = u1s[c % 2]
                            wv = [("c_pool", bf_done_p)]
                            if ("u", c % 2) in scr_n:
                                wv.append(("d_scr", scr_n[("u", c % 2)]))
                            P.emit("dve", lambda e, c=c, ping=ping, ub=ub:
                                   e.tensor_copy(ub[:, :],
                                                 ping[:, c * 128:
                                                      (c + 1) * 128]),
                                   wait=wv, inc="c_dve")
                            vn = P.n["c_dve"]
                            P.dma("syn",
                                  sU1[c, :, ts * 128:(ts + 1) * 128],
                                  ub[:, :], "d_scr", wait=[("c_dve", vn)])
                            scr_n[("u", c % 2)] = P.n["d_scr"]

                for half in range(2):
                    ln_pass(x_kv[half * 2048:(half + 1) * 2048, :], 16)
                    P.run_block()
                    kq_pass(H, 16, None, sK, sKo, half)
                    v_pass(half)
                    P.run_block()
                ln_pass(x_own, 16)
                if tap == "sXN":
                    P.dma("syn", sXN.rearrange("c p t -> p c t"),
                          xn_buf[:, :].rearrange("p (c t) -> p c t", c=NF),
                          "d_scr",
                          wait=[("c_dve", P.n["c_dve"]),
                                ("c_pool", P.n["c_pool"]),
                                ("c_act", P.n["c_act"]),
                                ("c_pe", P.n["c_pe"])])
                P.run_block()
                kq_pass(0, 0, 2.0 ** -7, sQ, None, 0)
                u1_pass()
                P.run_block()

            with ExitStack() as p2:
                kqw = sb(p2, "kqw", [128, KQOFF + 512], F32)
                trb = sb(p2, "trb", [128, 4 * 512], F32)
                trm = sb(p2, "trm", [128, 4 * 512], F32)
                trh = sb(p2, "trh", [128, 4 * 512], F32)
                c_expb = sb(p2, "c_expb", [128, NH * NSLOT_RECT], F32)
                kbuf = [sb(p2, f"kbuf{i}", [128, S], BF16) for i in range(2)]
                kobuf = [sb(p2, f"kobuf{i}", [128, TOK], BF16)
                         for i in range(2)]
                vbuf = [sb(p2, f"vbuf{i}", [128, 32 * HD], BF16)
                        for i in range(2)]
                vobuf = [sb(p2, f"vobuf{i}", [128, 16 * HD], BF16)
                         for i in range(2)]
                qbuf = [sb(p2, f"qbuf{i}", [128, TOK], BF16)
                        for i in range(2)]
                s2b = [sb(p2, f"s2b{i}", [128, 512], F32) for i in range(2)]
                ptb = [sb(p2, f"ptb{i}", [128, 512], BF16) for i in range(2)]
                rrep = sb(p2, "rrep", [128, 512], F32)
                spst = [sb(p2, f"spst{i}", [128, 512], BF16)
                        for i in range(2)]

                P.dma("syn", c_expb[:, :], expb_in[:, :])
                P.emit("pool", lambda e: e.iota(
                    kqw[:, :], [[-1, KQOFF + 512]], base=KQOFF,
                    channel_multiplier=1,
                    allow_small_or_imprecise_dtypes=True), inc="c_pool")
                for t in range(4):
                    P.emit("pool", lambda e, t=t: e.iota(
                        trb[:, t * 512:(t + 1) * 512], [[-1, 512]],
                        base=128 * t, channel_multiplier=1,
                        allow_small_or_imprecise_dtypes=True), inc="c_pool")
                pn0 = P.n["c_pool"]
                P.emit("dve", lambda e: e.tensor_scalar(
                    trm[:, :], trb[:, :], 0.0, NEGBIG,
                    op0=ALU.is_gt, op1=ALU.mult),
                    wait=("c_pool", pn0), inc="c_dve")
                P.emit("dve", lambda e: e.tensor_scalar(
                    trm[:, :], trm[:, :], -MCONST, None, op0=ALU.add),
                    inc="c_dve")
                P.run_block()

                def head_dmas(h, b, guard_pe):
                    P.dma("syn", kbuf[b][:, :], sK[h, :, :], "d_scr",
                          wait=[("c_pe", guard_pe)])
                    P.dma("syn", kobuf[b][:, :], sKo[h, :, :], "d_scr")
                    P.dma("syn",
                          vbuf[b][:, :].rearrange("p (k d) -> p k d", k=32),
                          sV.rearrange("h (k p) d -> h p k d", p=128)[h],
                          "d_scr")
                    P.dma("syn",
                          vobuf[b][:, :].rearrange("p (k d) -> p k d", k=16),
                          sVo.rearrange("h (k p) d -> h p k d", p=128)[h],
                          "d_scr")
                    P.dma("syn", qbuf[b][:, :], sQ[h, :, :], "d_scr")

                SBK = mkpsum(p2, "p2s", 5)
                PVB = mkpsum(p2, "p2pv", 2)
                LB = mkpsum(p2, "p2l", 1)[0]
                rrep2 = [sb(p2, f"rrep2{i}", [128, 512], F32)
                         for i in range(2)]
                rrow = sb(p2, "rrow", [1, 512], F32)
                s2b3 = [sb(p2, f"s2b3{i}", [128, 512], F32) for i in range(5)]
                ptb3 = [sb(p2, f"ptb3{i}", [128, 512], BF16)
                        for i in range(5)]

                flat = []
                for h in range(NH):
                    for l in range(4):
                        kts = rect_kts(l)
                        slots = ([("rect", kt) for kt in kts]
                                 + [("tri", t) for t in range(4)])
                        nsl = len(slots)
                        for si, (kind, kt) in enumerate(slots):
                            flat.append(dict(
                                h=h, l=l, si=si, kind=kind, kt=kt,
                                first=(si == 0), last=(si == nsl - 1),
                                g=h * 4 + l, nsl=nsl,
                                col=(h * NSLOT_RECT + RECT_BASE[l] + si
                                     if kind == "rect" else -1)))
                NSL = len(flat)
                hist_qk = [0] * NSL
                hist_s2 = [0] * NSL
                hist_exp = [0] * NSL
                hist_pv = [0] * NSL
                hist_recip = {}
                hist_actcopy = {}
                hist_tt = {}
                head_ready = [0, 0]
                spst_dma = [0, 0]
                trh_hist = [0] * NH

                head_dmas(0, 0, 0)
                head_ready[0] = P.n["d_scr"]

                def emit_qk(i):
                    s = flat[i]
                    hb = s["h"] % 2
                    sbank = SBK[i % 5]
                    if s["kind"] == "rect":
                        koff, kb = s["kt"] * 128, kbuf[hb]
                    else:
                        koff, kb = s["l"] * 512 + s["kt"] * 128, kobuf[hb]
                    w = []
                    if s["first"] and s["l"] == 0:
                        w.append(("d_scr", head_ready[hb]))
                    if i >= 5:
                        w.append(("c_dve", hist_s2[i - 5]))
                    P.emit("pe", lambda e, kb=kb, koff=koff, sbank=sbank,
                           hb=hb, l=s["l"]: e.matmul(
                               sbank[:, :], kb[:, koff:koff + 128],
                               qbuf[hb][:, l * 512:(l + 1) * 512],
                               start=True, stop=True),
                           wait=w or None, inc="c_pe")
                    hist_qk[i] = P.n["c_pe"]

                def emit_s2(i):
                    s = flat[i]
                    sbank = SBK[i % 5]
                    s2 = s2b3[i % 5]
                    w = [("c_pe", hist_qk[i])]
                    if i >= 5:
                        w.append(("c_act", hist_exp[i - 5]))
                    if s["kind"] == "rect":
                        sl = KQOFF - s["kt"] * 128
                        P.emit("dve", lambda e, s2=s2, sl=sl, sbank=sbank,
                               h=s["h"]: e.scalar_tensor_tensor(
                                   s2[:, :], kqw[:, sl:sl + 512],
                                   SLOPES[h], sbank[:, :],
                                   op0=ALU.mult, op1=ALU.add),
                               wait=w, inc="c_dve")
                    else:
                        P.emit("dve", lambda e, s2=s2, kt=s["kt"],
                               sbank=sbank: e.tensor_tensor(
                                   s2[:, :], sbank[:, :],
                                   trh[:, kt * 512:(kt + 1) * 512],
                                   op=ALU.add),
                               wait=w, inc="c_dve")
                    hist_s2[i] = P.n["c_dve"]

                def emit_exp(i):
                    s = flat[i]
                    s2 = s2b3[i % 5]
                    pt = ptb3[i % 5]
                    w = [("c_dve", hist_s2[i])]
                    if i >= 5:
                        w.append(("c_pe", hist_pv[i - 5]))
                    if s["kind"] == "rect":
                        P.emit("act", lambda e, pt=pt, s2=s2, col=s["col"]:
                               e.activation(pt[:, :], s2[:, :], AF.Exp,
                                            bias=c_expb[:, col:col + 1],
                                            scale=1.0),
                               wait=w, inc="c_act")
                    else:
                        P.emit("act", lambda e, pt=pt, s2=s2: e.activation(
                            pt[:, :], s2[:, :], AF.Exp, bias=0.0, scale=1.0),
                            wait=w, inc="c_act")
                    hist_exp[i] = P.n["c_act"]

                def emit_pv(i):
                    s = flat[i]
                    hb = s["h"] % 2
                    g = s["g"]
                    pt = ptb3[i % 5]
                    vb = vbuf[hb] if s["kind"] == "rect" else vobuf[hb]
                    voff = (s["kt"] * 128 if s["kind"] == "rect"
                            else (s["l"] * 4 + s["kt"]) * 128)
                    pvb = PVB[g % 2]
                    w = [("c_act", hist_exp[i])]
                    if s["first"]:
                        if g - 2 in hist_tt:
                            w.append(("c_dve", hist_tt[g - 2]))
                        if g - 1 in hist_recip:
                            w.append(("c_dve", hist_recip[g - 1]))
                        if g - 1 in hist_actcopy:
                            w.append(("c_act", hist_actcopy[g - 1]))

                    def pv(e, vb=vb, voff=voff, pt=pt, si=s["si"],
                           nsl=s["nsl"], pvb=pvb):
                        e.matmul(pvb[:, :], vb[:, voff:voff + 128],
                                 pt[:, :], start=(si == 0),
                                 stop=(si == nsl - 1))
                        last = e.matmul(LB[0:1, :], ones_b[:, :], pt[:, :],
                                        start=(si == 0),
                                        stop=(si == nsl - 1))
                        return last
                    P.emit("pe", pv, wait=w, inc="c_pe")
                    hist_pv[i] = P.n["c_pe"]
                    if s["last"]:
                        hist_pv_last[s["g"]] = P.n["c_pe"]
                        emit_gend(s)

                def emit_gend(s):
                    g = s["g"]
                    h, l = s["h"], s["l"]
                    P.emit("dve", lambda e: e.reciprocal(
                        rrow[:, :], LB[0:1, :]),
                        wait=[("c_pe", hist_pv_last[g])], inc="c_dve")
                    hist_recip[g] = P.n["c_dve"]
                    P.emit("pe", lambda e: e.matmul(
                        LB[:, :], ones_f[:, :], rrow[:, :],
                        start=True, stop=True),
                        wait=[("c_dve", hist_recip[g])], inc="c_pe")
                    rr_n = P.n["c_pe"]
                    rr = rrep2[g % 2]
                    w = [("c_pe", rr_n)]
                    if g - 2 in hist_tt:
                        w.append(("c_dve", hist_tt[g - 2]))
                    P.emit("act", lambda e, rr=rr: e.activation(
                        rr[:, :], LB[:, :], AF.Identity, bias=0.0,
                        scale=1.0), wait=w, inc="c_act")
                    hist_actcopy[g] = P.n["c_act"]
                    sp = spst[g % 2]
                    P.emit("dve", lambda e, sp=sp, rr=rr, pvb=PVB[g % 2]:
                           e.tensor_tensor(sp[:, :], pvb[:, :], rr[:, :],
                                           op=ALU.mult),
                           wait=[("c_act", hist_actcopy[g]),
                                 ("d_scr", spst_dma[g % 2])],
                           inc="c_dve")
                    hist_tt[g] = P.n["c_dve"]
                    P.dma("syn", sSp[h, :, l * 512:(l + 1) * 512],
                          sp[:, :], "d_scr",
                          wait=[("c_dve", hist_tt[g])])
                    spst_dma[g % 2] = P.n["d_scr"]

                hist_pv_last = {}
                DEPTH = 5
                for i in range(NSL):
                    s = flat[i]
                    if s["first"] and s["l"] == 0:
                        h = s["h"]
                        if h + 1 < NH:
                            head_dmas(h + 1, (h + 1) % 2, P.n["c_pe"])
                            head_ready[(h + 1) % 2] = P.n["d_scr"]
                        P.emit("dve", lambda e, h=h: e.scalar_tensor_tensor(
                            trh[:, :], trb[:, :], SLOPES[h], trm[:, :],
                            op0=ALU.mult, op1=ALU.add), inc="c_dve")
                        trh_hist[h] = P.n["c_dve"]
                    if i >= DEPTH:
                        emit_pv(i - DEPTH)
                    emit_qk(i)
                    emit_s2(i)
                    emit_exp(i)
                for j in range(NSL - DEPTH, NSL):
                    emit_pv(j)
                P.run_block()

            with ExitStack() as p3:
                pacc = mkpsum(p3, "pacc3", 4)
                spre = sb(p3, "spre", [128, NH * TOK], BF16)
                wfp = sb(p3, "wfp", [128, NF * 128], F32)
                wbp = [sb(p3, f"wbp{i}", [128, NF * 128], BF16)
                       for i in range(2)]
                s32s = sb(p3, "s32s", [128, 512], F32)
                s32t = [sb(p3, f"s32t{i}", [128, 512], F32)
                        for i in range(2)]
                P.dma("syn", spre[:, :].rearrange("p (h q) -> p h q", h=NH),
                      sSp.rearrange("h p q -> p h q"), "d_scr")
                sp_rdy = P.n["d_scr"]
                w_proj3 = w_proj.rearrange("(c p) o -> p c o", p=128)
                pool_p = 0
                wbp_pe = {}
                s32_dma = [0, 0]
                for oc in range(16):
                    b = oc % 2
                    P.dma("syn",
                          wfp[:, :].rearrange("p (c o) -> p c o", c=NF),
                          w_proj3[:, :, oc * 128:(oc + 1) * 128], "d_w",
                          wait=[("c_pool", pool_p)])
                    dw = P.n["d_w"]
                    P.emit("pool", lambda e, b=b: e.tensor_copy(
                        wbp[b][:, :], wfp[:, :]),
                        wait=[("d_w", dw), ("c_pe", wbp_pe.get(b, 0))],
                        inc="c_pool")
                    pool_p = P.n["c_pool"]
                    cn = P.n["c_pool"]
                    for qs in range(4):
                        pb = pacc[qs % 3]
                        waits = [("c_pool", cn), ("d_scr", sp_rdy),
                                 ("c_dve", P.n["c_dve"] - 5)]

                        def mmp(e, b=b, qs=qs, pb=pb):
                            for fc in range(NF):
                                last = e.matmul(pb[:, :],
                                         wbp[b][:, fc * 128:(fc + 1) * 128],
                                         spre[:, fc * TOK + qs * 512:
                                              fc * TOK + qs * 512 + 512],
                                         start=(fc == 0),
                                         stop=(fc == NF - 1))
                            return last
                        P.emit("pe", mmp, wait=waits, inc="c_pe")
                        wbp_pe[b] = P.n["c_pe"]
                        pn = P.n["c_pe"]
                        P.emit("dve", lambda e, pb=pb, oc=oc: e.tensor_scalar(
                            s32s[:, :], pb[:, :], c_bproj[:, oc:oc + 1],
                            None, op0=ALU.add), wait=("c_pe", pn),
                            inc="c_dve")
                        ev_n = P.n["c_dve"]

                        def trp(e):
                            for i4 in range(4):
                                last = e.transpose(
                                    pacc[3][:, i4 * 128:(i4 + 1) * 128],
                                    s32s[:, i4 * 128:(i4 + 1) * 128],
                                    c_idf[:, :])
                            return last
                        P.emit("pe", trp, wait=[("c_dve", ev_n)], inc="c_pe")
                        tn = P.n["c_pe"]
                        stt = s32t[qs % 2]
                        P.emit("dve", lambda e, stt=stt: e.tensor_copy(
                            stt[:, :], pacc[3][:, :]),
                            wait=[("c_pe", tn), ("d_scr", s32_dma[qs % 2])],
                            inc="c_dve")
                        vn = P.n["c_dve"]
                        P.dma("syn",
                              sS32.rearrange("(qs i p) o -> p qs i o",
                                             i=4, p=128)[
                                  :, qs, :, oc * 128:(oc + 1) * 128],
                              stt[:, :].rearrange("p (i o) -> p i o", i=4),
                              "d_scr", wait=[("c_dve", vn)])
                        s32_dma[qs % 2] = P.n["d_scr"]
                P.run_block()

            with ExitStack() as p4p:
                ptr_b = mkpsum(p4p, "ptrb4", 2, [128, 1024], BF16)
                wfL = [sb(p4p, f"wfL{i}", [128, INTER], F32)
                       for i in range(2)]
                wfS = [sb(p4p, f"wfS{i}", [128, INTER], F32)
                       for i in range(2)]
                wpb = [sb(p4p, f"wpb{i}", [128, INTER], BF16)
                       for i in range(2)]
                wts = [sb(p4p, f"wts{i}", [128, 8 * 128], BF16)
                       for i in range(2)]
                wt_dma = [0, 0]
                ev_hist = [0, 0]
                trw_p = [0, 0]
                cast_p = [0, 0]
                for r in range(16):
                    pr = r % 2
                    eng = "dve" if pr == 0 else "pool"
                    cnt = "c_dve" if pr == 0 else "c_pool"
                    L, Sp = wfL[pr], wfS[pr]
                    P.dma("syn", L[:, 0:4096],
                          wdown[r * 128:(r + 1) * 128, 0:4096], "d_w",
                          wait=[("c_act", cast_p[pr])])
                    P.dma("syn", L[:, 4096:8192],
                          wdown[r * 128:(r + 1) * 128, 4096:8192], "d_w")
                    dw = P.n["d_w"]
                    for stg in range(6):
                        bit = 1 << stg
                        src, dst = (L, Sp) if stg % 2 == 0 else (Sp, L)
                        sv = src[:, :].rearrange("p (a s r) -> p a s r",
                                                 s=2, r=bit * 128)
                        dv = dst[:, :].rearrange("p (a s r) -> p a s r",
                                                 s=2, r=bit * 128)
                        w = None
                        if stg == 0:
                            w = [("d_w", dw)]
                        elif stg == 1:
                            w = [("c_act", cast_p[pr])]
                        P.emit(eng, lambda e, sv=sv, dv=dv:
                               e.tensor_tensor(dv[:, :, 0, :], sv[:, :, 0, :],
                                               sv[:, :, 1, :], op=ALU.add),
                               wait=w, inc=cnt)
                        P.emit(eng, lambda e, sv=sv, dv=dv:
                               e.tensor_tensor(dv[:, :, 1, :], sv[:, :, 0, :],
                                               sv[:, :, 1, :],
                                               op=ALU.subtract), inc=cnt)
                    chain_n = P.n[cnt]
                    for hv in range(2):
                        P.emit("act", lambda e, hv=hv, L=L, pr=pr:
                               e.activation(
                                   wpb[pr][:, hv * 4096:(hv + 1) * 4096],
                                   L[:, hv * 4096:(hv + 1) * 4096],
                                   AF.Identity, bias=0.0, scale=2.0 ** -6),
                               wait=[(cnt, chain_n),
                                     ("c_pe", trw_p[pr])] if hv == 0
                               else None,
                               inc="c_act")
                    cast_p[pr] = P.n["c_act"]
                    cn = P.n["c_act"]
                    for g8 in range(8):
                        pb = ptr_b[g8 % 2]
                        w2 = [("c_act", max(cn, ev_hist[g8 % 2]))]

                        def trw(e, g8=g8, pb=pb, pr=pr):
                            for cc in range(8):
                                fc = g8 * 8 + cc
                                last = e.transpose(
                                    pb[:, cc * 128:(cc + 1) * 128],
                                    wpb[pr][:, fc * 128:(fc + 1) * 128],
                                    c_idb[:, :])
                            return last
                        P.emit("pe", trw, wait=w2, inc="c_pe")
                        pn = P.n["c_pe"]
                        stg_b = wts[g8 % 2]
                        P.emit("act", lambda e, stg_b=stg_b, pb=pb:
                               e.activation(stg_b[:, :], pb[:, :],
                                            AF.Identity, bias=0.0,
                                            scale=1.0),
                               wait=[("c_pe", pn),
                                     ("d_scr", wt_dma[g8 % 2])],
                               inc="c_act")
                        vn = P.n["c_act"]
                        ev_hist[g8 % 2] = P.n["c_act"]
                        P.dma("syn",
                              sWdT[:, r, g8 * 8:(g8 + 1) * 8, :],
                              wts[g8 % 2][:, :].rearrange(
                                  "p (fc o) -> p fc o", fc=8),
                              "d_scr", wait=[("c_act", vn)])
                        wt_dma[g8 % 2] = P.n["d_scr"]
                    trw_p[pr] = P.n["c_pe"]
                P.run_block()

            with ExitStack() as p4a:
                pacc = mkpsum(p4a, "pacc4a", 6)
                ptw = mkpsum(p4a, "ptw", 2, [128, 1024], BF16)
                u1b = sb(p4a, "u1b", [128, NF * TOK], BF16)
                hsu2 = sb(p4a, "hsu2", [128, NI * 128], F32)
                wuf = [sb(p4a, f"wuf{i}", [128, H], F32) for i in range(2)]
                wgf = [sb(p4a, f"wgf{i}", [128, H], F32) for i in range(2)]
                wub = [sb(p4a, f"wub{i}", [128, H], BF16) for i in range(2)]
                wgb = [sb(p4a, f"wgb{i}", [128, H], BF16) for i in range(2)]
                wuT = [sb(p4a, f"wuT{i}", [128, NF * 128], BF16)
                       for i in range(2)]
                wgT = [sb(p4a, f"wgT{i}", [128, NF * 128], BF16)
                       for i in range(2)]
                sgb = [sb(p4a, f"sgb{i}", [128, 512], F32) for i in range(2)]
                x2s = [sb(p4a, f"x2s{i}", [128, 512], F32) for i in range(2)]
                x2a_s = [sb(p4a, f"x2as{i}", [128, 512], BF16)
                         for i in range(2)]
                P.dma("syn", u1b[:, :].rearrange("p (c t) -> p c t", c=NF),
                      sU1.rearrange("c p t -> p c t"), "d_scr")
                u1_rdy = P.n["d_scr"]
                for c in range(NI):
                    P.emit("dve", lambda e, c=c: e.tensor_scalar(
                        hsu2[:, c * 128:(c + 1) * 128], c_h128[:, :],
                        c_su2[:, c:c + 1], 2.0 ** -6,
                        op0=ALU.mult, op1=ALU.mult), inc="c_dve")
                hsu_done = P.n["c_dve"]
                dw_n = [0, 0]
                cast_hist = [0, 0]
                trw_hist = [0, 0]
                wT_done = [0, 0]
                tw_ev = [0, 0]
                mm_hist = [0, 0]
                silu_hist = [0, 0]
                stt_hist = [0, 0]
                sa_hist = [0, 0]
                ev_hist = [0, 0]
                x2a_dma = [0, 0]
                tw_idx = 0

                tw_state = {"idx": 0}

                def load_w(oc):
                    b = oc % 2
                    P.dma("syn", wuf[b][:, :],
                          wup[oc * 128:(oc + 1) * 128, :], "d_w",
                          wait=[("c_dve", cast_hist[b])])
                    P.dma("syn", wgf[b][:, :],
                          wgate[oc * 128:(oc + 1) * 128, :], "d_w")
                    dw_n[b] = P.n["d_w"]

                def prep_w(oc):
                    b = oc % 2
                    P.emit("dve", lambda e, b=b: e.tensor_scalar(
                        wub[b][:, :], wuf[b][:, :], 2.0 ** -6, None,
                        op0=ALU.mult),
                        wait=[("d_w", dw_n[b]), ("c_pe", trw_hist[b])],
                        inc="c_dve")
                    P.emit("dve", lambda e, b=b: e.tensor_scalar(
                        wgb[b][:, :], wgf[b][:, :], 2.0 ** -6, None,
                        op0=ALU.mult), inc="c_dve")
                    cast_hist[b] = P.n["c_dve"]
                    cn = P.n["c_dve"]
                    for (srcb, dstT) in ((wub, wuT), (wgb, wgT)):
                        for g in range(2):
                            ti = tw_state["idx"]
                            bank = ptw[ti % 2]
                            w2 = [("c_dve", cn),
                                  ("c_dve", tw_ev[ti % 2])]

                            def trg(e, srcb=srcb, b=b, g=g, bank=bank):
                                for cc in range(8):
                                    fc = g * 8 + cc
                                    last = e.transpose(
                                        bank[:, cc * 128:(cc + 1) * 128],
                                        srcb[b][:, fc * 128:(fc + 1) * 128],
                                        c_idb[:, :])
                                return last
                            P.emit("pe", trg, wait=w2, inc="c_pe")
                            pn = P.n["c_pe"]
                            P.emit("dve",
                                   lambda e, dstT=dstT, b=b, g=g, bank=bank:
                                   e.tensor_copy(
                                       dstT[b][:, g * 1024:(g + 1) * 1024],
                                       bank[:, :]),
                                   wait=[("c_pe", max(pn, mm_hist[b]))],
                                   inc="c_dve")
                            tw_ev[ti % 2] = P.n["c_dve"]
                            tw_state["idx"] = ti + 1
                    trw_hist[b] = P.n["c_pe"]
                    wT_done[b] = P.n["c_dve"]

                def emit_tail(k):
                    oc, ts = divmod(k, 4)
                    tp_ = ts % 2
                    pa_, pb_, pc_ = pacc[tp_], pacc[2 + tp_], pacc[4 + tp_]
                    mm_n = mm_of[k]
                    sg = sgb[tp_]
                    P.emit("act", lambda e, oc=oc, sg=sg, pb_=pb_:
                           e.activation(
                               sg[:, :], pb_[:, :], AF.Silu,
                               scale=c_vuvg[:, 64 + oc:65 + oc]),
                           wait=[("c_pe", mm_n),
                                 ("c_dve", stt_hist[tp_])],
                           inc="c_act")
                    silu_hist[tp_] = P.n["c_act"]
                    xs = x2s[tp_]
                    P.emit("dve", lambda e, oc=oc, xs=xs, sg=sg,
                           pa_=pa_: e.scalar_tensor_tensor(
                               xs[:, :], pa_[:, :],
                               c_vuvg[:, oc:oc + 1], sg[:, :],
                               op0=ALU.mult, op1=ALU.mult),
                           wait=[("c_act", silu_hist[tp_]),
                                 ("c_pe", sa_hist[tp_])],
                           inc="c_dve")
                    stt_hist[tp_] = P.n["c_dve"]
                    P.emit("pe", lambda e, oc=oc, xs=xs, pc_=pc_:
                           e.matmul(
                               pc_[:, :],
                               hsu2[:, oc * 128:(oc + 1) * 128],
                               xs[:, :], start=True, stop=True),
                           wait=[("c_dve", max(stt_hist[tp_],
                                               ev_hist[tp_]))],
                           inc="c_pe")
                    sa_hist[tp_] = P.n["c_pe"]
                    sa_n = P.n["c_pe"]
                    xa = x2a_s[tp_]
                    P.emit("dve", lambda e, xa=xa, pc_=pc_:
                           e.tensor_copy(xa[:, :], pc_[:, :]),
                           wait=[("c_pe", sa_n),
                                 ("d_scr", x2a_dma[tp_])],
                           inc="c_dve")
                    ev_hist[tp_] = P.n["c_dve"]
                    vn = P.n["c_dve"]
                    P.dma("syn",
                          sX2b[oc * 128:(oc + 1) * 128,
                               ts * 512:(ts + 1) * 512],
                          xa[:, :], "d_scr", wait=[("c_dve", vn)])
                    x2a_dma[tp_] = P.n["d_scr"]

                mm_of = {}
                load_w(0)
                load_w(1)
                prep_w(0)
                for k in range(NI * 4):
                    oc, ts = divmod(k, 4)
                    b = oc % 2
                    tp_ = ts % 2
                    pa_, pb_ = pacc[tp_], pacc[2 + tp_]
                    w = [("c_dve", max(wT_done[b], stt_hist[tp_],
                                       hsu_done if oc == 0 else 0)),
                         ("c_act", silu_hist[tp_])]
                    if k == 0:
                        w.append(("d_scr", u1_rdy))

                    def mmu(e, b=b, ts=ts, pa_=pa_, pb_=pb_):
                        for fc in range(NF):
                            e.matmul(pa_[:, :],
                                     wuT[b][:, fc * 128:(fc + 1) * 128],
                                     u1b[:, fc * TOK + ts * 512:
                                         fc * TOK + ts * 512 + 512],
                                     start=(fc == 0),
                                     stop=(fc == NF - 1))
                        for fc in range(NF):
                            last = e.matmul(
                                pb_[:, :],
                                wgT[b][:, fc * 128:(fc + 1) * 128],
                                u1b[:, fc * TOK + ts * 512:
                                    fc * TOK + ts * 512 + 512],
                                start=(fc == 0), stop=(fc == NF - 1))
                        return last
                    P.emit("pe", mmu, wait=w, inc="c_pe")
                    mm_hist[b] = P.n["c_pe"]
                    mm_of[k] = P.n["c_pe"]
                    if k > 0:
                        emit_tail(k - 1)
                    if ts == 1 and oc + 1 < NI:
                        prep_w(oc + 1)
                    if ts == 2 and oc + 2 < NI:
                        load_w(oc + 2)
                emit_tail(NI * 4 - 1)
                P.run_block()

            with ExitStack() as p4c:
                pacc = mkpsum(p4c, "pacc4c", 4)
                x2bb = [sb(p4c, f"x2bb{i}", [128, NI * 512], BF16)
                        for i in range(2)]
                wdts = [sb(p4c, f"wdtt{i}", [128, NI * 128], BF16)
                        for i in range(2)]
                x3s = [sb(p4c, f"x3s{i}", [128, 512], F32) for i in range(2)]
                gst = [sb(p4c, f"gst{i}", [128, 512], F32) for i in range(2)]
                x2b_r = sX2b.rearrange("(fc p) (ts t) -> p fc ts t",
                                       p=128, t=512)
                g_dma = [0, 0]
                wdt_pe = [0, 0]
                mm_hist = [0, 0]
                ev_hist = [0, 0]
                tr_hist = [0, 0]
                silu_hist = [0, 0]
                xr = [0, 0]

                def load_x2b(ts):
                    P.dma("syn",
                          x2bb[ts % 2][:, :].rearrange(
                              "p (c t) -> p c t", c=NI),
                          x2b_r[:, :, ts, :], "d_scr",
                          wait=[("c_pe", mm_hist[ts % 2])])
                    xr[ts % 2] = P.n["d_scr"]

                mm_of = {}

                def emit_tail_c(k):
                    ts, oc = divmod(k, 16)
                    pa_ = pacc[oc % 2]
                    x3 = x3s[oc % 2]
                    P.emit("dve", lambda e, oc=oc, x3=x3, pa_=pa_:
                           e.tensor_scalar(
                               x3[:, :], pa_[:, :],
                               c_vuvg[:, 128 + oc:129 + oc], None,
                               op0=ALU.mult),
                           wait=[("c_pe", max(mm_of[k],
                                              tr_hist[oc % 2]))],
                           inc="c_dve")
                    ev_hist[oc % 2] = P.n["c_dve"]
                    ev_n = P.n["c_dve"]
                    ptr_ = pacc[2 + oc % 2]

                    def trg(e, x3=x3, ptr_=ptr_):
                        for i4 in range(4):
                            last = e.transpose(
                                ptr_[:, i4 * 128:(i4 + 1) * 128],
                                x3[:, i4 * 128:(i4 + 1) * 128],
                                c_idf[:, :])
                        return last
                    P.emit("pe", trg,
                           wait=[("c_dve", ev_n),
                                 ("c_act", silu_hist[oc % 2])],
                           inc="c_pe")
                    tr_hist[oc % 2] = P.n["c_pe"]
                    tn = P.n["c_pe"]
                    gb = gst[oc % 2]
                    P.emit("act", lambda e, gb=gb, ptr_=ptr_:
                           e.activation(gb[:, :], ptr_[:, :], AF.Silu),
                           wait=[("c_pe", tn), ("d_scr", g_dma[oc % 2])],
                           inc="c_act")
                    silu_hist[oc % 2] = P.n["c_act"]
                    an = P.n["c_act"]
                    P.dma("syn",
                          sG.rearrange("(ts i p) o -> p ts i o",
                                       i=4, p=128)[
                              :, ts, :, oc * 128:(oc + 1) * 128],
                          gb[:, :].rearrange("p (i o) -> p i o", i=4),
                          "d_scr", wait=[("c_act", an)])
                    g_dma[oc % 2] = P.n["d_scr"]

                load_x2b(0)
                for k in range(64):
                    ts, oc = divmod(k, 16)
                    tb = ts % 2
                    b = oc % 2
                    if oc == 0 and ts + 1 < 4:
                        load_x2b(ts + 1)
                    P.dma("syn",
                          wdts[b][:, :].rearrange("p (c o) -> p c o",
                                                  c=NI),
                          sWdT[:, oc, :, :], "d_scr",
                          wait=[("c_pe", wdt_pe[b])])
                    dwn = P.n["d_scr"]
                    pa_ = pacc[oc % 2]

                    def mmd(e, b=b, tb=tb, pa_=pa_):
                        for fc in range(NI):
                            last = e.matmul(
                                pa_[:, :],
                                wdts[b][:, fc * 128:(fc + 1) * 128],
                                x2bb[tb][:, fc * 512:(fc + 1) * 512],
                                start=(fc == 0), stop=(fc == NI - 1))
                        return last
                    P.emit("pe", mmd,
                           wait=[("d_scr", max(dwn, xr[tb])),
                                 ("c_dve", ev_hist[oc % 2])],
                           inc="c_pe")
                    wdt_pe[b] = P.n["c_pe"]
                    mm_hist[tb] = P.n["c_pe"]
                    mm_of[k] = P.n["c_pe"]
                    if k > 0:
                        emit_tail_c(k - 1)
                emit_tail_c(63)
                P.run_block()

            with ExitStack() as p4d:
                grow = [sb(p4d, f"grow{i}", [128, H], F32) for i in range(2)]
                srow = [sb(p4d, f"srow{i}", [128, H], F32) for i in range(2)]
                xrow = [sb(p4d, f"xrow{i}", [128, H], F32) for i in range(2)]
                orow = [sb(p4d, f"orow{i}", [128, H], F32) for i in range(2)]
                comb_n = [0, 0]
                out_n = [0, 0]
                pool_n4 = [0, 0]
                for tt in range(16):
                    b = tt % 2
                    w = ([("c_dve", comb_n[b]),
                          ("c_pool", pool_n4[b])] if tt >= 2 else None)
                    P.dma("syn", grow[b][:, :],
                          sG[tt * 128:(tt + 1) * 128, :], "d_scr", wait=w)
                    P.dma("syn", srow[b][:, :],
                          sS32[tt * 128:(tt + 1) * 128, :], "d_scr")
                    P.dma("syn", xrow[b][:, :],
                          x_own[tt * 128:(tt + 1) * 128, :], "d_in")
                    dr, di = P.n["d_scr"], P.n["d_in"]

                    P.emit("dve", lambda e, b=b: e.tensor_tensor(
                        grow[b][:, :], grow[b][:, :], srow[b][:, :],
                        op=ALU.mult),
                        wait=[("d_scr", dr), ("d_in", di),
                              ("d_out", out_n[b])], inc="c_dve")
                    mul_n = P.n["c_dve"]
                    P.emit("pool", lambda e, b=b: e.tensor_tensor(
                        orow[b][:, :], grow[b][:, :], xrow[b][:, :],
                        op=ALU.add),
                        wait=[("c_dve", mul_n)], inc="c_pool")
                    comb_n[b] = P.n["c_dve"]
                    pool_n4[b] = P.n["c_pool"]
                    vn = P.n["c_pool"]
                    P.dma("syn", out_own[tt * 128:(tt + 1) * 128, :],
                          orow[b][:, :], "d_out", wait=[("c_pool", vn)])
                    out_n[b] = P.n["d_out"]
                if tap is not None:
                    tsrc = scr_map[tap]
                    shp = list(tsrc.tensor.shape)
                    tout = nc.dram_tensor(
                        "tapout", shp, tsrc.dtype,
                        kind="ExternalOutput").ap()
                    nch = max(1, shp[0] // 2048)
                    step = shp[0] // nch
                    for ci in range(nch):
                        P.dma("syn", tout[ci * step:(ci + 1) * step],
                              tsrc[ci * step:(ci + 1) * step], "d_out")
                P.run_block()
    return nc


def make_in_maps(x, w_qkv, b_qkv, w_proj, b_proj, su_ug, su_dp, ws_ug, ws_dp,
                 w_up, w_gate, w_down):
    h128 = _hadamard(128).astype(np.float32)
    identf = np.eye(128, dtype=np.float32)

    shared = {
        "w_qkv": np.ascontiguousarray(w_qkv, np.float32),
        "w_proj": np.ascontiguousarray(w_proj, np.float32),
        "wup": np.ascontiguousarray(w_up, np.float32),
        "wgate": np.ascontiguousarray(w_gate, np.float32),
        "wdown": np.ascontiguousarray(w_down, np.float32),
        "bqkv_cols": np.ascontiguousarray(
            np.asarray(b_qkv, np.float32).reshape(48, 128).T),
        "bvrep": np.ascontiguousarray(np.broadcast_to(
            np.asarray(b_qkv, np.float32)[4096:6144], (128, 2048))),
        "bproj_cols": np.ascontiguousarray(
            np.asarray(b_proj, np.float32).reshape(16, 128).T),
        "su1_cols": np.ascontiguousarray(
            np.asarray(su_ug, np.float32).reshape(16, 128).T),
        "su2_cols": np.ascontiguousarray(
            np.asarray(su_dp, np.float32).reshape(64, 128).T),
        "ws_cols": np.ascontiguousarray(np.concatenate([
            np.asarray(ws_ug, np.float32)[:8192].reshape(64, 128).T,
            np.asarray(ws_ug, np.float32)[8192:].reshape(64, 128).T,
            np.asarray(ws_dp, np.float32).reshape(16, 128).T], axis=1)),
        "h128": h128,
        "ident_f": identf,
        "ident_b": identf.astype(ml_dtypes.bfloat16),
    }
    in_maps = []
    for c in range(8):
        bi, pa = c // 2, c % 2
        qgs = [1024 * l + 512 * pa for l in range(4)]
        x_own = np.ascontiguousarray(
            np.concatenate([x[bi, qg:qg + 512] for qg in qgs], axis=0))
        x_kv = np.ascontiguousarray(np.roll(x[bi], -512 * pa, axis=0))
        expb = np.zeros((NH, NSLOT_RECT), np.float32)
        for hh in range(NH):
            s = 0
            for l in range(4):
                for kt in rect_kts(l):
                    if kt < 28:
                        expb[hh, s] = -(SLOPES[hh] * 1024 * l + MCONST)
                    elif pa == 1:
                        expb[hh, s] = -(SLOPES[hh] * (4096 + 1024 * l)
                                        + MCONST)
                    else:
                        expb[hh, s] = NEGBIG
                    s += 1
        m = dict(shared)
        m["x_own"] = x_own
        m["x_kv"] = x_kv
        m["expb"] = np.ascontiguousarray(np.broadcast_to(
            expb.reshape(1, -1), (128, NH * NSLOT_RECT)))
        in_maps.append(m)
    return in_maps


def kernel(L0_attn_hidden_states, L0_attn_w_qkv, L0_attn_b_qkv,
           L0_attn_w_proj, L0_attn_b_proj, L0_ffn_SU_ug, L0_ffn_SU_dp,
           L0_ffn_Wscale_ug, L0_ffn_Wscale_dp, L0_ffn_W_up, L0_ffn_W_gate,
           L0_ffn_W_down):
    x = np.ascontiguousarray(L0_attn_hidden_states, dtype=np.float32)
    in_maps = make_in_maps(x, L0_attn_w_qkv, L0_attn_b_qkv, L0_attn_w_proj,
                           L0_attn_b_proj, L0_ffn_SU_ug, L0_ffn_SU_dp,
                           L0_ffn_Wscale_ug, L0_ffn_Wscale_dp, L0_ffn_W_up,
                           L0_ffn_W_gate, L0_ffn_W_down)
    nc = build()
    res = run_bass_kernel_spmd(nc, in_maps, core_ids=list(range(8)))
    out = np.empty((B, S, H), np.float32)
    for c in range(8):
        bi, pa = c // 2, c % 2
        shard = res.results[c]["out_own"]
        for l in range(4):
            qg = 1024 * l + 512 * pa
            out[bi, qg:qg + 512] = shard[l * 512:(l + 1) * 512]
    return out
